# version 05
# baseline (speedup 1.0000x reference)
import numpy as np
import ml_dtypes
import concourse.bass as bass
import concourse.mybir as mybir
from concourse.bass_utils import run_bass_kernel_spmd

F32 = mybir.dt.float32
BF16 = mybir.dt.bfloat16
FP8 = mybir.dt.float8e4
AF = mybir.ActivationFunctionType
ALU = mybir.AluOpType

NCORES = 8
SEQ = 8192
DM = 2048
FF = 8192
NH = 16
NT = 16
NSLOT = 8
TPC = 1024
NKT = 64
NPASS = 8
PTOK = 1024
SCALE = float(1.0 / np.sqrt(128.0))
EPS_LN = 1e-5
EPS_RMS = 1e-6

ENGINES = ("sync", "vector", "scalar", "tensor", "gpsimd")

BUILD_PHASES = "all"
DEBUG_OUTS = False


class Sched:

    def __init__(self):
        self.streams = {e: [] for e in ENGINES}
        self.cnt = {}
        self.dma_sems = set()

    def n(self, sem):
        return self.cnt.get(sem, 0)

    def op(self, eng, fn=None, waits=(), inc=None, inck=1, is_dma=False):
        if fn is None and inc is not None and not waits:
            st = self.streams[eng]
            if st and st[-1][0] is not None and st[-1][2] == inc:
                return self.cnt.get(inc, 0)
        if inc is not None:
            self.cnt[inc] = self.cnt.get(inc, 0) + inck
        ws = tuple((s, v) for (s, v) in waits if v > 0)
        self.streams[eng].append((fn, ws, inc, inck, is_dma))
        return self.cnt.get(inc, 0) if inc else 0

    def dma(self, eng, out, in_, waits=(), sem=None):
        sem = sem or ("dl" if eng == "sync" else "dg")
        self.dma_sems.add(sem)
        return self.op(eng, lambda E, o=out, i=in_: E.dma_start(out=o, in_=i),
                       waits=waits, inc=sem, inck=16, is_dma=True)

    def barrier(self):
        marks = []
        for e in ENGINES:
            st = self.streams[e]
            if not st:
                continue
            fn, ws, inc, inck, isd = st[-1]
            if inc is not None and not isd:
                marks.append((inc, self.n(inc)))
            elif fn is not None and not isd:
                m = self.op(e, None, inc="b_" + e)
                marks.append(("b_" + e, m))
        marks += [(s, self.n(s)) for s in sorted(self.dma_sems)]
        for e in ENGINES:
            self.op(e, None, waits=marks)

    def replay(self, sems):
        def make(name):
            stream = self.streams[name]

            def body(E):
                last_real = None
                last_was_dma = False
                for fn, waits, inc, inck, is_dma in stream:
                    for s, v in waits:
                        E.wait_ge(sems[s], v)
                    if fn is not None:
                        ins = fn(E)
                        last_real = ins
                        last_was_dma = is_dma
                        if inc is not None:
                            ins.then_inc(sems[inc], inck)
                    elif inc is not None:
                        if last_real is not None and not (last_was_dma and inck % 16):
                            last_real.then_inc(sems[inc], inck)
                        else:
                            E.nop().then_inc(sems[inc], inck)
            return body
        return {e: make(e) for e in ENGINES}


def build_nc():
    nc = bass.Bass("TRN2", target_bir_lowering=False, dynamic_dma_scratch_size=512)
    b = Sched()
    C = SCALE

    dt = nc.dram_tensor
    xf = dt("xf", [SEQ, DM], F32, kind="ExternalInput").ap()
    xo = dt("xo", [TPC, DM], F32, kind="ExternalInput").ap()
    qidx = dt("qidx", [1, TPC], F32, kind="ExternalInput").ap()
    kidx = dt("kidx", [128, NKT], F32, kind="ExternalInput").ap()
    wqkv = dt("wqkv", [DM, 3 * DM], F32, kind="ExternalInput").ap()
    wq2 = dt("wq2", [DM, DM], F32, kind="ExternalInput").ap()
    wk2 = dt("wk2", [DM, DM], F32, kind="ExternalInput").ap()
    wv2 = dt("wv2", [DM, DM], F32, kind="ExternalInput").ap()
    wo2 = dt("wo2", [DM, DM], F32, kind="ExternalInput").ap()
    aa = dt("aa", [DM, 16], F32, kind="ExternalInput").ap()
    bbi = dt("bbi", [16, FF], F32, kind="ExternalInput").ap()
    ln1r = dt("ln1r", [1, DM], F32, kind="ExternalInput").ap()
    ln2r = dt("ln2r", [1, DM], F32, kind="ExternalInput").ap()
    wout = dt("wout", [DM, DM], F32, kind="ExternalInput").ap()
    win = dt("win", [FF, DM], F32, kind="ExternalInput").ap()
    wup = dt("wup", [FF, DM], F32, kind="ExternalInput").ap()
    wdn = dt("wdn", [DM, FF], F32, kind="ExternalInput").ap()
    ident = dt("ident", [128, 128], F32, kind="ExternalInput").ap()
    out = dt("out", [TPC, DM], F32, kind="ExternalOutput").ap()

    k1w = dt("k1w", [DM, DM], BF16, kind="Internal").ap()
    v1w = dt("v1w", [DM, DM], BF16, kind="Internal").ap()
    q1w = dt("q1w", [DM, DM], BF16, kind="Internal").ap()
    wq2t = dt("wq2t", [DM, DM], BF16, kind="Internal").ap()
    wk2t = dt("wk2t", [DM, DM], BF16, kind="Internal").ap()
    bbd = dt("bbd", [16, FF], BF16, kind="Internal").ap()
    wvwoad = dt("wvwoad", [DM, 256], BF16, kind="Internal").ap()
    V2d = dt("V2d", [NH, 128, NKT, 16], BF16, kind="Internal").ap()
    K1d = dt("K1d", [NH, 128, SEQ], BF16, kind="Internal").ap()
    V1d = dt("V1d", [NH, SEQ, 128], BF16, kind="Internal").ap()
    K2d = dt("K2d", [NH, 128, SEQ], BF16, kind="Internal").ap()
    Q1d = dt("Q1d", [NH, 128, TPC], BF16, kind="Internal").ap()
    Q2d = dt("Q2d", [NH, 128, TPC], BF16, kind="Internal").ap()

    dbg = {}
    if DEBUG_OUTS:
        for nm, shp in [("d_K1", [NH, 128, SEQ]), ("d_V1", [NH, SEQ, 128]),
                        ("d_K2", [NH, 128, SEQ]), ("d_Q1", [NH, 128, TPC]),
                        ("d_Q2", [NH, 128, TPC]), ("d_V2p", [NH, 128, NKT, 16]),
                        ("d_av1", [128, NH, TPC]), ("d_mod", [16, TPC]),
                        ("d_acc0", [TPC, DM]), ("d_mhat", [128, 32]),
                        ("d_hA", [16, TPC]), ("d_k1w", [DM, DM]), ("d_wq2t", [DM, DM])]:
            dbg[nm] = dt(nm, shp, F32 if nm in ("d_acc0", "d_mhat", "d_mod", "d_hA") else BF16,
                         kind="ExternalOutput").ap()

    sbA = nc.alloc_sbuf_tensor
    id_bf = sbA("id_bf", [128, 128], BF16)
    ones_r = sbA("ones_r", [1, 128], F32)
    ones_c = sbA("ones_c", [128, 1], BF16)
    kidx_sb = sbA("kidx_sb", [128, NKT], F32)
    a2_bf = sbA("a2_bf", [128, NT, 16], BF16)
    wvwoa = sbA("wvwoa", [128, NT, 16], BF16)
    ln1b = sbA("ln1b", [128, DM], BF16)
    mhat = sbA("mhat", [128, 32], F32)
    nrm4 = sbA("nrm4", [1, 64], F32)
    v2stage = sbA("v2stage", [128, NSLOT, 256], BF16)
    ones32 = sbA("ones32", [128, 33], BF16)
    modpre = sbA("modpre", [16, TPC], F32)[:]
    msc = sbA("msc", [16, TPC], F32)[:]
    hAr = sbA("hAr", [16, TPC], F32)[:]
    idf16 = sbA("idf16", [16, 16], F32)
    stats = sbA("stats", [128, NSLOT, 16], F32)
    mrow = sbA("mrow", [1, 32], F32)
    cst = sbA("cst", [128, 4], F32)
    hT = sbA("hT", [128, NT, TPC], BF16)
    arA = sbA("arA", [128, NT * 1024], BF16)
    arB = sbA("arB", [128, 12288], BF16)
    arC = sbA("arC", [128, 16384], BF16)
    arD = sbA("arD", [128, 4096], BF16)
    arE = sbA("arE", [128, 2048], BF16)
    accT = [sbA(f"acc{i}", [128, DM], F32) for i in range(NSLOT)]

    pA = nc.alloc_psum_tensor("pA", [128, 1024], F32)
    pB = nc.alloc_psum_tensor("pB", [128, 1024], F32)
    pC = nc.alloc_psum_tensor("pC", [128, 1024], F32)
    pD = nc.alloc_psum_tensor("pD", [128, 1024], F32)

    arBf = arB[:].bitcast(F32)
    arCf = arC[:].bitcast(F32)
    arEf = arE[:].bitcast(F32)
    arAf = arA[:].bitcast(F32)
    qib = arAf[:, 1024:2048]
    pT2 = pD[:].bitcast(BF16)

    id_f = arCf[:, 512:640]
    b.dma("sync", id_f, ident, sem="di")
    b.dma("sync", idf16[:], ident[0:16, 0:16], sem="di")
    b.dma("sync", kidx_sb[:], kidx, sem="di")
    aa_sb = arCf[:, 0:NT * 16].rearrange("p (c r) -> p c r", r=16)
    b.dma("sync", aa_sb, aa.rearrange("(c p) r -> p c r", p=128), sem="di")
    qi_row = arAf[0:1, 0:TPC]
    b.dma("sync", qi_row, qidx, sem="di")
    l1_row = arBf[0:1, 0:DM]
    b.dma("sync", l1_row, ln1r, sem="di")
    dl0 = b.n("di")
    b.op("vector", lambda E: E.tensor_copy(id_bf[:], id_f), waits=[("di", dl0)])
    b.op("vector", lambda E: E.memset(ones_r[:], 1.0))
    b.op("vector", lambda E: E.memset(ones_c[:], 1.0))
    b.op("vector", lambda E: E.memset(cst[:, 0:1], EPS_LN))
    b.op("vector", lambda E: E.memset(cst[:, 1:2], EPS_RMS))
    b.op("vector", lambda E: E.memset(cst[:, 2:3], 1.0))
    svz = b.op("vector", lambda E: E.memset(ones32[:], 0.0), inc="sv")
    b.op("vector", lambda E: E.memset(ones32[:, 32:33], 1.0), waits=[("sv", svz)])
    sv0 = b.op("vector", None, inc="sv")
    b.op("tensor", lambda E: E.matmul(pD[:, 0:512], ones_r[:], qi_row[0:1, 0:512], start=True, stop=True),
         waits=[("sv", sv0)])
    b.op("tensor", lambda E: E.matmul(pD[:, 512:1024], ones_r[:], qi_row[0:1, 512:1024], start=True, stop=True))
    for _h in range(2):
        b.op("tensor", lambda E, _h=_h: E.matmul(pA[:, 512 * _h:512 * (_h + 1)], ones_r[:],
                                                 l1_row[0:1, 512 * _h:512 * (_h + 1)], start=True, stop=True))
    for _h in range(2):
        b.op("tensor", lambda E, _h=_h: E.matmul(pB[:, 512 * _h:512 * (_h + 1)], ones_r[:],
                                                 l1_row[0:1, 1024 + 512 * _h:1024 + 512 * (_h + 1)], start=True, stop=True))
    st1b = b.op("tensor", None, inc="st")
    sv_q = b.op("vector", lambda E: E.tensor_copy(qib[:], pD[:, 0:TPC]), waits=[("st", st1b)], inc="sv")
    b.op("vector", lambda E: E.tensor_copy(ln1b[:, 0:1024], pA[:]))
    svl = b.op("vector", lambda E: E.tensor_copy(ln1b[:, 1024:2048], pB[:]), inc="sv")
    mask_sb = arD[:].bitcast(FP8).rearrange("p (k q) -> p k q", q=128)
    for k in range(NKT):
        j = k // 8
        b.op("vector", lambda E, k=k, j=j: E.tensor_scalar(
            mask_sb[:, k, :], qib[:, j * 128:(j + 1) * 128], kidx_sb[:, k:k + 1], None, ALU.is_ge),
            waits=[("sv", sv_q)] if k == 0 else ())
    svm = b.op("vector", None, inc="sv")
    for i in range(NT):
        b.op("vector", lambda E, i=i: E.tensor_copy(a2_bf[:, i, :], aa_sb[:, i, :]))
    b.op("vector", None, inc="sv")
    bstage_f = arBf[0:16, 0:1024]
    bstage_b = arB[:][0:16, 2048:3072]
    for q in range(8):
        dq = b.dma("sync", bstage_f, bbi[:, q * 1024:(q + 1) * 1024],
                   waits=[("sv", b.n("sv"))], sem="db")
        svb = b.op("vector", lambda E: E.tensor_copy(bstage_b, bstage_f),
                   waits=[("db", dq), ("dbo", b.n("dbo"))], inc="sv")
        b.dma("scalar", bbd[:, q * 1024:(q + 1) * 1024], bstage_b, waits=[("sv", svb)], sem="dbo")
    woa_f = arCf[:, NT * 16:2 * NT * 16].rearrange("p (c r) -> p c r", r=16)
    wsl = arBf[:, 2048:6144].rearrange("p (u d) -> p u d", u=2)
    pwoa = pC[:, 0:256].rearrange("p (c r) -> p c r", r=16)
    for i in range(NT):
        dli = b.dma("sync", wsl[:, i % 2, :], wo2[128 * i:128 * (i + 1), :],
                    waits=[("st", b.n("st") - 1)] if i >= 2 else [("sv", svl)],
                    sem="dw%d" % (i % 2))
        w0 = [("dw%d" % (i % 2), dli), ("sv", svm)]
        for j in range(NT):
            b.op("tensor", lambda E, i=i, j=j: E.matmul(
                pwoa[:, j, :], wsl[:, i % 2, 128 * j:128 * (j + 1)], aa_sb[:, i, :],
                start=(i == 0 and j == 0), stop=(i == NT - 1 and j == NT - 1)),
                waits=w0 if j == 0 else ())
        b.op("tensor", None, inc="st")
    stw = b.n("st")
    svw = b.op("vector", lambda E: E.tensor_copy(woa_f[:], pwoa[:]), waits=[("st", stw)], inc="sv")
    wvstage = arC[:][:, 7168:11264].rearrange("p (c x) -> p c x", x=256)
    for i in range(NT):
        dli = b.dma("sync", wsl[:, i % 2, :], wv2[128 * i:128 * (i + 1), :],
                    waits=[("st", b.n("st") - 1)] if i >= 2 else [("st", b.n("st"))],
                    sem="dw%d" % (i % 2))
        w0 = [("dw%d" % (i % 2), dli), ("sv", svw), ("sv", b.n("sv"))]
        for j in range(NT):
            b.op("tensor", lambda E, i=i, j=j: E.matmul(
                pwoa[:, j, :], wsl[:, i % 2, 128 * j:128 * (j + 1)], woa_f[:, i, :],
                start=True, stop=True),
                waits=w0 if j == 0 else ())
        stt = b.op("tensor", None, inc="st")
        b.op("vector", lambda E, i=i: E.tensor_copy(wvstage[:, :, 16 * i:16 * (i + 1)], pwoa[:]),
             waits=[("st", stt)], inc="sv")
    svw2 = b.op("vector", None, inc="sv")
    b.dma("scalar", wvwoad.rearrange("(c p) x -> p c x", p=128), wvstage[:],
          waits=[("sv", svw2)], sem="dwv")

    cstage = arC[:][:, 2048:6144].rearrange("p (u d) -> p u d", u=2)
    it = 0
    for i in range(NT):
        for part, (wsrc, wdst) in enumerate([
                (wqkv[128 * i:128 * (i + 1), 0:DM], q1w[128 * i:128 * (i + 1), :]),
                (wqkv[128 * i:128 * (i + 1), DM:2 * DM], k1w[128 * i:128 * (i + 1), :]),
                (wqkv[128 * i:128 * (i + 1), 2 * DM:3 * DM], v1w[128 * i:128 * (i + 1), :])]):
            u = it % 2
            dli = b.dma("sync", wsl[:, u, :], wsrc,
                        waits=[("sv", b.n("sv") - 1)] if it >= 2 else [("sv", svw2)],
                        sem="dw%d" % u)
            svc = b.op("vector", lambda E, u=u: E.tensor_copy(cstage[:, u, :], wsl[:, u, :]),
                       waits=[("dw%d" % u, dli), ("dc%d" % u, b.n("dc%d" % u))],
                       inc="sv")
            b.dma("scalar", wdst, cstage[:, u, :], waits=[("sv", svc)], sem="dc%d" % u)
            it += 1
    tstage = arC[:][:, 6144:7168].rearrange("p (u d) -> p u d", u=2)
    for wsrc, wdst in [(wq2, wq2t), (wk2, wk2t)]:
        for i in range(NT):
            u = i % 2
            dli = b.dma("sync", wsl[:, u, :], wsrc[128 * i:128 * (i + 1), :],
                        waits=[("st", b.n("st") - 1)] if i >= 2 else [("sv", b.n("sv"))],
                        sem="dw%d" % u)
            svc = b.op("vector", lambda E, u=u: E.tensor_copy(cstage[:, u, :], wsl[:, u, :]),
                       waits=[("dw%d" % u, dli), ("dc%d" % u, b.n("dc%d" % u))], inc="sv")
            for g in range(4):
                for jj in range(4):
                    j = 4 * g + jj
                    ws = []
                    if jj == 0:
                        ws = [("sv", max(svc, b.n("sv")))]
                    b.op("tensor", lambda E, u=u, j=j, jj=jj, g=g: E.transpose(
                        pT2[:, (g % 2) * 1024 + jj * 128:(g % 2) * 1024 + (jj + 1) * 128],
                        cstage[:, u, 128 * j:128 * (j + 1)], id_bf[:]),
                        waits=ws)
                stt = b.op("tensor", None, inc="st")
                uu = g % 2
                sve = b.op("vector", lambda E, uu=uu: E.tensor_copy(
                    tstage[:, uu, :], pT2[:, uu * 1024:uu * 1024 + 512]),
                    waits=[("st", stt), ("dt%d" % uu, b.n("dt%d" % uu))], inc="sv")
                b.dma("scalar",
                      wdst[512 * g:512 * (g + 1), 128 * i:128 * (i + 1)].rearrange(
                          "(jj p) o -> p jj o", p=128),
                      tstage[:, uu, :].rearrange("p (jj o) -> p jj o", o=128),
                      waits=[("sv", sve)], sem="dt%d" % uu)
    def early_out():
        if BUILD_PHASES in ("p0", "p1a1", "p1a"):
            for tt in range(NSLOT):
                b.op("vector", lambda E, tt=tt: E.memset(accT[tt][:], 0.0))
            b.op("vector", None, inc="sv")
        svq = b.n("sv")
        for tt in range(NSLOT):
            b.dma("sync", out[128 * tt:128 * (tt + 1), :], accT[tt][:],
                  waits=[("sv", svq)] if tt == 0 else (), sem="so")
        b.op("sync", None, waits=[("so", b.n("so"))])
        return _finish(nc, b)

    b.barrier()
    if BUILD_PHASES == "p0":
        return early_out()

    xnT = arA[:].rearrange("p (c t) -> p c t", t=1024)
    x2Tv = hT[:]
    xdb = arBf[:, 0:4096].rearrange("p (u d) -> p u d", u=2)
    xsc = arBf[:, 4096:6144]
    xsq = arB[:][:, 8192:10240]
    xnbf = arE[:][:, 0:2048]
    x2bf = arC[:][:, 13312:15360]
    kwsl = arC[:][:, 0:4096].rearrange("p (u c o) -> p u c o", u=2, c=NT)
    vwsl = arC[:][:, 4096:12288].rearrange("p (u c o) -> p u c o", u=2, c=NT)
    estg = arC[:][:, 12288:13312].rearrange("p (u d) -> p u d", u=2)

    def stat_chain(tt, sv_sum):
        s = lambda i: stats[:, tt, i:i + 1]
        steps = [
            lambda E: E.activation(s(2), s(0), AF.Copy, scale=1.0 / DM),
            lambda E: E.activation(s(3), s(2), AF.Square),
            lambda E: E.activation(s(4), s(1), AF.Copy, scale=1.0 / DM),
            lambda E: E.activation(s(5), s(3), AF.Relu, scale=-1.0, bias=s(4)),
            lambda E: E.activation(s(6), s(5), AF.Ln, bias=cst[:, 0:1]),
            lambda E: E.activation(s(7), s(6), AF.Exp, scale=-0.5),
            lambda E: E.activation(s(8), s(6), AF.Exp, scale=-1.0),
            lambda E: E.activation(s(9), s(8), AF.Relu, scale=-EPS_LN, bias=cst[:, 2:3]),
            lambda E: E.activation(s(10), s(9), AF.Ln, bias=cst[:, 1:2]),
            lambda E: E.activation(s(11), s(6), AF.Copy, scale=-0.5),
            lambda E: E.activation(s(12), s(10), AF.Exp, scale=-0.5, bias=s(11)),
        ]
        last = 0
        for i, fn in enumerate(steps):
            w = [("sv", sv_sum)] if i == 0 else [("sa", last)]
            last = b.op("scalar", fn, waits=w, inc="sa")
        return last

    def emit_pass(src, own):
        for tt in range(8):
            u = tt % 2
            dl_x = b.dma("sync", xdb[:, u, :], src[128 * tt:128 * (tt + 1), :],
                         waits=[("sv", b.n("sv")), ("sa", b.n("sa"))], sem="dx%d" % u)
            sv_sum = b.op("vector", lambda E, u=u, tt=tt: E.tensor_reduce(
                stats[:, tt, 0:1], xdb[:, u, :], mybir.AxisListType.X, ALU.add),
                waits=[("dx%d" % u, dl_x)], inc="sv")
            b.op("scalar", lambda E, u=u, tt=tt: E.activation(
                xsq[:], xdb[:, u, :], AF.Square, accum_out=stats[:, tt, 1:2]),
                waits=[("dx%d" % u, dl_x), ("sv", b.n("sv"))])
            sa_c = stat_chain(tt, sv_sum)
            s = lambda i: stats[:, tt, i:i + 1]
            sv_xn = b.op("vector", lambda E, u=u, m=s(2), r=s(7): E.tensor_scalar(
                xsc, xdb[:, u, :], m, r, ALU.subtract, ALU.mult),
                waits=[("sa", sa_c)], inc="sv")
            sv_xnb = b.op("vector", lambda E: E.tensor_copy(xnbf, xsc),
                          waits=[("sv", sv_xn)], inc="sv")
            if own:
                b.op("vector", lambda E, u=u, tt=tt: E.tensor_tensor(
                    accT[tt][:], xdb[:, u, :], xsc, ALU.add), waits=[("sv", sv_xnb)], inc="sv")
            sv_x2 = b.op("vector", lambda E, u=u, m=s(2), r=s(12): E.tensor_scalar(
                xsc, xdb[:, u, :], m, r, ALU.subtract, ALU.mult),
                waits=[("sa", sa_c), ("sv", b.n("sv"))], inc="sv")
            sv_x2b = b.op("vector", lambda E: E.tensor_tensor(x2bf, xsc, ln1b[:], ALU.mult),
                          waits=[("sv", sv_x2)], inc="sv")
            for (srcbf, dstT, svr) in ((xnbf, xnT, sv_xnb), (x2bf, x2Tv, sv_x2b)):
                for g in range(4):
                    for jj in range(4):
                        j = 4 * g + jj
                        ws = []
                        if jj == 0:
                            ws = [("sv", max(svr, b.n("sv")))]
                        b.op("tensor", lambda E, j=j, jj=jj, g=g, srcbf=srcbf: E.transpose(
                            pT2[:, (g % 2) * 1024 + jj * 128:(g % 2) * 1024 + jj * 128 + 128],
                            srcbf[:, 128 * j:128 * (j + 1)], id_bf[:]),
                            waits=ws)
                    stt = b.op("tensor", None, inc="st")
                    b.op("vector", lambda E, g=g, tt=tt, dstT=dstT: E.tensor_copy(
                        dstT[:, 4 * g:4 * g + 4, 128 * tt:128 * (tt + 1)],
                        pT2[:, (g % 2) * 1024:(g % 2) * 1024 + 512].rearrange("p (c t) -> p c t", t=128)),
                        waits=[("st", stt)], inc="sv")
        return b.n("sv")

    def proj_fm(wsrc_bf, dst, rhsT, sv_ready, col0):
        for oc in range(NT):
            u = oc % 2
            dli = b.dma("sync", kwsl[:, u, :, :],
                        wsrc_bf[:, 128 * oc:128 * (oc + 1)].rearrange("(c p) o -> p c o", p=128),
                        waits=[("st", b.n("st") - 1)] if oc >= 2 else [("st", b.n("st"))],
                        sem="dk%d" % u)
            for tch in range(2):
                pu = tch % 2
                pdst = (pA if pu == 0 else pB)[:, 0:512]
                w0 = [("dk%d" % u, dli), ("sv", sv_ready), ("sv", b.n("sv"))]
                for c in range(NT):
                    b.op("tensor", lambda E, u=u, c=c, tch=tch, pdst=pdst, rhsT=rhsT: E.matmul(
                        pdst, kwsl[:, u, c, :], rhsT[:, c, 512 * tch:512 * (tch + 1)],
                        start=(c == 0), stop=(c == NT - 1)),
                        waits=w0 if c == 0 else ())
                stt = b.op("tensor", None, inc="st")
                eu = (2 * oc + tch) % 2
                sve = b.op("vector", lambda E, eu=eu, pdst=pdst: E.tensor_copy(
                    estg[:, eu, :], pdst),
                    waits=[("st", stt), ("de%d" % eu, b.n("de%d" % eu))], inc="sv")
                b.dma("scalar", dst[oc][:, col0 + 512 * tch:col0 + 512 * (tch + 1)],
                      estg[:, eu, :], waits=[("sv", sve)], sem="de%d" % eu)

    K1l = [K1d[h] for h in range(NH)]
    K2l = [K2d[h] for h in range(NH)]
    for p in range(NPASS):
        sv_ready = emit_pass(xf[PTOK * p:PTOK * (p + 1), :], own=False)
        proj_fm(k1w, K1l, xnT, sv_ready, PTOK * p)
        for avs in range(8):
            u = avs % 2
            dli = b.dma("sync", vwsl[:, u, :, :],
                        v1w[:, 256 * avs:256 * (avs + 1)].rearrange("(c p) o -> p c o", p=128),
                        waits=[("st", b.n("st") - 1)] if avs >= 2 else [("st", b.n("st"))],
                        sem="dv%d" % u)
            for tt in range(8):
                pu = tt % 2
                pdst = (pA if pu == 0 else pB)[:, 512:768]
                w0 = [("dv%d" % u, dli), ("sv", sv_ready), ("sv", b.n("sv"))]
                for c in range(NT):
                    b.op("tensor", lambda E, u=u, c=c, tt=tt, pdst=pdst: E.matmul(
                        pdst, xnT[:, c, 128 * tt:128 * (tt + 1)], vwsl[:, u, c, :],
                        start=(c == 0), stop=(c == NT - 1)),
                        waits=w0 if c == 0 else ())
                stt = b.op("tensor", None, inc="st")
                eu = (8 * avs + tt) % 2
                sve = b.op("vector", lambda E, eu=eu, pdst=pdst: E.tensor_copy(
                    estg[:, eu, 0:256], pdst),
                    waits=[("st", stt), ("de%d" % eu, b.n("de%d" % eu))], inc="sv")
                h0 = avs * 2
                b.dma("scalar",
                      V1d[h0:h0 + 2, PTOK * p + 128 * tt:PTOK * p + 128 * (tt + 1), :].rearrange(
                          "h t d -> t h d"),
                      estg[:, eu, 0:256].rearrange("p (h d) -> p h d", h=2),
                      waits=[("sv", sve)], sem="de%d" % eu)
        proj_fm(wk2t, K2l, x2Tv, sv_ready, PTOK * p)
        dwv = b.dma("sync", kwsl[:, 0, :, :].rearrange("p c o -> p (c o)").rearrange(
            "p (c x) -> p c x", x=256) if False else kwsl[:].rearrange("p u c o -> p (u c o)")[:, 0:NT * 256].rearrange("p (c x) -> p c x", x=256),
            wvwoad.rearrange("(c p) x -> p c x", p=128),
            waits=[("st", b.n("st")), ("dwv", b.n("dwv"))], sem="dwv")
        wvsl = kwsl[:].rearrange("p u c o -> p (u c o)")[:, 0:NT * 256].rearrange("p (c x) -> p c x", x=256)
        pv2 = pC[:, 0:256]
        for tt in range(8):
            w0 = [("sv", sv_ready), ("sv", b.n("sv")), ("st", b.n("st")), ("dwv", dwv)] if tt == 0 else [("sv", b.n("sv"))]
            for c in range(NT):
                b.op("tensor", lambda E, c=c, tt=tt: E.matmul(
                    pv2, x2Tv[:, c, 128 * tt:128 * (tt + 1)], wvsl[:, c, :],
                    start=(c == 0), stop=(c == NT - 1)),
                    waits=w0 if c == 0 else ())
            stt = b.op("tensor", None, inc="st")
            b.op("vector", lambda E, tt=tt: E.tensor_copy(v2stage[:, tt, :], pv2),
                 waits=[("st", stt), ("dvo", b.n("dvo"))], inc="sv")
        svv = b.n("sv")
        for h in range(NH):
            b.dma("scalar", V2d[h, :, 8 * p:8 * (p + 1), :], v2stage[:, :, 16 * h:16 * (h + 1)],
                  waits=[("sv", svv)] if h == 0 else (), sem="dvo")
        if BUILD_PHASES == "p1a1":
            b.barrier()
            return early_out()

    b.barrier()
    if BUILD_PHASES == "p1a":
        return early_out()

    sv_ready = emit_pass(xo, own=True)
    Q1l = [Q1d[h] for h in range(NH)]
    Q2l = [Q2d[h] for h in range(NH)]
    proj_fm(q1w, Q1l, xnT, sv_ready, 0)
    proj_fm(wq2t, Q2l, x2Tv, sv_ready, 0)
    hTv = hT[:]
    st_q2 = b.n("st")
    ln2bv = x2bf
    l2_row = arAf[0:1, 0:DM]
    dl2 = b.dma("sync", l2_row, ln2r, waits=[("st", st_q2)], sem="di2")
    for _h in range(2):
        b.op("tensor", lambda E, _h=_h: E.matmul(pA[:, 512 * _h:512 * (_h + 1)], ones_r[:],
                                                 l2_row[0:1, 512 * _h:512 * (_h + 1)], start=True, stop=True),
             waits=[("di2", dl2), ("sv", b.n("sv"))] if _h == 0 else ())
    for _h in range(2):
        b.op("tensor", lambda E, _h=_h: E.matmul(pB[:, 512 * _h:512 * (_h + 1)], ones_r[:],
                                                 l2_row[0:1, 1024 + 512 * _h:1024 + 512 * (_h + 1)], start=True, stop=True))
    stl2b = b.op("tensor", None, inc="st")
    b.op("vector", lambda E: E.tensor_copy(ln2bv[:, 0:1024], pA[:]), waits=[("st", stl2b)])
    svl2 = b.op("vector", lambda E: E.tensor_copy(ln2bv[:, 1024:2048], pB[:]), inc="sv")
    for tt in range(8):
        u = tt % 2
        dl_x = b.dma("sync", xdb[:, u, :], xo[128 * tt:128 * (tt + 1), :],
                     waits=[("sv", b.n("sv")), ("sa", b.n("sa"))], sem="dx%d" % u)
        s = lambda i: stats[:, tt, i:i + 1]
        sv_x2 = b.op("vector", lambda E, u=u, m=s(2), r=s(12): E.tensor_scalar(
            xsc, xdb[:, u, :], m, r, ALU.subtract, ALU.mult),
            waits=[("dx%d" % u, dl_x), ("sv", b.n("sv"))], inc="sv")
        sv_hb = b.op("vector", lambda E: E.tensor_tensor(xnbf, xsc, ln2bv, ALU.mult),
                     waits=[("sv", sv_x2), ("sv", svl2)], inc="sv")
        for g in range(4):
            for jj in range(4):
                j = 4 * g + jj
                ws = []
                if jj == 0:
                    ws = [("sv", max(sv_hb, b.n("sv"))), ("st", st_q2)]
                b.op("tensor", lambda E, j=j, jj=jj, g=g: E.transpose(
                    pT2[:, (g % 2) * 1024 + jj * 128:(g % 2) * 1024 + jj * 128 + 128],
                    xnbf[:, 128 * j:128 * (j + 1)], id_bf[:]),
                    waits=ws)
            stt = b.op("tensor", None, inc="st")
            b.op("vector", lambda E, g=g, tt=tt: E.tensor_copy(
                hTv[:, 4 * g:4 * g + 4, 128 * tt:128 * (tt + 1)],
                pT2[:, (g % 2) * 1024:(g % 2) * 1024 + 512].rearrange("p (c t) -> p c t", t=128)),
                waits=[("st", stt)], inc="sv")
    sv_h2 = b.n("sv")
    ph = pC[0:16, 0:1024]
    for tch in range(2):
        w0 = [("sv", sv_h2), ("st", b.n("st"))]
        for c in range(NT):
            b.op("tensor", lambda E, c=c, tch=tch: E.matmul(
                ph[:, 512 * tch:512 * (tch + 1)], a2_bf[:, c, :],
                hTv[:, c, 512 * tch:512 * (tch + 1)],
                start=(c == 0), stop=(c == NT - 1)),
                waits=w0 if c == 0 else ())
    stt = b.op("tensor", None, inc="st")
    b.op("vector", lambda E: E.tensor_copy(hAr, ph), waits=[("st", stt)], inc="sv")

    smp = arC[:][:, 0:512]
    ssq = arC[:][:, 512:1024]
    pr = pD[0:1, 0:512]
    for idx, srcl in enumerate([K1l, K2l, Q1l, Q2l]):
        for h in range(NH):
            dls = b.dma("sync", smp, srcl[h][:, 0:512],
                        waits=[("sv", b.n("sv")), ("sa", b.n("sa")), ("st", b.n("st")),
                               ("de0", b.n("de0")), ("de1", b.n("de1"))], sem="dsm")
            sas = b.op("scalar", lambda E: E.activation(ssq, smp, AF.Square),
                       waits=[("dsm", dls), ("st", b.n("st"))], inc="sa")
            stt = b.op("tensor", lambda E: E.matmul(pr, ones_c[:], ssq, start=True, stop=True),
                       waits=[("sa", sas), ("sv", b.n("sv"))], inc="st")
            b.op("vector", lambda E, idx=idx, h=h: E.tensor_reduce(
                nrm4[0:1, idx * 16 + h:idx * 16 + h + 1], pr, mybir.AxisListType.X, ALU.max),
                waits=[("st", stt)], inc="sv")
    svn = b.n("sv")
    t1 = arEf[0:1, 0:32]
    b.op("gpsimd", lambda E: E.tensor_tensor(t1[0:1, 0:16], nrm4[0:1, 32:48], nrm4[0:1, 0:16], ALU.mult),
         waits=[("sv", svn)])
    sg2 = b.op("gpsimd", lambda E: E.tensor_tensor(t1[0:1, 16:32], nrm4[0:1, 48:64], nrm4[0:1, 16:32], ALU.mult),
               inc="sg")
    sa1 = b.op("scalar", lambda E: E.activation(mrow[0:1, :], t1, AF.Ln), waits=[("sg", sg2)], inc="sa")
    sa2 = b.op("scalar", lambda E: E.activation(mrow[0:1, :], mrow[0:1, :], AF.Exp, scale=0.5),
               waits=[("sa", sa1)], inc="sa")
    sa3 = b.op("scalar", lambda E: E.mul(mrow[0:1, :], mrow[0:1, :], -1.25 * C), waits=[("sa", sa2)], inc="sa")
    sv_cl = b.op("vector", lambda E: E.tensor_scalar_max(mrow[0:1, :], mrow[0:1, :], -30.0),
                 waits=[("sa", sa3)], inc="sv")
    stt = b.op("tensor", lambda E: E.matmul(pD[:, 0:32], ones_r[:], mrow[0:1, :], start=True, stop=True),
               waits=[("sv", sv_cl)], inc="st")
    b.op("vector", lambda E: E.tensor_copy(mhat[:], pD[:, 0:32]), waits=[("st", stt)], inc="sv")

    if DEBUG_OUTS:
        sv_all = b.n("sv")
        b.dma("scalar", dbg["d_V2p"], V2d, waits=[("sv", sv_all)], sem="so")
        b.dma("scalar", dbg["d_k1w"], k1w, sem="so")
        b.dma("scalar", dbg["d_wq2t"], wq2t, sem="so")
        b.dma("sync", dbg["d_mhat"], mhat[:], waits=[("sv", sv_all)], sem="so")
        b.dma("sync", dbg["d_hA"], hAr[:], waits=[("sv", sv_all)], sem="so")
        for tt in range(NSLOT):
            b.dma("sync", dbg["d_acc0"][128 * tt:128 * (tt + 1), :], accT[tt][:],
                  waits=[("sv", sv_all)], sem="so")
        for h in range(NH):
            b.dma("scalar", dbg["d_K1"][h], K1d[h], sem="so")
            b.dma("scalar", dbg["d_V1"][h], V1d[h], sem="so")
            b.dma("scalar", dbg["d_K2"][h], K2d[h], sem="so")
            b.dma("scalar", dbg["d_Q1"][h], Q1d[h], sem="so")
            b.dma("scalar", dbg["d_Q2"][h], Q2d[h], sem="so")

    b.barrier()

    if BUILD_PHASES == "p1":
        return early_out()

    ksb1 = arB[:][:, 0:8192]
    vsb = arC[:][:, 0:8192].rearrange("p (n d) -> p n d", d=128)
    qsb = arC[:][:, 8192:10240].rearrange("p (u t) -> p u t", u=2)
    ptb = arC[:][:, 10240:12288].rearrange("p (u t) -> p u t", u=2)
    rcb = arCf[:, 6144:7168]
    sumr = arEf[0:1, 0:1024]
    lnr = arEf[32:33, 0:1024]

    def attention(attn_i, Kl, Ql, av_out_fn, v_load_fn, v_lhsT_fn, av_rows):
        two_mm = (av_rows == 128)
        head_s_fin = {}
        for h in range(NH):
            u = h % 2
            wk_ld = [("st", head_s_fin[h - 1])] if h >= 1 else [("st", b.n("st"))]
            wq_ld = [("st", head_s_fin[h - 2])] if h >= 2 else [("st", b.n("st"))]
            dk = b.dma("sync", ksb1, Kl[h], waits=wk_ld, sem="dKs")
            dq = b.dma("sync", qsb[:, u, :], Ql[h], waits=wq_ld, sem="dq%d" % u)
            dv = v_load_fn(h)
            bias = mhat[:, attn_i * 16 + h:attn_i * 16 + h + 1]
            sa_exp = {}
            st_av = {}
            prev_done_sv = b.n("sv")
            sa_head0 = b.n("sa")
            for k in range(NKT):
                jmin = k // 8
                qoff = 128 * jmin
                nq = TPC - qoff
                pu = k % 2
                ps_k = (pA if pu == 0 else pB)
                w0 = [("dKs", dk), ("dq%d" % u, dq)]
                if k >= 2:
                    w0.append(("sa", sa_exp[k - 2]))
                else:
                    w0.append(("sa", sa_head0))
                    w0.append(("sv", prev_done_sv))
                nch = (nq + 511) // 512
                for ch in range(nch):
                    c0 = 512 * ch
                    c1 = min(nq, c0 + 512)
                    b.op("tensor", lambda E, u=u, k=k, c0=c0, c1=c1, ps_k=ps_k, qoff=qoff: E.matmul(
                        ps_k[:, c0:c1], ksb1[:, 128 * k:128 * (k + 1)],
                        qsb[:, u, qoff + c0:qoff + c1], start=True, stop=True),
                        waits=w0 if ch == 0 else ())
                st_s = b.op("tensor", None, inc="st")
                if k == NKT - 1:
                    head_s_fin[h] = st_s
                we = [("st", st_s)]
                if k >= 2:
                    we.append(("st", st_av[k - 2]))
                sa_exp[k] = b.op("scalar", lambda E, pu=pu, nq=nq, ps_k=ps_k, bias=bias: E.activation(
                    ptb[:, pu, 0:nq], ps_k[:, 0:nq], AF.Exp, bias=bias, scale=C),
                    waits=we, inc="sa")
                sg_m = b.op("gpsimd", lambda E, pu=pu, k=k: E.tensor_tensor(
                    ptb[:, pu, 0:128], ptb[:, pu, 0:128], mask_sb[:, k, :], ALU.mult),
                    waits=[("sa", sa_exp[k])], inc="sg")
                wa = [("sg", sg_m)]
                if k == 0:
                    wa.append(("sv", prev_done_sv))
                    if dv:
                        wa.append(("dVs", dv))
                pts = [qoff, qoff + 128]
                if qoff + 128 < 512:
                    pts.append(512)
                if pts[-1] < TPC:
                    pts.append(TPC)
                avch = [(a0, a1) for a0, a1 in zip(pts[:-1], pts[1:]) if a1 > a0]
                for ci, (a0, a1) in enumerate(avch):
                    stopf = (a0 == qoff and k in (31, 63))
                    startf = (k == 0 and a0 in (0, 512))
                    if two_mm:
                        b.op("tensor", lambda E, k=k, a0=a0, a1=a1, pu=pu, qoff=qoff, stopf=stopf, startf=startf: E.matmul(
                            pC[0:av_rows, a0:a1], v_lhsT_fn(k), ptb[:, pu, a0 - qoff:a1 - qoff],
                            start=startf, stop=stopf),
                            waits=wa if ci == 0 else ())
                        b.op("tensor", lambda E, k=k, a0=a0, a1=a1, pu=pu, qoff=qoff, stopf=stopf, startf=startf: E.matmul(
                            pD[0:1, a0:a1], ones_c[:], ptb[:, pu, a0 - qoff:a1 - qoff],
                            start=startf, stop=stopf))
                    else:
                        sums_fn = lambda E, k=k, a0=a0, a1=a1, pu=pu, qoff=qoff, stopf=stopf, startf=startf: E.matmul(
                            pC[0:33, a0:a1], ones32[:], ptb[:, pu, a0 - qoff:a1 - qoff],
                            start=startf, stop=stopf)
                        av_fn = lambda E, k=k, a0=a0, a1=a1, pu=pu, qoff=qoff: E.matmul(
                            pC[0:av_rows, a0:a1], v_lhsT_fn(k), ptb[:, pu, a0 - qoff:a1 - qoff],
                            start=False, stop=False)
                        if k == 0:
                            b.op("tensor", sums_fn, waits=wa if ci == 0 else ())
                            b.op("tensor", av_fn)
                        else:
                            b.op("tensor", av_fn, waits=wa if ci == 0 else ())
                            b.op("tensor", sums_fn)
                st_av[k] = b.op("tensor", None, inc="st")
            sum_src = pD[0:1, 0:1024] if two_mm else pC[32:33, 0:1024]
            sv_s = b.op("vector", lambda E, sum_src=sum_src: E.tensor_copy(sumr, sum_src),
                        waits=[("st", st_av[NKT - 1])], inc="sv")
            sa_l = b.op("scalar", lambda E: E.activation(lnr, sumr, AF.Ln),
                        waits=[("sv", sv_s)], inc="sa")
            sa_r = b.op("scalar", lambda E: E.activation(sumr, lnr, AF.Exp, scale=-1.0),
                        waits=[("sa", sa_l)], inc="sa")
            b.op("tensor", lambda E: E.matmul(pA[:, 0:512], ones_r[:], sumr[0:1, 0:512],
                                              start=True, stop=True),
                 waits=[("sa", sa_r)])
            st_b = b.op("tensor", lambda E: E.matmul(pA[:, 512:1024], ones_r[:], sumr[0:1, 512:1024],
                                                     start=True, stop=True), inc="st")
            sv_rb = b.op("vector", lambda E: E.tensor_copy(rcb, pA[:, 0:1024]),
                         waits=[("st", st_b)], inc="sv")
            av_out_fn(h, sv_rb)

    av1T = arA[:].rearrange("p (c t) -> p c t", t=1024)

    def v1_load(h):
        return b.dma("sync", vsb[:], V1d[h].rearrange("(n p) d -> p n d", p=128),
                     waits=[("st", b.n("st"))] if h >= 1 else [], sem="dVs")

    def av1_out(h, sv_rb):
        b.op("vector", lambda E, h=h: E.tensor_tensor(
            av1T[:, h, :], pC[:, 0:1024], rcb, ALU.mult),
            waits=[("sv", sv_rb)], inc="sv")

    attention(0, K1l, Q1l, av1_out, v1_load, lambda k: vsb[:, k, :], 128)

    def av2_out(h, sv_rb):
        svt = b.op("vector", lambda E: E.tensor_tensor(
            msc, pC[0:16, 0:1024], rcb[0:16, :], ALU.mult),
            waits=[("sv", sv_rb), ("st", b.n("st"))], inc="sv")
        b.op("tensor", lambda E, h=h: E.matmul(
            pD[0:16, 0:512], idf16[:], msc[:, 0:512], start=(h == 0), stop=(h == NH - 1)),
            waits=[("sv", svt)])
        b.op("tensor", lambda E, h=h: E.matmul(
            pD[0:16, 512:1024], idf16[:], msc[:, 512:1024], start=(h == 0), stop=(h == NH - 1)),
            inc="st")
        if h == NH - 1:
            b.op("vector", lambda E: E.tensor_copy(modpre, pD[0:16, 0:1024]),
                 waits=[("st", b.n("st"))], inc="sv")

    v2sb = arC[:][:, 14336:14336 + NKT * 16].rearrange("p (n r) -> p n r", r=16)

    def v2_load(h):
        return b.dma("sync", v2sb[:], V2d[h],
                     waits=[("st", b.n("st"))], sem="dVs")

    attention(1, K2l, Q2l, av2_out, v2_load, lambda k: v2sb[:, k, :], 16)

    if DEBUG_OUTS:
        sv_all = b.n("sv")
        for h in range(NH):
            b.dma("sync", dbg["d_av1"].rearrange("p h t -> p h t")[:, h, :], av1T[:, h, :],
                  waits=[("sv", sv_all)], sem="so")
        b.dma("sync", dbg["d_mod"], modpre[:], waits=[("sv", sv_all)], sem="so")

    b.barrier()

    wof = arCf[:, 0:512]
    wob = arC[:][:, 1024:1024 + NT * 512].rearrange("p (c o) -> p c o", o=512)
    for ds in range(4):
        for i in range(NT):
            dli = b.dma("sync", wof, wout[128 * i:128 * (i + 1), 512 * ds:512 * (ds + 1)],
                        waits=[("sv", b.n("sv")), ("st", b.n("st"))] if i == 0 else [("sv", b.n("sv"))],
                        sem="dwo")
            b.op("vector", lambda E, i=i: E.tensor_copy(wob[:, i, :], wof),
                 waits=[("dwo", dli)], inc="sv")
        sv_w = b.n("sv")
        for tt in range(NSLOT):
            pu = tt % 2
            pdst = (pA if pu == 0 else pB)[:, 0:512]
            w0 = [("sv", sv_w), ("sv", b.n("sv"))]
            for i in range(NT):
                b.op("tensor", lambda E, i=i, tt=tt, pdst=pdst: E.matmul(
                    pdst, av1T[:, i, 128 * tt:128 * (tt + 1)], wob[:, i, :],
                    start=(i == 0), stop=(i == NT - 1)),
                    waits=w0 if i == 0 else ())
            stt = b.op("tensor", None, inc="st")
            b.op("vector", lambda E, tt=tt, ds=ds, pdst=pdst: E.tensor_tensor(
                accT[tt][:, 512 * ds:512 * (ds + 1)], accT[tt][:, 512 * ds:512 * (ds + 1)],
                pdst, ALU.add), waits=[("st", stt)], inc="sv")
    b.barrier()

    sa_t = b.op("scalar", lambda E: E.activation(msc, modpre, AF.Tanh), inc="sa")
    sv_p = b.op("vector", lambda E: E.scalar_tensor_tensor(
        prod_bf[:], hAr, 0.1, msc, ALU.mult, ALU.mult), waits=[("sa", sa_t)], inc="sv")

    wT = arC[:][:, 0:8192].rearrange("p (c f) -> p c f", f=512)
    wdnT = arC[:][:, 8192:8192 + 4 * 2048].rearrange("p (c o) -> p c o", o=2048)
    g_all = arB[:][:, 8192:12288].rearrange("p (t f) -> p t f", f=512)
    wstg = arBf[:, 0:512]
    wstgb = arB[:][:, 1024:1536]
    wdnf = arBf[:, 1024:1536]
    wdnb = arB[:][:, 3072:3584]
    bsl = arB[:][0:16, 4096:4608]
    gubf = arB[:][:, 6144:6656]
    guT = arB[:][:, 7168:7680].rearrange("p (q t) -> p q t", q=4)
    hTv5 = hT[:]
    prod_bf = arB[:][0:16, 4608:4608 + TPC]

    def build_wT(wsrc, dstT, fs, kind):
        if kind == "row":
            for dc in range(NT):
                dli = b.dma("sync", wstg.rearrange("p (q d) -> p q d", q=4),
                            wsrc[512 * fs:512 * (fs + 1), 128 * dc:128 * (dc + 1)].rearrange(
                                "(q p) d -> p q d", p=128),
                            waits=[("sv", b.n("sv")), ("st", b.n("st"))], sem="dws")
                svc = b.op("vector", lambda E: E.tensor_copy(wstgb, wstg),
                           waits=[("dws", dli)], inc="sv")
                for q in range(4):
                    b.op("tensor", lambda E, q=q: E.transpose(
                        pT2[:, q * 128:(q + 1) * 128], wstgb[:, 128 * q:128 * (q + 1)], id_bf[:]),
                        waits=[("sv", svc), ("sv", b.n("sv") - 1)] if q == 0 else ())
                stt = b.op("tensor", None, inc="st")
                b.op("vector", lambda E, dc=dc, dstT=dstT: E.tensor_copy(dstT[:, dc, :], pT2[:, 0:512]),
                     waits=[("st", stt)], inc="sv")
        else:
            for do in range(NT):
                dli = b.dma("sync", wdnf, wsrc[128 * do:128 * (do + 1), 512 * fs:512 * (fs + 1)],
                            waits=[("sv", b.n("sv")), ("st", b.n("st"))], sem="dwd")
                svc = b.op("vector", lambda E: E.tensor_copy(wdnb, wdnf),
                           waits=[("dwd", dli)], inc="sv")
                for fc in range(4):
                    b.op("tensor", lambda E, fc=fc: E.transpose(
                        pT2[:, fc * 128:(fc + 1) * 128], wdnb[:, 128 * fc:128 * (fc + 1)], id_bf[:]),
                        waits=[("sv", svc), ("sv", b.n("sv") - 1)] if fc == 0 else ())
                stt = b.op("tensor", None, inc="st")
                b.op("vector", lambda E, do=do, dstT=dstT: E.tensor_copy(
                    dstT[:, :, 128 * do:128 * (do + 1)],
                    pT2[:, 0:512].rearrange("p (c o) -> p c o", o=128)),
                    waits=[("st", stt)], inc="sv")

    for fs in range(NT):
        dlb = b.dma("sync", bsl, bbd[:, 512 * fs:512 * (fs + 1)],
                    waits=[("sv", b.n("sv")), ("st", b.n("st"))], sem="dbl")
        build_wT(wdn, wdnT, fs, "col")
        build_wT(win, wT, fs, "row")
        sv_w = b.n("sv")
        for tt in range(NSLOT):
            pu = tt % 2
            pg = (pA if pu == 0 else pB)[:, 0:512]
            w0 = [("sv", sv_w), ("sv", sv_p), ("dbl", dlb), ("sa", b.n("sa"))]
            for c in range(NT):
                b.op("tensor", lambda E, c=c, tt=tt, pg=pg: E.matmul(
                    pg, hTv5[:, c, 128 * tt:128 * (tt + 1)], wT[:, c, :],
                    start=(c == 0), stop=False),
                    waits=w0 if c == 0 else ())
            b.op("tensor", lambda E, tt=tt, pg=pg: E.matmul(
                pg, prod_bf[:, 128 * tt:128 * (tt + 1)], bsl,
                start=False, stop=True))
            st_g = b.op("tensor", None, inc="st")
            b.op("scalar", lambda E, tt=tt, pg=pg: E.activation(g_all[:, tt, :], pg, AF.Silu),
                 waits=[("st", st_g), ("sv", b.n("sv"))], inc="sa")
        sa_gate = b.n("sa")
        build_wT(wup, wT, fs, "row")
        sv_w2 = b.n("sv")
        for tt in range(NSLOT):
            pu = tt % 2
            pg = (pA if pu == 0 else pB)[:, 512:1024]
            w0 = [("sv", sv_w2), ("sa", sa_gate), ("sv", b.n("sv"))]
            for c in range(NT):
                b.op("tensor", lambda E, c=c, tt=tt, pg=pg: E.matmul(
                    pg, hTv5[:, c, 128 * tt:128 * (tt + 1)], wT[:, c, :],
                    start=(c == 0), stop=(c == NT - 1)),
                    waits=w0 if c == 0 else ())
            st_u = b.op("tensor", None, inc="st")
            sv_gu = b.op("vector", lambda E, tt=tt, pg=pg: E.tensor_tensor(
                gubf, g_all[:, tt, :], pg, ALU.mult),
                waits=[("sa", sa_gate), ("st", st_u), ("sv", b.n("sv"))], inc="sv")
            for q in range(4):
                b.op("tensor", lambda E, q=q: E.transpose(
                    pT2[:, 1024 + q * 128:1024 + (q + 1) * 128],
                    gubf[:, 128 * q:128 * (q + 1)], id_bf[:]),
                    waits=[("sv", sv_gu)] if q == 0 else ())
            st_t = b.op("tensor", None, inc="st")
            sv_gt = b.op("vector", lambda E: E.tensor_copy(
                guT[:].rearrange("p q t -> p (q t)"), pT2[:, 1024:1536]),
                waits=[("st", st_t)], inc="sv")
            for ds in range(4):
                pu2 = ds % 2
                pdst = pC[:, 512 * pu2:512 * (pu2 + 1)]
                w1 = [("sv", sv_gt), ("sv", b.n("sv"))]
                for fc in range(4):
                    b.op("tensor", lambda E, fc=fc, ds=ds, pdst=pdst: E.matmul(
                        pdst, guT[:, fc, :], wdnT[:, fc, 512 * ds:512 * (ds + 1)],
                        start=(fc == 0), stop=(fc == 3)),
                        waits=w1 if fc == 0 else ())
                stt = b.op("tensor", None, inc="st")
                b.op("vector", lambda E, tt=tt, ds=ds, pdst=pdst: E.tensor_tensor(
                    accT[tt][:, 512 * ds:512 * (ds + 1)], accT[tt][:, 512 * ds:512 * (ds + 1)],
                    pdst, ALU.add), waits=[("st", stt)], inc="sv")

    sv_f = b.n("sv")
    for tt in range(NSLOT):
        b.dma("sync", out[128 * tt:128 * (tt + 1), :], accT[tt][:],
              waits=[("sv", sv_f)], sem="so")
    b.op("sync", None, waits=[("so", b.n("so"))])
    return _finish(nc, b)


def _finish(nc, b):
    import contextlib
    es = contextlib.ExitStack()
    sems = {}
    names = set()
    for streams in b.streams.values():
        for fn, waits, inc, inck, is_dma in streams:
            if inc:
                names.add(inc)
            for s, v in waits:
                names.add(s)
    with es:
        for nm in sorted(names):
            sems[nm] = es.enter_context(nc.semaphore(nm))
        block = es.enter_context(nc.Block())
        bodies = b.replay(sems)
        block.sync(bodies["sync"])
        block.vector(bodies["vector"])
        block.scalar(bodies["scalar"])
        block.tensor(bodies["tensor"])
        block.gpsimd(bodies["gpsimd"])
    return nc


_ROWS = {}


def _core_rows(c):
    if c not in _ROWS:
        _ROWS[c] = np.concatenate([np.arange(128 * (8 * j + c), 128 * (8 * j + c) + 128)
                                   for j in range(NSLOT)])
    return _ROWS[c]


def kernel(L0_attn_x, L0_attn_w_qkv, L0_attn_w_out, L0_ffn_ln1_w, L0_ffn_ln2_w,
           L0_ffn_Wq, L0_ffn_Wk, L0_ffn_Wv, L0_ffn_Wo, L0_ffn_A, L0_ffn_B,
           L0_ffn_W_in_base, L0_ffn_W_up_base, L0_ffn_W_down):
    f32 = np.float32
    x = np.ascontiguousarray(np.asarray(L0_attn_x, dtype=f32).reshape(SEQ, DM))
    common = dict(
        xf=x,
        wqkv=np.ascontiguousarray(np.asarray(L0_attn_w_qkv, dtype=f32)),
        wq2=np.ascontiguousarray(np.asarray(L0_ffn_Wq, dtype=f32)),
        wk2=np.ascontiguousarray(np.asarray(L0_ffn_Wk, dtype=f32)),
        wv2=np.ascontiguousarray(np.asarray(L0_ffn_Wv, dtype=f32)),
        wo2=np.ascontiguousarray(np.asarray(L0_ffn_Wo, dtype=f32)),
        aa=np.ascontiguousarray(np.asarray(L0_ffn_A, dtype=f32)),
        bbi=np.ascontiguousarray(np.asarray(L0_ffn_B, dtype=f32)),
        ln1r=np.ascontiguousarray(np.asarray(L0_ffn_ln1_w, dtype=f32).reshape(1, DM)),
        ln2r=np.ascontiguousarray(np.asarray(L0_ffn_ln2_w, dtype=f32).reshape(1, DM)),
        wout=np.ascontiguousarray(np.asarray(L0_attn_w_out, dtype=f32)),
        win=np.ascontiguousarray(np.asarray(L0_ffn_W_in_base, dtype=f32)),
        wup=np.ascontiguousarray(np.asarray(L0_ffn_W_up_base, dtype=f32)),
        wdn=np.ascontiguousarray(np.asarray(L0_ffn_W_down, dtype=f32)),
        ident=np.eye(128, dtype=f32),
        kidx=np.ascontiguousarray(np.arange(SEQ, dtype=f32).reshape(NKT, 128).T),
    )
    in_maps = []
    for c in range(NCORES):
        rows = _core_rows(c)
        m = dict(common)
        m["xo"] = np.ascontiguousarray(x[rows])
        m["qidx"] = np.ascontiguousarray(rows.astype(f32).reshape(1, TPC))
        in_maps.append(m)
    nc = build_nc()
    res = run_bass_kernel_spmd(nc, in_maps, core_ids=list(range(NCORES)))
    outp = np.empty((SEQ, DM), dtype=f32)
    for c in range(NCORES):
        outp[_core_rows(c)] = res.results[c]["out"]
    return outp.reshape(1, SEQ, DM)


# version 06
# speedup vs baseline: 1.4230x; 1.4230x over previous
import numpy as np
import ml_dtypes
import concourse.bass as bass
import concourse.mybir as mybir
from concourse.bass_utils import run_bass_kernel_spmd

F32 = mybir.dt.float32
BF16 = mybir.dt.bfloat16
FP8 = mybir.dt.float8e4
AF = mybir.ActivationFunctionType
ALU = mybir.AluOpType

NCORES = 8
SEQ = 8192
DM = 2048
FF = 8192
NH = 16
NT = 16
NSLOT = 8
TPC = 1024
NKT = 64
NPASS = 8
PTOK = 1024
SCALE = float(1.0 / np.sqrt(128.0))
EPS_LN = 1e-5
EPS_RMS = 1e-6

ENGINES = ("sync", "vector", "scalar", "tensor", "gpsimd")

BUILD_PHASES = "all"
DEBUG_OUTS = False


class Sched:

    def __init__(self):
        self.streams = {e: [] for e in ENGINES}
        self.cnt = {}
        self.dma_sems = set()

    def n(self, sem):
        return self.cnt.get(sem, 0)

    def op(self, eng, fn=None, waits=(), inc=None, inck=1, is_dma=False):
        if fn is None and inc is not None and not waits:
            st = self.streams[eng]
            if st and st[-1][0] is not None and st[-1][2] == inc:
                return self.cnt.get(inc, 0)
        if inc is not None:
            self.cnt[inc] = self.cnt.get(inc, 0) + inck
        ws = tuple((s, v) for (s, v) in waits if v > 0)
        self.streams[eng].append((fn, ws, inc, inck, is_dma))
        return self.cnt.get(inc, 0) if inc else 0

    def dma(self, eng, out, in_, waits=(), sem=None):
        sem = sem or ("dl" if eng == "sync" else "dg")
        self.dma_sems.add(sem)
        return self.op(eng, lambda E, o=out, i=in_: E.dma_start(out=o, in_=i),
                       waits=waits, inc=sem, inck=16, is_dma=True)

    def barrier(self):
        marks = []
        for e in ENGINES:
            st = self.streams[e]
            if not st:
                continue
            fn, ws, inc, inck, isd = st[-1]
            if inc is not None and not isd:
                marks.append((inc, self.n(inc)))
            elif fn is not None and not isd:
                m = self.op(e, None, inc="b_" + e)
                marks.append(("b_" + e, m))
        marks += [(s, self.n(s)) for s in sorted(self.dma_sems)]
        for e in ENGINES:
            self.op(e, None, waits=marks)

    def replay(self, sems):
        def make(name):
            stream = self.streams[name]

            def body(E):
                last_real = None
                last_was_dma = False
                for fn, waits, inc, inck, is_dma in stream:
                    for s, v in waits:
                        E.wait_ge(sems[s], v)
                    if fn is not None:
                        ins = fn(E)
                        last_real = ins
                        last_was_dma = is_dma
                        if inc is not None:
                            ins.then_inc(sems[inc], inck)
                    elif inc is not None:
                        if last_real is not None and not (last_was_dma and inck % 16):
                            last_real.then_inc(sems[inc], inck)
                        else:
                            E.nop().then_inc(sems[inc], inck)
            return body
        return {e: make(e) for e in ENGINES}


def build_nc():
    nc = bass.Bass("TRN2", target_bir_lowering=False, dynamic_dma_scratch_size=512)
    b = Sched()
    QU = {0: "sync", 1: "scalar"}
    QO = {0: "scalar", 1: "sync"}
    C = SCALE

    dt = nc.dram_tensor
    xf = dt("xf", [SEQ, DM], F32, kind="ExternalInput").ap()
    xo = dt("xo", [TPC, DM], F32, kind="ExternalInput").ap()
    qidx = dt("qidx", [1, TPC], F32, kind="ExternalInput").ap()
    kidx = dt("kidx", [128, NKT], F32, kind="ExternalInput").ap()
    wqkv = dt("wqkv", [DM, 3 * DM], F32, kind="ExternalInput").ap()
    wq2 = dt("wq2", [DM, DM], F32, kind="ExternalInput").ap()
    wk2 = dt("wk2", [DM, DM], F32, kind="ExternalInput").ap()
    wv2 = dt("wv2", [DM, DM], F32, kind="ExternalInput").ap()
    wo2 = dt("wo2", [DM, DM], F32, kind="ExternalInput").ap()
    aa = dt("aa", [DM, 16], F32, kind="ExternalInput").ap()
    bbi = dt("bbi", [16, FF], F32, kind="ExternalInput").ap()
    ln1r = dt("ln1r", [1, DM], F32, kind="ExternalInput").ap()
    ln2r = dt("ln2r", [1, DM], F32, kind="ExternalInput").ap()
    wout = dt("wout", [DM, DM], F32, kind="ExternalInput").ap()
    win = dt("win", [FF, DM], F32, kind="ExternalInput").ap()
    wup = dt("wup", [FF, DM], F32, kind="ExternalInput").ap()
    wdn = dt("wdn", [DM, FF], F32, kind="ExternalInput").ap()
    ident = dt("ident", [128, 128], F32, kind="ExternalInput").ap()
    out = dt("out", [TPC, DM], F32, kind="ExternalOutput").ap()

    k1w = dt("k1w", [DM, DM], BF16, kind="Internal").ap()
    v1w = dt("v1w", [DM, DM], BF16, kind="Internal").ap()
    q1w = dt("q1w", [DM, DM], BF16, kind="Internal").ap()
    wq2t = dt("wq2t", [DM, DM], BF16, kind="Internal").ap()
    wk2t = dt("wk2t", [DM, DM], BF16, kind="Internal").ap()
    bbd = dt("bbd", [16, FF], BF16, kind="Internal").ap()
    wvwoad = dt("wvwoad", [DM, 256], BF16, kind="Internal").ap()
    V2d = dt("V2d", [NH, 128, NKT, 16], BF16, kind="Internal").ap()
    K1d = dt("K1d", [NH, 128, SEQ], BF16, kind="Internal").ap()
    V1d = dt("V1d", [NH, SEQ, 128], BF16, kind="Internal").ap()
    K2d = dt("K2d", [NH, 128, SEQ], BF16, kind="Internal").ap()
    Q1d = dt("Q1d", [NH, 128, TPC], BF16, kind="Internal").ap()
    Q2d = dt("Q2d", [NH, 128, TPC], BF16, kind="Internal").ap()

    dbg = {}
    if DEBUG_OUTS:
        for nm, shp in [("d_K1", [NH, 128, SEQ]), ("d_V1", [NH, SEQ, 128]),
                        ("d_K2", [NH, 128, SEQ]), ("d_Q1", [NH, 128, TPC]),
                        ("d_Q2", [NH, 128, TPC]), ("d_V2p", [NH, 128, NKT, 16]),
                        ("d_av1", [128, NH, TPC]), ("d_mod", [16, TPC]),
                        ("d_acc0", [TPC, DM]), ("d_mhat", [128, 32]),
                        ("d_hA", [16, TPC]), ("d_k1w", [DM, DM]), ("d_wq2t", [DM, DM])]:
            dbg[nm] = dt(nm, shp, F32 if nm in ("d_acc0", "d_mhat", "d_mod", "d_hA") else BF16,
                         kind="ExternalOutput").ap()

    sbA = nc.alloc_sbuf_tensor
    id_bf = sbA("id_bf", [128, 128], BF16)
    ones_r = sbA("ones_r", [1, 128], F32)
    ones_c = sbA("ones_c", [128, 1], BF16)
    kidx_sb = sbA("kidx_sb", [128, NKT], F32)
    a2_bf = sbA("a2_bf", [128, NT, 16], BF16)
    wvwoa = sbA("wvwoa", [128, NT, 16], BF16)
    ln1b = sbA("ln1b", [128, DM], BF16)
    mhat = sbA("mhat", [128, 32], F32)
    nrm4 = sbA("nrm4", [1, 64], F32)
    v2stage = sbA("v2stage", [128, NSLOT, 256], BF16)
    ones32 = sbA("ones32", [128, 33], BF16)
    modpre = sbA("modpre", [16, TPC], F32)[:]
    msc = sbA("msc", [16, TPC], F32)[:]
    hAr = sbA("hAr", [16, TPC], F32)[:]
    idf16 = sbA("idf16", [16, 16], F32)
    stats = sbA("stats", [128, NSLOT, 16], F32)
    mrow = sbA("mrow", [1, 32], F32)
    cst = sbA("cst", [128, 4], F32)
    hT = sbA("hT", [128, NT, TPC], BF16)
    arA = sbA("arA", [128, NT * 1024], BF16)
    arB = sbA("arB", [128, 12288], BF16)
    arC = sbA("arC", [128, 16384], BF16)
    arD = sbA("arD", [128, 4096], BF16)
    arE = sbA("arE", [128, 2048], BF16)
    accT = [sbA(f"acc{i}", [128, DM], F32) for i in range(NSLOT)]

    pA = nc.alloc_psum_tensor("pA", [128, 1024], F32)
    pB = nc.alloc_psum_tensor("pB", [128, 1024], F32)
    pC = nc.alloc_psum_tensor("pC", [128, 1024], F32)
    pD = nc.alloc_psum_tensor("pD", [128, 1024], F32)

    arBf = arB[:].bitcast(F32)
    arCf = arC[:].bitcast(F32)
    arEf = arE[:].bitcast(F32)
    arAf = arA[:].bitcast(F32)
    qib = arAf[:, 1024:2048]
    pT2 = pD[:].bitcast(BF16)

    id_f = arCf[:, 512:640]
    b.dma("sync", id_f, ident, sem="di")
    b.dma("sync", idf16[:], ident[0:16, 0:16], sem="di")
    b.dma("sync", kidx_sb[:], kidx, sem="di")
    aa_sb = arCf[:, 0:NT * 16].rearrange("p (c r) -> p c r", r=16)
    b.dma("sync", aa_sb, aa.rearrange("(c p) r -> p c r", p=128), sem="di")
    qi_row = arAf[0:1, 0:TPC]
    b.dma("sync", qi_row, qidx, sem="di")
    l1_row = arBf[0:1, 0:DM]
    b.dma("sync", l1_row, ln1r, sem="di")
    dl0 = b.n("di")
    b.op("vector", lambda E: E.tensor_copy(id_bf[:], id_f), waits=[("di", dl0)])
    b.op("vector", lambda E: E.memset(ones_r[:], 1.0))
    b.op("vector", lambda E: E.memset(ones_c[:], 1.0))
    b.op("vector", lambda E: E.memset(cst[:, 0:1], EPS_LN))
    b.op("vector", lambda E: E.memset(cst[:, 1:2], EPS_RMS))
    b.op("vector", lambda E: E.memset(cst[:, 2:3], 1.0))
    svz = b.op("vector", lambda E: E.memset(ones32[:], 0.0), inc="sv")
    b.op("vector", lambda E: E.memset(ones32[:, 32:33], 1.0), waits=[("sv", svz)])
    sv0 = b.op("vector", None, inc="sv")
    b.op("tensor", lambda E: E.matmul(pD[:, 0:512], ones_r[:], qi_row[0:1, 0:512], start=True, stop=True),
         waits=[("sv", sv0)])
    b.op("tensor", lambda E: E.matmul(pD[:, 512:1024], ones_r[:], qi_row[0:1, 512:1024], start=True, stop=True))
    for _h in range(2):
        b.op("tensor", lambda E, _h=_h: E.matmul(pA[:, 512 * _h:512 * (_h + 1)], ones_r[:],
                                                 l1_row[0:1, 512 * _h:512 * (_h + 1)], start=True, stop=True))
    for _h in range(2):
        b.op("tensor", lambda E, _h=_h: E.matmul(pB[:, 512 * _h:512 * (_h + 1)], ones_r[:],
                                                 l1_row[0:1, 1024 + 512 * _h:1024 + 512 * (_h + 1)], start=True, stop=True))
    st1b = b.op("tensor", None, inc="st")
    sv_q = b.op("vector", lambda E: E.tensor_copy(qib[:], pD[:, 0:TPC]), waits=[("st", st1b)], inc="sv")
    b.op("vector", lambda E: E.tensor_copy(ln1b[:, 0:1024], pA[:]))
    svl = b.op("vector", lambda E: E.tensor_copy(ln1b[:, 1024:2048], pB[:]), inc="sv")
    mask_sb = arD[:].bitcast(FP8).rearrange("p (k q) -> p k q", q=128)
    for k in range(NKT):
        j = k // 8
        b.op("vector", lambda E, k=k, j=j: E.tensor_scalar(
            mask_sb[:, k, :], qib[:, j * 128:(j + 1) * 128], kidx_sb[:, k:k + 1], None, ALU.is_ge),
            waits=[("sv", sv_q)] if k == 0 else ())
    svm = b.op("vector", None, inc="sv")
    for i in range(NT):
        b.op("vector", lambda E, i=i: E.tensor_copy(a2_bf[:, i, :], aa_sb[:, i, :]))
    b.op("vector", None, inc="sv")
    bstage_f = arBf[0:16, 0:1024]
    bstage_b = arB[:][0:16, 2048:3072]
    for q in range(8):
        dq = b.dma("sync", bstage_f, bbi[:, q * 1024:(q + 1) * 1024],
                   waits=[("sv", b.n("sv"))], sem="db")
        svb = b.op("vector", lambda E: E.tensor_copy(bstage_b, bstage_f),
                   waits=[("db", dq), ("dbo", b.n("dbo"))], inc="sv")
        b.dma("scalar", bbd[:, q * 1024:(q + 1) * 1024], bstage_b, waits=[("sv", svb)], sem="dbo")
    woa_f = arCf[:, NT * 16:2 * NT * 16].rearrange("p (c r) -> p c r", r=16)
    wsl = arBf[:, 2048:6144].rearrange("p (u d) -> p u d", u=2)
    pwoa = pC[:, 0:256].rearrange("p (c r) -> p c r", r=16)
    for i in range(NT):
        dli = b.dma("sync", wsl[:, i % 2, :], wo2[128 * i:128 * (i + 1), :],
                    waits=[("st", b.n("st") - 1)] if i >= 2 else [("sv", svl)],
                    sem="dw%d" % (i % 2))
        w0 = [("dw%d" % (i % 2), dli), ("sv", svm)]
        for j in range(NT):
            b.op("tensor", lambda E, i=i, j=j: E.matmul(
                pwoa[:, j, :], wsl[:, i % 2, 128 * j:128 * (j + 1)], aa_sb[:, i, :],
                start=(i == 0 and j == 0), stop=(i == NT - 1 and j == NT - 1)),
                waits=w0 if j == 0 else ())
        b.op("tensor", None, inc="st")
    stw = b.n("st")
    svw = b.op("vector", lambda E: E.tensor_copy(woa_f[:], pwoa[:]), waits=[("st", stw)], inc="sv")
    wvstage = arC[:][:, 7168:11264].rearrange("p (c x) -> p c x", x=256)
    for i in range(NT):
        dli = b.dma("sync", wsl[:, i % 2, :], wv2[128 * i:128 * (i + 1), :],
                    waits=[("st", b.n("st") - 1)] if i >= 2 else [("st", b.n("st"))],
                    sem="dw%d" % (i % 2))
        w0 = [("dw%d" % (i % 2), dli), ("sv", svw), ("sv", b.n("sv"))]
        for j in range(NT):
            b.op("tensor", lambda E, i=i, j=j: E.matmul(
                pwoa[:, j, :], wsl[:, i % 2, 128 * j:128 * (j + 1)], woa_f[:, i, :],
                start=True, stop=True),
                waits=w0 if j == 0 else ())
        stt = b.op("tensor", None, inc="st")
        b.op("vector", lambda E, i=i: E.tensor_copy(wvstage[:, :, 16 * i:16 * (i + 1)], pwoa[:]),
             waits=[("st", stt)], inc="sv")
    svw2 = b.op("vector", None, inc="sv")
    b.dma("scalar", wvwoad.rearrange("(c p) x -> p c x", p=128), wvstage[:],
          waits=[("sv", svw2)], sem="dwv")

    cstage = arC[:][:, 2048:6144].rearrange("p (u d) -> p u d", u=2)
    it = 0
    for i in range(NT):
        for part, (wsrc, wdst) in enumerate([
                (wqkv[128 * i:128 * (i + 1), 0:DM], q1w[128 * i:128 * (i + 1), :]),
                (wqkv[128 * i:128 * (i + 1), DM:2 * DM], k1w[128 * i:128 * (i + 1), :]),
                (wqkv[128 * i:128 * (i + 1), 2 * DM:3 * DM], v1w[128 * i:128 * (i + 1), :])]):
            u = it % 2
            dli = b.dma("sync", wsl[:, u, :], wsrc,
                        waits=[("sv", b.n("sv") - 1)] if it >= 2 else [("sv", svw2)],
                        sem="dw%d" % u)
            svc = b.op("vector", lambda E, u=u: E.tensor_copy(cstage[:, u, :], wsl[:, u, :]),
                       waits=[("dw%d" % u, dli), ("dc%d" % u, b.n("dc%d" % u))],
                       inc="sv")
            b.dma("scalar", wdst, cstage[:, u, :], waits=[("sv", svc)], sem="dc%d" % u)
            it += 1
    tstage = arC[:][:, 6144:7168].rearrange("p (u d) -> p u d", u=2)
    for wsrc, wdst in [(wq2, wq2t), (wk2, wk2t)]:
        for i in range(NT):
            u = i % 2
            dli = b.dma("sync", wsl[:, u, :], wsrc[128 * i:128 * (i + 1), :],
                        waits=[("st", b.n("st") - 1)] if i >= 2 else [("sv", b.n("sv"))],
                        sem="dw%d" % u)
            svc = b.op("vector", lambda E, u=u: E.tensor_copy(cstage[:, u, :], wsl[:, u, :]),
                       waits=[("dw%d" % u, dli), ("dc%d" % u, b.n("dc%d" % u))], inc="sv")
            for g in range(4):
                for jj in range(4):
                    j = 4 * g + jj
                    ws = []
                    if jj == 0:
                        ws = [("sv", max(svc, b.n("sv")))]
                    b.op("tensor", lambda E, u=u, j=j, jj=jj, g=g: E.transpose(
                        pT2[:, (g % 2) * 1024 + jj * 128:(g % 2) * 1024 + (jj + 1) * 128],
                        cstage[:, u, 128 * j:128 * (j + 1)], id_bf[:]),
                        waits=ws)
                stt = b.op("tensor", None, inc="st")
                uu = g % 2
                sve = b.op("vector", lambda E, uu=uu: E.tensor_copy(
                    tstage[:, uu, :], pT2[:, uu * 1024:uu * 1024 + 512]),
                    waits=[("st", stt), ("dt%d" % uu, b.n("dt%d" % uu))], inc="sv")
                b.dma("scalar",
                      wdst[512 * g:512 * (g + 1), 128 * i:128 * (i + 1)].rearrange(
                          "(jj p) o -> p jj o", p=128),
                      tstage[:, uu, :].rearrange("p (jj o) -> p jj o", o=128),
                      waits=[("sv", sve)], sem="dt%d" % uu)
    def early_out():
        if BUILD_PHASES in ("p0", "p1a1", "p1a"):
            for tt in range(NSLOT):
                b.op("vector", lambda E, tt=tt: E.memset(accT[tt][:], 0.0))
            b.op("vector", None, inc="sv")
        svq = b.n("sv")
        for tt in range(NSLOT):
            b.dma("sync", out[128 * tt:128 * (tt + 1), :], accT[tt][:],
                  waits=[("sv", svq)] if tt == 0 else (), sem="so")
        b.op("sync", None, waits=[("so", b.n("so"))])
        return _finish(nc, b)

    b.barrier()
    if BUILD_PHASES == "p0":
        return early_out()

    xnT = arA[:].rearrange("p (c t) -> p c t", t=1024)
    x2Tv = hT[:]
    xdb = arBf[:, 0:4096].rearrange("p (u d) -> p u d", u=2)
    xsc = arBf[:, 4096:6144]
    xsq = arB[:][:, 8192:10240]
    xnbf = arE[:][:, 0:2048]
    x2bf = arC[:][:, 13312:15360]
    kwsl = arC[:][:, 0:4096].rearrange("p (u c o) -> p u c o", u=2, c=NT)
    vwsl = arC[:][:, 4096:12288].rearrange("p (u c o) -> p u c o", u=2, c=NT)
    estg = arC[:][:, 12288:13312].rearrange("p (u d) -> p u d", u=2)

    def stat_chain(tt, sv_sum):
        s = lambda i: stats[:, tt, i:i + 1]
        steps = [
            lambda E: E.activation(s(2), s(0), AF.Copy, scale=1.0 / DM),
            lambda E: E.activation(s(3), s(2), AF.Square),
            lambda E: E.activation(s(4), s(1), AF.Copy, scale=1.0 / DM),
            lambda E: E.activation(s(5), s(3), AF.Relu, scale=-1.0, bias=s(4)),
            lambda E: E.activation(s(6), s(5), AF.Ln, bias=cst[:, 0:1]),
            lambda E: E.activation(s(7), s(6), AF.Exp, scale=-0.5),
            lambda E: E.activation(s(8), s(6), AF.Exp, scale=-1.0),
            lambda E: E.activation(s(9), s(8), AF.Relu, scale=-EPS_LN, bias=cst[:, 2:3]),
            lambda E: E.activation(s(10), s(9), AF.Ln, bias=cst[:, 1:2]),
            lambda E: E.activation(s(11), s(6), AF.Copy, scale=-0.5),
            lambda E: E.activation(s(12), s(10), AF.Exp, scale=-0.5, bias=s(11)),
        ]
        last = 0
        for i, fn in enumerate(steps):
            w = [("sv", sv_sum)] if i == 0 else [("sa", last)]
            last = b.op("scalar", fn, waits=w, inc="sa")
        return last

    def emit_pass(src, own):
        ptev = {0: b.n("sv"), 1: b.n("sv")}
        xev = {0: (b.n("sv"), b.n("sa")), 1: (b.n("sv"), b.n("sa"))}
        for tt in range(8):
            u = tt % 2
            dl_x = b.dma(QU[u], xdb[:, u, :], src[128 * tt:128 * (tt + 1), :],
                         waits=[("sv", xev[u][0]), ("sa", xev[u][1])], sem="dx%d" % u)
            sv_sum = b.op("vector", lambda E, u=u, tt=tt: E.tensor_reduce(
                stats[:, tt, 0:1], xdb[:, u, :], mybir.AxisListType.X, ALU.add),
                waits=[("dx%d" % u, dl_x)], inc="sv")
            b.op("scalar", lambda E, u=u, tt=tt: E.activation(
                xsq[:], xdb[:, u, :], AF.Square, accum_out=stats[:, tt, 1:2]),
                waits=[("dx%d" % u, dl_x), ("sv", b.n("sv"))])
            sa_c = stat_chain(tt, sv_sum)
            s = lambda i: stats[:, tt, i:i + 1]
            sv_xn = b.op("vector", lambda E, u=u, m=s(2), r=s(7): E.tensor_scalar(
                xsc, xdb[:, u, :], m, r, ALU.subtract, ALU.mult),
                waits=[("sa", sa_c)], inc="sv")
            sv_xnb = b.op("vector", lambda E: E.tensor_copy(xnbf, xsc),
                          waits=[("sv", sv_xn)], inc="sv")
            if own:
                b.op("vector", lambda E, u=u, tt=tt: E.tensor_tensor(
                    accT[tt][:], xdb[:, u, :], xsc, ALU.add), waits=[("sv", sv_xnb)], inc="sv")
            sv_x2 = b.op("vector", lambda E, u=u, m=s(2), r=s(12): E.tensor_scalar(
                xsc, xdb[:, u, :], m, r, ALU.subtract, ALU.mult),
                waits=[("sa", sa_c), ("sv", b.n("sv"))], inc="sv")
            sv_x2b = b.op("vector", lambda E: E.tensor_tensor(x2bf, xsc, ln1b[:], ALU.mult),
                          waits=[("sv", sv_x2)], inc="sv")
            xev[u] = (sv_x2, sa_c)
            for (srcbf, dstT, svr) in ((xnbf, xnT, sv_xnb), (x2bf, x2Tv, sv_x2b)):
                for g in range(4):
                    for jj in range(4):
                        j = 4 * g + jj
                        ws = []
                        if jj == 0:
                            ws = [("sv", max(svr, ptev[g % 2]))]
                        b.op("tensor", lambda E, j=j, jj=jj, g=g, srcbf=srcbf: E.transpose(
                            pT2[:, (g % 2) * 1024 + jj * 128:(g % 2) * 1024 + jj * 128 + 128],
                            srcbf[:, 128 * j:128 * (j + 1)], id_bf[:]),
                            waits=ws)
                    stt = b.op("tensor", None, inc="st")
                    ptev[g % 2] = b.op("vector", lambda E, g=g, tt=tt, dstT=dstT: E.tensor_copy(
                        dstT[:, 4 * g:4 * g + 4, 128 * tt:128 * (tt + 1)],
                        pT2[:, (g % 2) * 1024:(g % 2) * 1024 + 512].rearrange("p (c t) -> p c t", t=128)),
                        waits=[("st", stt)], inc="sv")
        return b.n("sv")

    def proj_fm(wsrc_bf, dst, rhsT, sv_ready, col0):
        ev = {0: b.n("sv"), 1: b.n("sv")}
        for oc in range(NT):
            u = oc % 2
            dli = b.dma(QU[u], kwsl[:, u, :, :],
                        wsrc_bf[:, 128 * oc:128 * (oc + 1)].rearrange("(c p) o -> p c o", p=128),
                        waits=[("st", b.n("st") - 1)] if oc >= 2 else [("st", b.n("st"))],
                        sem="dk%d" % u)
            for tch in range(2):
                pu = tch % 2
                pdst = (pA if pu == 0 else pB)[:, 0:512]
                w0 = [("dk%d" % u, dli), ("sv", sv_ready), ("sv", ev[pu])]
                for c in range(NT):
                    b.op("tensor", lambda E, u=u, c=c, tch=tch, pdst=pdst, rhsT=rhsT: E.matmul(
                        pdst, kwsl[:, u, c, :], rhsT[:, c, 512 * tch:512 * (tch + 1)],
                        start=(c == 0), stop=(c == NT - 1)),
                        waits=w0 if c == 0 else ())
                stt = b.op("tensor", None, inc="st")
                eu = (2 * oc + tch) % 2
                sve = b.op("vector", lambda E, eu=eu, pdst=pdst: E.tensor_copy(
                    estg[:, eu, :], pdst),
                    waits=[("st", stt), ("de%d" % eu, b.n("de%d" % eu))], inc="sv")
                ev[pu] = sve
                b.dma(QO[eu], dst[oc][:, col0 + 512 * tch:col0 + 512 * (tch + 1)],
                      estg[:, eu, :], waits=[("sv", sve)], sem="de%d" % eu)

    K1l = [K1d[h] for h in range(NH)]
    K2l = [K2d[h] for h in range(NH)]
    for p in range(NPASS):
        sv_ready = emit_pass(xf[PTOK * p:PTOK * (p + 1), :], own=False)
        proj_fm(k1w, K1l, xnT, sv_ready, PTOK * p)
        ev = {0: b.n("sv"), 1: b.n("sv")}
        for avs in range(8):
            u = avs % 2
            dli = b.dma(QU[u], vwsl[:, u, :, :],
                        v1w[:, 256 * avs:256 * (avs + 1)].rearrange("(c p) o -> p c o", p=128),
                        waits=[("st", b.n("st") - 1)] if avs >= 2 else [("st", b.n("st"))],
                        sem="dv%d" % u)
            for tt in range(8):
                pu = tt % 2
                pdst = (pA if pu == 0 else pB)[:, 512:768]
                w0 = [("dv%d" % u, dli), ("sv", sv_ready), ("sv", ev[pu])]
                for c in range(NT):
                    b.op("tensor", lambda E, u=u, c=c, tt=tt, pdst=pdst: E.matmul(
                        pdst, xnT[:, c, 128 * tt:128 * (tt + 1)], vwsl[:, u, c, :],
                        start=(c == 0), stop=(c == NT - 1)),
                        waits=w0 if c == 0 else ())
                stt = b.op("tensor", None, inc="st")
                eu = (8 * avs + tt) % 2
                sve = b.op("vector", lambda E, eu=eu, pdst=pdst: E.tensor_copy(
                    estg[:, eu, 0:256], pdst),
                    waits=[("st", stt), ("de%d" % eu, b.n("de%d" % eu))], inc="sv")
                ev[pu] = sve
                h0 = avs * 2
                b.dma(QO[eu],
                      V1d[h0:h0 + 2, PTOK * p + 128 * tt:PTOK * p + 128 * (tt + 1), :].rearrange(
                          "h t d -> t h d"),
                      estg[:, eu, 0:256].rearrange("p (h d) -> p h d", h=2),
                      waits=[("sv", sve)], sem="de%d" % eu)
        proj_fm(wk2t, K2l, x2Tv, sv_ready, PTOK * p)
        dwv = b.dma("sync", kwsl[:, 0, :, :].rearrange("p c o -> p (c o)").rearrange(
            "p (c x) -> p c x", x=256) if False else kwsl[:].rearrange("p u c o -> p (u c o)")[:, 0:NT * 256].rearrange("p (c x) -> p c x", x=256),
            wvwoad.rearrange("(c p) x -> p c x", p=128),
            waits=[("st", b.n("st")), ("dwv", b.n("dwv"))], sem="dwv")
        wvsl = kwsl[:].rearrange("p u c o -> p (u c o)")[:, 0:NT * 256].rearrange("p (c x) -> p c x", x=256)
        pv2h = {0: pC[:, 0:256], 1: pC[:, 512:768]}
        ev = {0: b.n("sv"), 1: b.n("sv")}
        for tt in range(8):
            pu = tt % 2
            pv2 = pv2h[pu]
            w0 = [("sv", sv_ready), ("sv", ev[pu]), ("st", b.n("st")), ("dwv", dwv)] if tt <= 1 else [("sv", ev[pu])]
            for c in range(NT):
                b.op("tensor", lambda E, c=c, tt=tt, pv2=pv2: E.matmul(
                    pv2, x2Tv[:, c, 128 * tt:128 * (tt + 1)], wvsl[:, c, :],
                    start=(c == 0), stop=(c == NT - 1)),
                    waits=w0 if c == 0 else ())
            stt = b.op("tensor", None, inc="st")
            ev[pu] = b.op("vector", lambda E, tt=tt, pv2=pv2: E.tensor_copy(v2stage[:, tt, :], pv2),
                          waits=[("st", stt), ("dvo", b.n("dvo"))], inc="sv")
        svv = b.n("sv")
        for h in range(NH):
            b.dma("scalar", V2d[h, :, 8 * p:8 * (p + 1), :], v2stage[:, :, 16 * h:16 * (h + 1)],
                  waits=[("sv", svv)] if h == 0 else (), sem="dvo")
        if BUILD_PHASES == "p1a1":
            b.barrier()
            return early_out()

    b.barrier()
    if BUILD_PHASES == "p1a":
        return early_out()

    sv_ready = emit_pass(xo, own=True)
    Q1l = [Q1d[h] for h in range(NH)]
    Q2l = [Q2d[h] for h in range(NH)]
    proj_fm(q1w, Q1l, xnT, sv_ready, 0)
    proj_fm(wq2t, Q2l, x2Tv, sv_ready, 0)
    hTv = hT[:]
    st_q2 = b.n("st")
    ln2bv = x2bf
    l2_row = arAf[0:1, 0:DM]
    dl2 = b.dma("sync", l2_row, ln2r, waits=[("st", st_q2)], sem="di2")
    for _h in range(2):
        b.op("tensor", lambda E, _h=_h: E.matmul(pA[:, 512 * _h:512 * (_h + 1)], ones_r[:],
                                                 l2_row[0:1, 512 * _h:512 * (_h + 1)], start=True, stop=True),
             waits=[("di2", dl2), ("sv", b.n("sv"))] if _h == 0 else ())
    for _h in range(2):
        b.op("tensor", lambda E, _h=_h: E.matmul(pB[:, 512 * _h:512 * (_h + 1)], ones_r[:],
                                                 l2_row[0:1, 1024 + 512 * _h:1024 + 512 * (_h + 1)], start=True, stop=True))
    stl2b = b.op("tensor", None, inc="st")
    b.op("vector", lambda E: E.tensor_copy(ln2bv[:, 0:1024], pA[:]), waits=[("st", stl2b)])
    svl2 = b.op("vector", lambda E: E.tensor_copy(ln2bv[:, 1024:2048], pB[:]), inc="sv")
    for tt in range(8):
        u = tt % 2
        dl_x = b.dma(QU[u], xdb[:, u, :], xo[128 * tt:128 * (tt + 1), :],
                     waits=[("sv", b.n("sv")), ("sa", b.n("sa"))], sem="dx%d" % u)
        s = lambda i: stats[:, tt, i:i + 1]
        sv_x2 = b.op("vector", lambda E, u=u, m=s(2), r=s(12): E.tensor_scalar(
            xsc, xdb[:, u, :], m, r, ALU.subtract, ALU.mult),
            waits=[("dx%d" % u, dl_x), ("sv", b.n("sv"))], inc="sv")
        sv_hb = b.op("vector", lambda E: E.tensor_tensor(xnbf, xsc, ln2bv, ALU.mult),
                     waits=[("sv", sv_x2), ("sv", svl2)], inc="sv")
        for g in range(4):
            for jj in range(4):
                j = 4 * g + jj
                ws = []
                if jj == 0:
                    ws = [("sv", max(sv_hb, b.n("sv"))), ("st", st_q2)]
                b.op("tensor", lambda E, j=j, jj=jj, g=g: E.transpose(
                    pT2[:, (g % 2) * 1024 + jj * 128:(g % 2) * 1024 + jj * 128 + 128],
                    xnbf[:, 128 * j:128 * (j + 1)], id_bf[:]),
                    waits=ws)
            stt = b.op("tensor", None, inc="st")
            b.op("vector", lambda E, g=g, tt=tt: E.tensor_copy(
                hTv[:, 4 * g:4 * g + 4, 128 * tt:128 * (tt + 1)],
                pT2[:, (g % 2) * 1024:(g % 2) * 1024 + 512].rearrange("p (c t) -> p c t", t=128)),
                waits=[("st", stt)], inc="sv")
    sv_h2 = b.n("sv")
    ph = pC[0:16, 0:1024]
    for tch in range(2):
        w0 = [("sv", sv_h2), ("st", b.n("st"))]
        for c in range(NT):
            b.op("tensor", lambda E, c=c, tch=tch: E.matmul(
                ph[:, 512 * tch:512 * (tch + 1)], a2_bf[:, c, :],
                hTv[:, c, 512 * tch:512 * (tch + 1)],
                start=(c == 0), stop=(c == NT - 1)),
                waits=w0 if c == 0 else ())
    stt = b.op("tensor", None, inc="st")
    b.op("vector", lambda E: E.tensor_copy(hAr, ph), waits=[("st", stt)], inc="sv")

    smp = arC[:][:, 0:512]
    ssq = arC[:][:, 512:1024]
    pr = pD[0:1, 0:512]
    for idx, srcl in enumerate([K1l, K2l, Q1l, Q2l]):
        for h in range(NH):
            dls = b.dma("sync", smp, srcl[h][:, 0:512],
                        waits=[("sv", b.n("sv")), ("sa", b.n("sa")), ("st", b.n("st")),
                               ("de0", b.n("de0")), ("de1", b.n("de1"))], sem="dsm")
            sas = b.op("scalar", lambda E: E.activation(ssq, smp, AF.Square),
                       waits=[("dsm", dls), ("st", b.n("st"))], inc="sa")
            stt = b.op("tensor", lambda E: E.matmul(pr, ones_c[:], ssq, start=True, stop=True),
                       waits=[("sa", sas), ("sv", b.n("sv"))], inc="st")
            b.op("vector", lambda E, idx=idx, h=h: E.tensor_reduce(
                nrm4[0:1, idx * 16 + h:idx * 16 + h + 1], pr, mybir.AxisListType.X, ALU.max),
                waits=[("st", stt)], inc="sv")
    svn = b.n("sv")
    t1 = arEf[0:1, 0:32]
    b.op("gpsimd", lambda E: E.tensor_tensor(t1[0:1, 0:16], nrm4[0:1, 32:48], nrm4[0:1, 0:16], ALU.mult),
         waits=[("sv", svn)])
    sg2 = b.op("gpsimd", lambda E: E.tensor_tensor(t1[0:1, 16:32], nrm4[0:1, 48:64], nrm4[0:1, 16:32], ALU.mult),
               inc="sg")
    sa1 = b.op("scalar", lambda E: E.activation(mrow[0:1, :], t1, AF.Ln), waits=[("sg", sg2)], inc="sa")
    sa2 = b.op("scalar", lambda E: E.activation(mrow[0:1, :], mrow[0:1, :], AF.Exp, scale=0.5),
               waits=[("sa", sa1)], inc="sa")
    sa3 = b.op("scalar", lambda E: E.mul(mrow[0:1, :], mrow[0:1, :], -1.25 * C), waits=[("sa", sa2)], inc="sa")
    sv_cl = b.op("vector", lambda E: E.tensor_scalar_max(mrow[0:1, :], mrow[0:1, :], -30.0),
                 waits=[("sa", sa3)], inc="sv")
    stt = b.op("tensor", lambda E: E.matmul(pD[:, 0:32], ones_r[:], mrow[0:1, :], start=True, stop=True),
               waits=[("sv", sv_cl)], inc="st")
    b.op("vector", lambda E: E.tensor_copy(mhat[:], pD[:, 0:32]), waits=[("st", stt)], inc="sv")

    if DEBUG_OUTS:
        sv_all = b.n("sv")
        b.dma("scalar", dbg["d_V2p"], V2d, waits=[("sv", sv_all)], sem="so")
        b.dma("scalar", dbg["d_k1w"], k1w, sem="so")
        b.dma("scalar", dbg["d_wq2t"], wq2t, sem="so")
        b.dma("sync", dbg["d_mhat"], mhat[:], waits=[("sv", sv_all)], sem="so")
        b.dma("sync", dbg["d_hA"], hAr[:], waits=[("sv", sv_all)], sem="so")
        for tt in range(NSLOT):
            b.dma("sync", dbg["d_acc0"][128 * tt:128 * (tt + 1), :], accT[tt][:],
                  waits=[("sv", sv_all)], sem="so")
        for h in range(NH):
            b.dma("scalar", dbg["d_K1"][h], K1d[h], sem="so")
            b.dma("scalar", dbg["d_V1"][h], V1d[h], sem="so")
            b.dma("scalar", dbg["d_K2"][h], K2d[h], sem="so")
            b.dma("scalar", dbg["d_Q1"][h], Q1d[h], sem="so")
            b.dma("scalar", dbg["d_Q2"][h], Q2d[h], sem="so")

    b.barrier()

    if BUILD_PHASES == "p1":
        return early_out()

    ksb1 = arB[:][:, 0:8192]
    vsb = arC[:][:, 0:8192].rearrange("p (n d) -> p n d", d=128)
    qsb = arC[:][:, 8192:10240].rearrange("p (u t) -> p u t", u=2)
    ptb = arC[:][:, 10240:12288].rearrange("p (u t) -> p u t", u=2)
    rcb = arCf[:, 6144:7168]
    sumr = arEf[0:1, 0:1024]
    lnr = arEf[32:33, 0:1024]

    def attention(attn_i, Kl, Ql, av_out_fn, v_load_half, v_lhsT_fn, av_rows):
        two_mm = (av_rows == 128)
        head_s_fin = {}
        st0 = b.n("st")
        dkA = {0: b.dma("sync", ksb1[:, 0:4096], Kl[0][:, 0:4096], waits=[("st", st0)], sem="dKa")}
        dkB = {0: b.dma("sync", ksb1[:, 4096:8192], Kl[0][:, 4096:8192], sem="dKb")}
        dvA = {0: v_load_half(0, 0, st0)}
        dvB = {0: v_load_half(0, 1, st0)}
        for h in range(NH):
            u = h % 2
            wq_ld = [("st", head_s_fin[h - 2])] if h >= 2 else [("st", st0)]
            dq = b.dma("sync", qsb[:, u, :], Ql[h], waits=wq_ld, sem="dq%d" % u)
            bias = mhat[:, attn_i * 16 + h:attn_i * 16 + h + 1]
            sa_exp = {}
            st_av = {}
            prev_done_sv = b.n("sv")
            sa_head0 = b.n("sa")
            def emit_av(k):
                jmin = k // 8
                qoff = 128 * jmin
                pu = k % 2
                sg_m = sg_mask[k]
                pts = [qoff, 512, TPC] if qoff < 512 else [qoff, TPC]
                avch = [(a0, a1) for a0, a1 in zip(pts[:-1], pts[1:]) if a1 > a0]
                wa0 = [("sg", sg_m)]
                if k == 0:
                    wa0.append(("sv", prev_done_sv))
                    wa0.append(("dVa", dvA[h]))
                if k == 32:
                    wa0.append(("dVb", dvB[h]))
                for ci, (a0, a1) in enumerate(avch):
                    stopf = (a0 == qoff and k in (31, 63))
                    startf = (k == 0 and a0 in (0, 512))
                    wa = wa0 if ci == 0 else [("sa", sa_exp[k])]
                    if two_mm:
                        b.op("tensor", lambda E, k=k, a0=a0, a1=a1, pu=pu, qoff=qoff, stopf=stopf, startf=startf: E.matmul(
                            pC[0:av_rows, a0:a1], v_lhsT_fn(k), ptb[:, pu, a0 - qoff:a1 - qoff],
                            start=startf, stop=stopf),
                            waits=wa)
                        b.op("tensor", lambda E, k=k, a0=a0, a1=a1, pu=pu, qoff=qoff, stopf=stopf, startf=startf: E.matmul(
                            pD[0:1, a0:a1], ones_c[:], ptb[:, pu, a0 - qoff:a1 - qoff],
                            start=startf, stop=stopf))
                    else:
                        sums_fn = lambda E, k=k, a0=a0, a1=a1, pu=pu, qoff=qoff, stopf=stopf, startf=startf: E.matmul(
                            pC[0:33, a0:a1], ones32[:], ptb[:, pu, a0 - qoff:a1 - qoff],
                            start=startf, stop=stopf)
                        av_fn = lambda E, k=k, a0=a0, a1=a1, pu=pu, qoff=qoff: E.matmul(
                            pC[0:av_rows, a0:a1], v_lhsT_fn(k), ptb[:, pu, a0 - qoff:a1 - qoff],
                            start=False, stop=False)
                        if k == 0:
                            b.op("tensor", sums_fn, waits=wa)
                            b.op("tensor", av_fn)
                        else:
                            b.op("tensor", av_fn, waits=wa)
                            b.op("tensor", sums_fn)
                st_av[k] = b.op("tensor", None, inc="st")

            sg_mask = {}
            for k in range(NKT):
                jmin = k // 8
                qoff = 128 * jmin
                nq = TPC - qoff
                pu = k % 2
                ps_k = (pA if pu == 0 else pB)
                w0 = [("dq%d" % u, dq)]
                if k == 0:
                    w0.append(("dKa", dkA[h]))
                elif k == 32:
                    w0.append(("dKb", dkB[h]))
                if k >= 2:
                    w0.append(("sa", sa_exp[k - 2]))
                else:
                    w0.append(("sa", sa_head0))
                    w0.append(("sv", prev_done_sv))
                nch = (nq + 511) // 512
                for ch in range(nch):
                    c0 = 512 * ch
                    c1 = min(nq, c0 + 512)
                    b.op("tensor", lambda E, u=u, k=k, c0=c0, c1=c1, ps_k=ps_k, qoff=qoff: E.matmul(
                        ps_k[:, c0:c1], ksb1[:, 128 * k:128 * (k + 1)],
                        qsb[:, u, qoff + c0:qoff + c1], start=True, stop=True),
                        waits=w0 if ch == 0 else ())
                st_s = b.op("tensor", None, inc="st")
                if k == 31:
                    st_s31 = st_s
                if k == NKT - 1:
                    head_s_fin[h] = st_s
                we = [("st", st_s)]
                if k >= 2:
                    we.append(("st", st_av[k - 2]))
                sa_exp[k] = b.op("scalar", lambda E, pu=pu, nq=nq, ps_k=ps_k, bias=bias: E.activation(
                    ptb[:, pu, 0:nq], ps_k[:, 0:nq], AF.Exp, bias=bias, scale=C),
                    waits=we, inc="sa")
                sg_mask[k] = b.op("gpsimd", lambda E, pu=pu, k=k: E.tensor_tensor(
                    ptb[:, pu, 0:128], ptb[:, pu, 0:128], mask_sb[:, k, :], ALU.mult),
                    waits=[("sa", sa_exp[k])], inc="sg")
                if k >= 1:
                    emit_av(k - 1)
                if k == 32 and h + 1 < NH:
                    dkA[h + 1] = b.dma("sync", ksb1[:, 0:4096], Kl[h + 1][:, 0:4096],
                                       waits=[("st", st_s31)], sem="dKa")
                    dvA[h + 1] = v_load_half(h + 1, 0, st_av[31])
            emit_av(NKT - 1)
            if h + 1 < NH:
                dkB[h + 1] = b.dma("sync", ksb1[:, 4096:8192], Kl[h + 1][:, 4096:8192],
                                   waits=[("st", head_s_fin[h])], sem="dKb")
                dvB[h + 1] = v_load_half(h + 1, 1, st_av[NKT - 1])
            sum_src = pD[0:1, 0:1024] if two_mm else pC[32:33, 0:1024]
            sv_s = b.op("vector", lambda E, sum_src=sum_src: E.tensor_copy(sumr, sum_src),
                        waits=[("st", st_av[NKT - 1])], inc="sv")
            sa_l = b.op("scalar", lambda E: E.activation(lnr, sumr, AF.Ln),
                        waits=[("sv", sv_s)], inc="sa")
            sa_r = b.op("scalar", lambda E: E.activation(sumr, lnr, AF.Exp, scale=-1.0),
                        waits=[("sa", sa_l)], inc="sa")
            b.op("tensor", lambda E: E.matmul(pA[:, 0:512], ones_r[:], sumr[0:1, 0:512],
                                              start=True, stop=True),
                 waits=[("sa", sa_r)])
            st_b = b.op("tensor", lambda E: E.matmul(pA[:, 512:1024], ones_r[:], sumr[0:1, 512:1024],
                                                     start=True, stop=True), inc="st")
            sv_rb = b.op("vector", lambda E: E.tensor_copy(rcb, pA[:, 0:1024]),
                         waits=[("st", st_b)], inc="sv")
            av_out_fn(h, sv_rb)

    av1T = arA[:].rearrange("p (c t) -> p c t", t=1024)

    def v1_load(h, half, stw):
        return b.dma("scalar", vsb[:, 32 * half:32 * (half + 1), :],
                     V1d[h, 4096 * half:4096 * (half + 1), :].rearrange("(n p) d -> p n d", p=128),
                     waits=[("st", stw)], sem="dV%s" % "ab"[half])

    def av1_out(h, sv_rb):
        b.op("vector", lambda E, h=h: E.tensor_tensor(
            av1T[:, h, :], pC[:, 0:1024], rcb, ALU.mult),
            waits=[("sv", sv_rb)], inc="sv")

    attention(0, K1l, Q1l, av1_out, v1_load, lambda k: vsb[:, k, :], 128)

    def av2_out(h, sv_rb):
        svt = b.op("vector", lambda E: E.tensor_tensor(
            msc, pC[0:16, 0:1024], rcb[0:16, :], ALU.mult),
            waits=[("sv", sv_rb), ("st", b.n("st"))], inc="sv")
        b.op("tensor", lambda E, h=h: E.matmul(
            pD[0:16, 0:512], idf16[:], msc[:, 0:512], start=(h == 0), stop=(h == NH - 1)),
            waits=[("sv", svt)])
        b.op("tensor", lambda E, h=h: E.matmul(
            pD[0:16, 512:1024], idf16[:], msc[:, 512:1024], start=(h == 0), stop=(h == NH - 1)),
            inc="st")
        if h == NH - 1:
            b.op("vector", lambda E: E.tensor_copy(modpre, pD[0:16, 0:1024]),
                 waits=[("st", b.n("st"))], inc="sv")

    v2sb = arC[:][:, 14336:14336 + NKT * 16].rearrange("p (n r) -> p n r", r=16)

    def v2_load(h, half, stw):
        return b.dma("scalar", v2sb[:, 32 * half:32 * (half + 1), :],
                     V2d[h, :, 32 * half:32 * (half + 1), :],
                     waits=[("st", stw)], sem="dV%s" % "ab"[half])

    attention(1, K2l, Q2l, av2_out, v2_load, lambda k: v2sb[:, k, :], 16)

    if DEBUG_OUTS:
        sv_all = b.n("sv")
        for h in range(NH):
            b.dma("sync", dbg["d_av1"].rearrange("p h t -> p h t")[:, h, :], av1T[:, h, :],
                  waits=[("sv", sv_all)], sem="so")
        b.dma("sync", dbg["d_mod"], modpre[:], waits=[("sv", sv_all)], sem="so")

    b.barrier()

    wofu = {0: arCf[:, 0:512], 1: arCf[:, 4608:5120]}
    wob = arC[:][:, 1024:1024 + NT * 512].rearrange("p (c o) -> p c o", o=512)
    wev = {0: b.n("sv"), 1: b.n("sv")}
    pev = {0: b.n("sv"), 1: b.n("sv")}
    for ds in range(4):
        stg = b.n("st")
        for i in range(NT):
            wu = i % 2
            dli = b.dma(QU[wu], wofu[wu], wout[128 * i:128 * (i + 1), 512 * ds:512 * (ds + 1)],
                        waits=[("sv", wev[wu])] + ([("st", stg)] if i < 2 else []),
                        sem="dwo%d" % wu)
            wev[wu] = b.op("vector", lambda E, i=i, wu=wu: E.tensor_copy(wob[:, i, :], wofu[wu]),
                           waits=[("dwo%d" % wu, dli)], inc="sv")
        sv_w = b.n("sv")
        for tt in range(NSLOT):
            pu = tt % 2
            pdst = (pA if pu == 0 else pB)[:, 0:512]
            w0 = [("sv", sv_w), ("sv", pev[pu])]
            for i in range(NT):
                b.op("tensor", lambda E, i=i, tt=tt, pdst=pdst: E.matmul(
                    pdst, av1T[:, i, 128 * tt:128 * (tt + 1)], wob[:, i, :],
                    start=(i == 0), stop=(i == NT - 1)),
                    waits=w0 if i == 0 else ())
            stt = b.op("tensor", None, inc="st")
            pev[pu] = b.op("vector", lambda E, tt=tt, ds=ds, pdst=pdst: E.tensor_tensor(
                accT[tt][:, 512 * ds:512 * (ds + 1)], accT[tt][:, 512 * ds:512 * (ds + 1)],
                pdst, ALU.add), waits=[("st", stt)], inc="sv")
    b.barrier()

    wT = arC[:][:, 0:8192].rearrange("p (c f) -> p c f", f=512)
    wdnT = arC[:][:, 8192:8192 + 4 * 2048].rearrange("p (c o) -> p c o", o=2048)
    g_all = arB[:][:, 8192:12288].rearrange("p (t f) -> p t f", f=512)
    wstgu = {0: arBf[:, 0:512], 1: arAf[:, 0:512]}
    wstgbu = {0: arB[:][:, 1024:1536], 1: arA[:][:, 2048:2560]}
    wdnfu = {0: arBf[:, 1024:1536], 1: arAf[:, 512:1024]}
    wdnbu = {0: arB[:][:, 3072:3584], 1: arA[:][:, 2560:3072]}
    bsl = arB[:][0:16, 4096:4608]
    gubf = arB[:][:, 6144:6656]
    guT = arB[:][:, 7168:7680].rearrange("p (q t) -> p q t", q=4)
    hTv5 = hT[:]
    prod_bf = arB[:][0:16, 4608:4608 + TPC]

    sa_t = b.op("scalar", lambda E: E.activation(msc, modpre, AF.Tanh), inc="sa")
    sv_p = b.op("vector", lambda E: E.scalar_tensor_tensor(
        prod_bf[:], hAr, 0.1, msc, ALU.mult, ALU.mult), waits=[("sa", sa_t)], inc="sv")

    ptev5 = {0: b.n("sv"), 1: b.n("sv")}
    wsev = {0: b.n("sv"), 1: b.n("sv")}
    wstv = {0: 0, 1: 0}
    wdev = {0: b.n("sv"), 1: b.n("sv")}
    wdtv = {0: 0, 1: 0}
    upev = {0: b.n("sv"), 1: b.n("sv")}
    mev = {0: b.n("sv"), 1: b.n("sv")}

    def build_row_dc(wsrc, dstT, fs, dc, stguard):
        u = dc % 2
        dli = b.dma("sync", wstgu[u].rearrange("p (q d) -> p q d", q=4),
                    wsrc[512 * fs:512 * (fs + 1), 128 * dc:128 * (dc + 1)].rearrange(
                        "(q p) d -> p q d", p=128),
                    waits=[("sv", wsev[u]), ("st", stguard)], sem="dws%d" % u)
        svc = b.op("vector", lambda E, u=u: E.tensor_copy(wstgbu[u], wstgu[u]),
                   waits=[("dws%d" % u, dli), ("st", wstv[u])], inc="sv")
        wsev[u] = svc
        for q in range(4):
            b.op("tensor", lambda E, q=q, u=u: E.transpose(
                pT2[:, 1024 * u + q * 128:1024 * u + (q + 1) * 128],
                wstgbu[u][:, 128 * q:128 * (q + 1)], id_bf[:]),
                waits=[("sv", max(svc, ptev5[u]))] if q == 0 else ())
        stt = b.op("tensor", None, inc="st")
        wstv[u] = stt
        ptev5[u] = b.op("vector", lambda E, dc=dc, dstT=dstT, u=u: E.tensor_copy(
            dstT[:, dc, :], pT2[:, 1024 * u:1024 * u + 512]),
            waits=[("st", stt)], inc="sv")

    def build_col_dc(wsrc, dstT, fs, do):
        u = do % 2
        dli = b.dma("scalar", wdnfu[u], wsrc[128 * do:128 * (do + 1), 512 * fs:512 * (fs + 1)],
                    waits=[("sv", wdev[u])], sem="dwd%d" % u)
        svc = b.op("vector", lambda E, u=u: E.tensor_copy(wdnbu[u], wdnfu[u]),
                   waits=[("dwd%d" % u, dli), ("st", wdtv[u])], inc="sv")
        wdev[u] = svc
        for fc in range(4):
            b.op("tensor", lambda E, fc=fc, u=u: E.transpose(
                pT2[:, 1024 * u + fc * 128:1024 * u + (fc + 1) * 128],
                wdnbu[u][:, 128 * fc:128 * (fc + 1)], id_bf[:]),
                waits=[("sv", max(svc, ptev5[u]))] if fc == 0 else ())
        stt = b.op("tensor", None, inc="st")
        wdtv[u] = stt
        ptev5[u] = b.op("vector", lambda E, do=do, dstT=dstT, u=u: E.tensor_copy(
            dstT[:, :, 128 * do:128 * (do + 1)],
            pT2[:, 1024 * u:1024 * u + 512].rearrange("p (c o) -> p c o", o=128)),
            waits=[("st", stt)], inc="sv")

    silu_mark = [0] * NSLOT
    for fs in range(NT):
        stg0 = b.n("st")
        gv0 = b.n("sv")
        dlb = b.dma("scalar", bsl, bbd[:, 512 * fs:512 * (fs + 1)],
                    waits=[("st", stg0)], sem="dbl")
        for dc in range(NT):
            build_row_dc(win, wT, fs, dc, stg0)
        sv_w = b.n("sv")
        gaev = {0: b.n("sa"), 1: b.n("sa")}
        for tt in range(NSLOT):
            pu = tt % 2
            pg = (pA if pu == 0 else pB)[:, 0:512]
            w0 = [("sv", sv_w), ("sv", sv_p), ("dbl", dlb), ("sa", gaev[pu])]
            for c in range(NT):
                b.op("tensor", lambda E, c=c, tt=tt, pg=pg: E.matmul(
                    pg, hTv5[:, c, 128 * tt:128 * (tt + 1)], wT[:, c, :],
                    start=(c == 0), stop=False),
                    waits=w0 if c == 0 else ())
            b.op("tensor", lambda E, tt=tt, pg=pg: E.matmul(
                pg, prod_bf[:, 128 * tt:128 * (tt + 1)], bsl,
                start=False, stop=True))
            st_g = b.op("tensor", None, inc="st")
            gaev[pu] = b.op("scalar", lambda E, tt=tt, pg=pg: E.activation(g_all[:, tt, :], pg, AF.Silu),
                            waits=[("st", st_g), ("sv", gv0)], inc="sa")
            silu_mark[tt] = gaev[pu]
            build_col_dc(wdn, wdnT, fs, 2 * tt)
            build_col_dc(wdn, wdnT, fs, 2 * tt + 1)
        stg1 = b.n("st")
        for dc in range(NT):
            build_row_dc(wup, wT, fs, dc, stg1)
        sv_w2 = b.n("sv")
        for tt in range(NSLOT):
            pu = tt % 2
            pg = (pA if pu == 0 else pB)[:, 512:1024]
            w0 = [("sv", sv_w2), ("sv", upev[pu])]
            for c in range(NT):
                b.op("tensor", lambda E, c=c, tt=tt, pg=pg: E.matmul(
                    pg, hTv5[:, c, 128 * tt:128 * (tt + 1)], wT[:, c, :],
                    start=(c == 0), stop=(c == NT - 1)),
                    waits=w0 if c == 0 else ())
            st_u = b.op("tensor", None, inc="st")
            sv_gu = b.op("vector", lambda E, tt=tt, pg=pg: E.tensor_tensor(
                gubf, g_all[:, tt, :], pg, ALU.mult),
                waits=[("sa", silu_mark[tt]), ("st", st_u)], inc="sv")
            upev[pu] = sv_gu
            for q in range(4):
                b.op("tensor", lambda E, q=q, pu=pu: E.transpose(
                    pT2[:, 1024 * pu + q * 128:1024 * pu + (q + 1) * 128],
                    gubf[:, 128 * q:128 * (q + 1)], id_bf[:]),
                    waits=[("sv", max(sv_gu, ptev5[pu]))] if q == 0 else ())
            st_t = b.op("tensor", None, inc="st")
            sv_gt = b.op("vector", lambda E, pu=pu: E.tensor_copy(
                guT[:].rearrange("p q t -> p (q t)"), pT2[:, 1024 * pu:1024 * pu + 512]),
                waits=[("st", st_t)], inc="sv")
            ptev5[pu] = sv_gt
            for ds in range(4):
                pu2 = ds % 2
                pdst = pC[:, 512 * pu2:512 * (pu2 + 1)]
                w1 = [("sv", sv_gt), ("sv", mev[pu2])]
                for fc in range(4):
                    b.op("tensor", lambda E, fc=fc, ds=ds, pdst=pdst: E.matmul(
                        pdst, guT[:, fc, :], wdnT[:, fc, 512 * ds:512 * (ds + 1)],
                        start=(fc == 0), stop=(fc == 3)),
                        waits=w1 if fc == 0 else ())
                stt = b.op("tensor", None, inc="st")
                mev[pu2] = b.op("vector", lambda E, tt=tt, ds=ds, pdst=pdst: E.tensor_tensor(
                    accT[tt][:, 512 * ds:512 * (ds + 1)], accT[tt][:, 512 * ds:512 * (ds + 1)],
                    pdst, ALU.add), waits=[("st", stt)], inc="sv")

    sv_f = b.n("sv")
    for tt in range(NSLOT):
        b.dma("sync", out[128 * tt:128 * (tt + 1), :], accT[tt][:],
              waits=[("sv", sv_f)], sem="so")
    b.op("sync", None, waits=[("so", b.n("so"))])
    return _finish(nc, b)


def _finish(nc, b):
    import contextlib
    es = contextlib.ExitStack()
    sems = {}
    names = set()
    for streams in b.streams.values():
        for fn, waits, inc, inck, is_dma in streams:
            if inc:
                names.add(inc)
            for s, v in waits:
                names.add(s)
    with es:
        for nm in sorted(names):
            sems[nm] = es.enter_context(nc.semaphore(nm))
        block = es.enter_context(nc.Block())
        bodies = b.replay(sems)
        block.sync(bodies["sync"])
        block.vector(bodies["vector"])
        block.scalar(bodies["scalar"])
        block.tensor(bodies["tensor"])
        block.gpsimd(bodies["gpsimd"])
    return nc


_ROWS = {}


def _core_rows(c):
    if c not in _ROWS:
        _ROWS[c] = np.concatenate([np.arange(128 * (8 * j + c), 128 * (8 * j + c) + 128)
                                   for j in range(NSLOT)])
    return _ROWS[c]


def kernel(L0_attn_x, L0_attn_w_qkv, L0_attn_w_out, L0_ffn_ln1_w, L0_ffn_ln2_w,
           L0_ffn_Wq, L0_ffn_Wk, L0_ffn_Wv, L0_ffn_Wo, L0_ffn_A, L0_ffn_B,
           L0_ffn_W_in_base, L0_ffn_W_up_base, L0_ffn_W_down):
    f32 = np.float32
    x = np.ascontiguousarray(np.asarray(L0_attn_x, dtype=f32).reshape(SEQ, DM))
    common = dict(
        xf=x,
        wqkv=np.ascontiguousarray(np.asarray(L0_attn_w_qkv, dtype=f32)),
        wq2=np.ascontiguousarray(np.asarray(L0_ffn_Wq, dtype=f32)),
        wk2=np.ascontiguousarray(np.asarray(L0_ffn_Wk, dtype=f32)),
        wv2=np.ascontiguousarray(np.asarray(L0_ffn_Wv, dtype=f32)),
        wo2=np.ascontiguousarray(np.asarray(L0_ffn_Wo, dtype=f32)),
        aa=np.ascontiguousarray(np.asarray(L0_ffn_A, dtype=f32)),
        bbi=np.ascontiguousarray(np.asarray(L0_ffn_B, dtype=f32)),
        ln1r=np.ascontiguousarray(np.asarray(L0_ffn_ln1_w, dtype=f32).reshape(1, DM)),
        ln2r=np.ascontiguousarray(np.asarray(L0_ffn_ln2_w, dtype=f32).reshape(1, DM)),
        wout=np.ascontiguousarray(np.asarray(L0_attn_w_out, dtype=f32)),
        win=np.ascontiguousarray(np.asarray(L0_ffn_W_in_base, dtype=f32)),
        wup=np.ascontiguousarray(np.asarray(L0_ffn_W_up_base, dtype=f32)),
        wdn=np.ascontiguousarray(np.asarray(L0_ffn_W_down, dtype=f32)),
        ident=np.eye(128, dtype=f32),
        kidx=np.ascontiguousarray(np.arange(SEQ, dtype=f32).reshape(NKT, 128).T),
    )
    in_maps = []
    for c in range(NCORES):
        rows = _core_rows(c)
        m = dict(common)
        m["xo"] = np.ascontiguousarray(x[rows])
        m["qidx"] = np.ascontiguousarray(rows.astype(f32).reshape(1, TPC))
        in_maps.append(m)
    nc = build_nc()
    res = run_bass_kernel_spmd(nc, in_maps, core_ids=list(range(NCORES)))
    outp = np.empty((SEQ, DM), dtype=f32)
    for c in range(NCORES):
        outp[_core_rows(c)] = res.results[c]["out"]
    return outp.reshape(1, SEQ, DM)


# version 07
# speedup vs baseline: 1.6299x; 1.1454x over previous
import numpy as np
import ml_dtypes
import concourse.bass as bass
import concourse.mybir as mybir
from concourse.bass_utils import run_bass_kernel_spmd

F32 = mybir.dt.float32
BF16 = mybir.dt.bfloat16
FP8 = mybir.dt.float8e4
AF = mybir.ActivationFunctionType
ALU = mybir.AluOpType

NCORES = 8
SEQ = 8192
DM = 2048
FF = 8192
NH = 16
NT = 16
NSLOT = 8
TPC = 1024
NKT = 64
NPASS = 4
PTOK = 2048
NPP = 16
SCALE = float(1.0 / np.sqrt(128.0))
EPS_LN = 1e-5
EPS_RMS = 1e-6

ENGINES = ("sync", "vector", "scalar", "tensor", "gpsimd")

BUILD_PHASES = "all"
DEBUG_OUTS = False


class Sched:

    def __init__(self):
        self.streams = {e: [] for e in ENGINES}
        self.cnt = {}
        self.dma_sems = set()

    def n(self, sem):
        return self.cnt.get(sem, 0)

    def op(self, eng, fn=None, waits=(), inc=None, inck=1, is_dma=False):
        if fn is None and inc is not None and not waits:
            st = self.streams[eng]
            if st and st[-1][0] is not None and st[-1][2] == inc:
                return self.cnt.get(inc, 0)
        if inc is not None:
            self.cnt[inc] = self.cnt.get(inc, 0) + inck
        ws = tuple((s, v) for (s, v) in waits if v > 0)
        self.streams[eng].append((fn, ws, inc, inck, is_dma))
        return self.cnt.get(inc, 0) if inc else 0

    def dma(self, eng, out, in_, waits=(), sem=None, slow=False):
        sem = sem or ("dl" if eng == "sync" else "dg")
        self.dma_sems.add(sem)
        if slow:
            fn = lambda E, o=out, i=in_: E.dma_start(out=o, in_=i,
                                                     allow_slow_non_contiguous=True)
        else:
            fn = lambda E, o=out, i=in_: E.dma_start(out=o, in_=i)
        return self.op(eng, fn, waits=waits, inc=sem, inck=16, is_dma=True)

    def barrier(self):
        marks = []
        for e in ENGINES:
            st = self.streams[e]
            if not st:
                continue
            fn, ws, inc, inck, isd = st[-1]
            if inc is not None and not isd:
                marks.append((inc, self.n(inc)))
            elif fn is not None and not isd:
                m = self.op(e, None, inc="b_" + e)
                marks.append(("b_" + e, m))
        marks += [(s, self.n(s)) for s in sorted(self.dma_sems)]
        for e in ENGINES:
            self.op(e, None, waits=marks)

    def replay(self, sems):
        def make(name):
            stream = self.streams[name]

            def body(E):
                last_real = None
                last_was_dma = False
                for fn, waits, inc, inck, is_dma in stream:
                    for s, v in waits:
                        E.wait_ge(sems[s], v)
                    if fn is not None:
                        ins = fn(E)
                        last_real = ins
                        last_was_dma = is_dma
                        if inc is not None:
                            ins.then_inc(sems[inc], inck)
                    elif inc is not None:
                        if last_real is not None and not (last_was_dma and inck % 16):
                            last_real.then_inc(sems[inc], inck)
                        else:
                            E.nop().then_inc(sems[inc], inck)
            return body
        return {e: make(e) for e in ENGINES}


def build_nc():
    nc = bass.Bass("TRN2", target_bir_lowering=False, dynamic_dma_scratch_size=512)
    b = Sched()
    QU = {0: "sync", 1: "scalar"}
    QO = {0: "scalar", 1: "sync"}
    C = SCALE

    dt = nc.dram_tensor
    xf = dt("xf", [SEQ, DM], F32, kind="ExternalInput").ap()
    xo = dt("xo", [TPC, DM], F32, kind="ExternalInput").ap()
    qidx = dt("qidx", [1, TPC], F32, kind="ExternalInput").ap()
    kidx = dt("kidx", [128, NKT], F32, kind="ExternalInput").ap()
    wqkv = dt("wqkv", [DM, 3 * DM], F32, kind="ExternalInput").ap()
    wq2 = dt("wq2", [DM, DM], F32, kind="ExternalInput").ap()
    wk2 = dt("wk2", [DM, DM], F32, kind="ExternalInput").ap()
    wv2 = dt("wv2", [DM, DM], F32, kind="ExternalInput").ap()
    wo2 = dt("wo2", [DM, DM], F32, kind="ExternalInput").ap()
    aa = dt("aa", [DM, 16], F32, kind="ExternalInput").ap()
    bbi = dt("bbi", [16, FF], F32, kind="ExternalInput").ap()
    ln1r = dt("ln1r", [1, DM], F32, kind="ExternalInput").ap()
    ln2r = dt("ln2r", [1, DM], F32, kind="ExternalInput").ap()
    wout = dt("wout", [DM, DM], F32, kind="ExternalInput").ap()
    win = dt("win", [FF, DM], F32, kind="ExternalInput").ap()
    wup = dt("wup", [FF, DM], F32, kind="ExternalInput").ap()
    wdn = dt("wdn", [DM, FF], F32, kind="ExternalInput").ap()
    ident = dt("ident", [128, 128], F32, kind="ExternalInput").ap()
    out = dt("out", [TPC, DM], F32, kind="ExternalOutput").ap()

    k1w = dt("k1w", [DM, DM], BF16, kind="Internal").ap()
    v1w = dt("v1w", [DM, DM], BF16, kind="Internal").ap()
    q1w = dt("q1w", [DM, DM], BF16, kind="Internal").ap()
    wq2t = dt("wq2t", [DM, DM], BF16, kind="Internal").ap()
    wk2t = dt("wk2t", [DM, DM], BF16, kind="Internal").ap()
    bbd = dt("bbd", [16, FF], BF16, kind="Internal").ap()
    wvwoad = dt("wvwoad", [DM, 256], BF16, kind="Internal").ap()
    V2d = dt("V2d", [NH, 128, NKT, 16], BF16, kind="Internal").ap()
    K1d = dt("K1d", [NH, 128, SEQ], BF16, kind="Internal").ap()
    V1d = dt("V1d", [NH, SEQ, 128], BF16, kind="Internal").ap()
    K2d = dt("K2d", [NH, 128, SEQ], BF16, kind="Internal").ap()
    Q1d = dt("Q1d", [NH, 128, TPC], BF16, kind="Internal").ap()
    Q2d = dt("Q2d", [NH, 128, TPC], BF16, kind="Internal").ap()

    dbg = {}
    if DEBUG_OUTS:
        for nm, shp in [("d_K1", [NH, 128, SEQ]), ("d_V1", [NH, SEQ, 128]),
                        ("d_K2", [NH, 128, SEQ]), ("d_Q1", [NH, 128, TPC]),
                        ("d_Q2", [NH, 128, TPC]), ("d_V2p", [NH, 128, NKT, 16]),
                        ("d_av1", [128, NH, TPC]), ("d_mod", [16, TPC]),
                        ("d_acc0", [TPC, DM]), ("d_mhat", [128, 32]),
                        ("d_hA", [16, TPC]), ("d_k1w", [DM, DM]), ("d_wq2t", [DM, DM])]:
            dbg[nm] = dt(nm, shp, F32 if nm in ("d_acc0", "d_mhat", "d_mod", "d_hA") else BF16,
                         kind="ExternalOutput").ap()

    sbA = nc.alloc_sbuf_tensor
    id_bf = sbA("id_bf", [128, 128], BF16)
    ones_r = sbA("ones_r", [1, 128], F32)
    ones_c = sbA("ones_c", [128, 1], BF16)
    kidx_sb = sbA("kidx_sb", [128, NKT], F32)
    a2_bf = sbA("a2_bf", [128, NT, 16], BF16)
    ln1b = sbA("ln1b", [128, DM], BF16)
    id_f32 = sbA("id_f32", [128, 128], F32)
    ln1c = sbA("ln1c", [128, 16], F32)
    ln2c = sbA("ln2c", [128, 16], F32)
    mhat = sbA("mhat", [128, 32], F32)
    nrm4 = sbA("nrm4", [1, 64], F32)
    ones32 = sbA("ones32", [128, 33], BF16)
    modpre = sbA("modpre", [16, TPC], F32)[:]
    msc = sbA("msc", [16, TPC], F32)[:]
    hAr = sbA("hAr", [16, TPC], F32)[:]
    idf16 = sbA("idf16", [16, 16], F32)
    stats = sbA("stats", [128, NPP, 16], F32)
    mrow = sbA("mrow", [1, 32], F32)
    cst = sbA("cst", [128, 4], F32)
    arX = sbA("arX", [128, NT * 2048], BF16)
    arB = sbA("arB", [128, 15360], BF16)
    arC = sbA("arC", [128, 16384], BF16)
    arD = sbA("arD", [128, 4096], BF16)
    arE = sbA("arE", [128, 2048], BF16)
    accT = [sbA(f"acc{i}", [128, DM], F32) for i in range(NSLOT)]

    pA = nc.alloc_psum_tensor("pA", [128, 1024], F32)
    pB = nc.alloc_psum_tensor("pB", [128, 1024], F32)
    pC = nc.alloc_psum_tensor("pC", [128, 1024], F32)
    pD = nc.alloc_psum_tensor("pD", [128, 1024], F32)

    arBf = arB[:].bitcast(F32)
    arCf = arC[:].bitcast(F32)
    arEf = arE[:].bitcast(F32)
    arXf = arX[:].bitcast(F32)
    qib = arXf[:, 1024:2048]
    pT2 = pD[:].bitcast(BF16)

    id_f = id_f32[:]
    b.dma("sync", id_f, ident, sem="di")
    b.dma("sync", idf16[:], ident[0:16, 0:16], sem="di")
    b.dma("sync", kidx_sb[:], kidx, sem="di")
    b.dma("sync", ln1c[:], ln1r.rearrange("o (c p) -> p (o c)", p=128), sem="di", slow=True)
    b.dma("sync", ln2c[:], ln2r.rearrange("o (c p) -> p (o c)", p=128), sem="di", slow=True)
    aa_sb = arCf[:, 0:NT * 16].rearrange("p (c r) -> p c r", r=16)
    b.dma("sync", aa_sb, aa.rearrange("(c p) r -> p c r", p=128), sem="di")
    qi_row = arXf[0:1, 0:TPC]
    b.dma("sync", qi_row, qidx, sem="di")
    l1_row = arBf[0:1, 0:DM]
    b.dma("sync", l1_row, ln1r, sem="di")
    dl0 = b.n("di")
    b.op("vector", lambda E: E.tensor_copy(id_bf[:], id_f), waits=[("di", dl0)])
    b.op("vector", lambda E: E.memset(ones_r[:], 1.0))
    b.op("vector", lambda E: E.memset(ones_c[:], 1.0))
    b.op("vector", lambda E: E.memset(cst[:, 0:1], EPS_LN))
    b.op("vector", lambda E: E.memset(cst[:, 1:2], EPS_RMS))
    b.op("vector", lambda E: E.memset(cst[:, 2:3], 1.0))
    svz = b.op("vector", lambda E: E.memset(ones32[:], 0.0), inc="sv")
    b.op("vector", lambda E: E.memset(ones32[:, 32:33], 1.0), waits=[("sv", svz)])
    sv0 = b.op("vector", None, inc="sv")
    b.op("tensor", lambda E: E.matmul(pD[:, 0:512], ones_r[:], qi_row[0:1, 0:512], start=True, stop=True),
         waits=[("sv", sv0)])
    b.op("tensor", lambda E: E.matmul(pD[:, 512:1024], ones_r[:], qi_row[0:1, 512:1024], start=True, stop=True))
    for _h in range(2):
        b.op("tensor", lambda E, _h=_h: E.matmul(pA[:, 512 * _h:512 * (_h + 1)], ones_r[:],
                                                 l1_row[0:1, 512 * _h:512 * (_h + 1)], start=True, stop=True))
    for _h in range(2):
        b.op("tensor", lambda E, _h=_h: E.matmul(pB[:, 512 * _h:512 * (_h + 1)], ones_r[:],
                                                 l1_row[0:1, 1024 + 512 * _h:1024 + 512 * (_h + 1)], start=True, stop=True))
    st1b = b.op("tensor", None, inc="st")
    sv_q = b.op("vector", lambda E: E.tensor_copy(qib[:], pD[:, 0:TPC]), waits=[("st", st1b)], inc="sv")
    b.op("vector", lambda E: E.tensor_copy(ln1b[:, 0:1024], pA[:]))
    svl = b.op("vector", lambda E: E.tensor_copy(ln1b[:, 1024:2048], pB[:]), inc="sv")
    mask_sb = arD[:].bitcast(FP8).rearrange("p (k q) -> p k q", q=128)
    for k in range(NKT):
        j = k // 8
        b.op("vector", lambda E, k=k, j=j: E.tensor_scalar(
            mask_sb[:, k, :], qib[:, j * 128:(j + 1) * 128], kidx_sb[:, k:k + 1], None, ALU.is_ge),
            waits=[("sv", sv_q)] if k == 0 else ())
    svm = b.op("vector", None, inc="sv")
    for i in range(NT):
        b.op("vector", lambda E, i=i: E.tensor_scalar(
            a2_bf[:, i, :], aa_sb[:, i, :], ln2c[:, i:i + 1], None, ALU.mult))
    b.op("vector", None, inc="sv")
    bstage_f = arBf[0:16, 0:1024]
    bstage_b = arB[:][0:16, 2048:3072]
    for q in range(8):
        dq = b.dma("sync", bstage_f, bbi[:, q * 1024:(q + 1) * 1024],
                   waits=[("sv", b.n("sv"))], sem="db")
        svb = b.op("vector", lambda E: E.tensor_copy(bstage_b, bstage_f),
                   waits=[("db", dq), ("dbo", b.n("dbo"))], inc="sv")
        b.dma("scalar", bbd[:, q * 1024:(q + 1) * 1024], bstage_b, waits=[("sv", svb)], sem="dbo")
    woa_f = arCf[:, NT * 16:2 * NT * 16].rearrange("p (c r) -> p c r", r=16)
    wsl = arBf[:, 2048:6144].rearrange("p (u d) -> p u d", u=2)
    pwoa = pC[:, 0:256].rearrange("p (c r) -> p c r", r=16)
    for i in range(NT):
        dli = b.dma("sync", wsl[:, i % 2, :], wo2[128 * i:128 * (i + 1), :],
                    waits=[("st", b.n("st") - 1)] if i >= 2 else [("sv", svl)],
                    sem="dw%d" % (i % 2))
        w0 = [("dw%d" % (i % 2), dli), ("sv", svm)]
        for j in range(NT):
            b.op("tensor", lambda E, i=i, j=j: E.matmul(
                pwoa[:, j, :], wsl[:, i % 2, 128 * j:128 * (j + 1)], aa_sb[:, i, :],
                start=(i == 0 and j == 0), stop=(i == NT - 1 and j == NT - 1)),
                waits=w0 if j == 0 else ())
        b.op("tensor", None, inc="st")
    stw = b.n("st")
    svw = b.op("vector", lambda E: E.tensor_copy(woa_f[:], pwoa[:]), waits=[("st", stw)], inc="sv")
    wvstage = arC[:][:, 7168:11264].rearrange("p (c x) -> p c x", x=256)
    for i in range(NT):
        dli = b.dma("sync", wsl[:, i % 2, :], wv2[128 * i:128 * (i + 1), :],
                    waits=[("st", b.n("st") - 1)] if i >= 2 else [("st", b.n("st"))],
                    sem="dw%d" % (i % 2))
        svf = b.op("vector", lambda E, i=i: E.tensor_tensor(
            wsl[:, i % 2, :], wsl[:, i % 2, :], ln1b[:], ALU.mult),
            waits=[("dw%d" % (i % 2), dli), ("sv", svl)], inc="sv")
        w0 = [("sv", svf), ("sv", svw), ("sv", b.n("sv"))]
        for j in range(NT):
            b.op("tensor", lambda E, i=i, j=j: E.matmul(
                pwoa[:, j, :], wsl[:, i % 2, 128 * j:128 * (j + 1)], woa_f[:, i, :],
                start=True, stop=True),
                waits=w0 if j == 0 else ())
        stt = b.op("tensor", None, inc="st")
        b.op("vector", lambda E, i=i: E.tensor_copy(wvstage[:, :, 16 * i:16 * (i + 1)], pwoa[:]),
             waits=[("st", stt)], inc="sv")
    svw2 = b.op("vector", None, inc="sv")
    b.dma("scalar", wvwoad.rearrange("(c p) x -> p c x", p=128), wvstage[:],
          waits=[("sv", svw2)], sem="dwv")

    cstage = arC[:][:, 2048:6144].rearrange("p (u d) -> p u d", u=2)
    it = 0
    for i in range(NT):
        for part, (wsrc, wdst) in enumerate([
                (wqkv[128 * i:128 * (i + 1), 0:DM], q1w[128 * i:128 * (i + 1), :]),
                (wqkv[128 * i:128 * (i + 1), DM:2 * DM], k1w[128 * i:128 * (i + 1), :]),
                (wqkv[128 * i:128 * (i + 1), 2 * DM:3 * DM], v1w[128 * i:128 * (i + 1), :])]):
            u = it % 2
            dli = b.dma("sync", wsl[:, u, :], wsrc,
                        waits=[("sv", b.n("sv") - 1)] if it >= 2 else [("sv", svw2)],
                        sem="dw%d" % u)
            svc = b.op("vector", lambda E, u=u: E.tensor_copy(cstage[:, u, :], wsl[:, u, :]),
                       waits=[("dw%d" % u, dli), ("dc%d" % u, b.n("dc%d" % u))],
                       inc="sv")
            b.dma("scalar", wdst, cstage[:, u, :], waits=[("sv", svc)], sem="dc%d" % u)
            it += 1
    tstage = arC[:][:, 6144:7168].rearrange("p (u d) -> p u d", u=2)
    for wsrc, wdst in [(wq2, wq2t), (wk2, wk2t)]:
        for i in range(NT):
            u = i % 2
            dli = b.dma("sync", wsl[:, u, :], wsrc[128 * i:128 * (i + 1), :],
                        waits=[("st", b.n("st") - 1)] if i >= 2 else [("sv", b.n("sv"))],
                        sem="dw%d" % u)
            svc = b.op("vector", lambda E, u=u: E.tensor_tensor(
                cstage[:, u, :], wsl[:, u, :], ln1b[:], ALU.mult),
                waits=[("dw%d" % u, dli), ("dc%d" % u, b.n("dc%d" % u))], inc="sv")
            for g in range(4):
                for jj in range(4):
                    j = 4 * g + jj
                    ws = []
                    if jj == 0:
                        ws = [("sv", max(svc, b.n("sv")))]
                    b.op("tensor", lambda E, u=u, j=j, jj=jj, g=g: E.transpose(
                        pT2[:, (g % 2) * 1024 + jj * 128:(g % 2) * 1024 + (jj + 1) * 128],
                        cstage[:, u, 128 * j:128 * (j + 1)], id_bf[:]),
                        waits=ws)
                stt = b.op("tensor", None, inc="st")
                uu = g % 2
                sve = b.op("vector", lambda E, uu=uu: E.tensor_copy(
                    tstage[:, uu, :], pT2[:, uu * 1024:uu * 1024 + 512]),
                    waits=[("st", stt), ("dt%d" % uu, b.n("dt%d" % uu))], inc="sv")
                b.dma("scalar",
                      wdst[512 * g:512 * (g + 1), 128 * i:128 * (i + 1)].rearrange(
                          "(jj p) o -> p jj o", p=128),
                      tstage[:, uu, :].rearrange("p (jj o) -> p jj o", o=128),
                      waits=[("sv", sve)], sem="dt%d" % uu)
    def early_out():
        if BUILD_PHASES in ("p0", "p1a1", "p1a"):
            for tt in range(NSLOT):
                b.op("vector", lambda E, tt=tt: E.memset(accT[tt][:], 0.0))
            b.op("vector", None, inc="sv")
        svq = b.n("sv")
        for tt in range(NSLOT):
            b.dma("sync", out[128 * tt:128 * (tt + 1), :], accT[tt][:],
                  waits=[("sv", svq)] if tt == 0 else (), sem="so")
        b.op("sync", None, waits=[("so", b.n("so"))])
        return _finish(nc, b)

    b.barrier()
    if BUILD_PHASES == "p0":
        return early_out()

    arXv = arX[:].rearrange("p (c t) -> p c t", t=2048)
    xnT = arXv
    xdb = arCf[:, 0:6144].rearrange("p (u d) -> p u d", u=3)
    v2stage = arC[:][:, 12288:16384].rearrange("p (n x) -> p n x", x=256)
    xsc = arCf[:, 4096:6144]
    sqf = arCf[:, 6144:8192]
    xnbu = {0: arE[:][:, 0:2048], 1: accT[7][:].bitcast(BF16)[:, 0:2048]}
    r1row = arEf[0:1, 0:1024]
    r1bc = ln1b[:]
    kwsl = arB[:][:, 0:6144].rearrange("p (u c o) -> p u c o", u=3, c=NT)
    vwsl = arB[:][:, 6144:14336].rearrange("p (u c o) -> p u c o", u=2, c=NT)
    estg = arB[:][:, 14336:15360].rearrange("p (u d) -> p u d", u=2)

    def stat_chain(tt, sv_sum):
        s = lambda i: stats[:, tt, i:i + 1]
        steps = [
            lambda E: E.activation(s(2), s(0), AF.Copy, scale=1.0 / DM),
            lambda E: E.activation(s(3), s(2), AF.Square),
            lambda E: E.activation(s(4), s(1), AF.Copy, scale=1.0 / DM),
            lambda E: E.activation(s(5), s(3), AF.Relu, scale=-1.0, bias=s(4)),
            lambda E: E.activation(s(6), s(5), AF.Ln, bias=cst[:, 0:1]),
            lambda E: E.activation(s(7), s(6), AF.Exp, scale=-0.5),
            lambda E: E.activation(s(8), s(6), AF.Exp, scale=-1.0),
            lambda E: E.activation(s(9), s(8), AF.Relu, scale=-EPS_LN, bias=cst[:, 2:3]),
            lambda E: E.activation(s(10), s(9), AF.Ln, bias=cst[:, 1:2]),
            lambda E: E.activation(s(13), s(10), AF.Exp, scale=-0.5),
        ]
        last = 0
        for i, fn in enumerate(steps):
            w = [("sv", sv_sum)] if i == 0 else [("sa", last)]
            last = b.op("scalar", fn, waits=w, inc="sa")
        return last

    def emit_pass(src, ntiles, own, guard_st):
        ptev = {0: b.n("sv"), 1: b.n("sv")}
        nslots = 2 if own else 3
        xev = {u: [("sv", b.n("sv")), ("sa", b.n("sa"))] for u in range(nslots)}
        tsv = {}
        bmark = {}
        sa_cd = {}

        def stage_a(tt):
            u = tt % nslots
            dl_x = b.dma(QU[tt % 2], xdb[:, u, :], src[128 * tt:128 * (tt + 1), :],
                         waits=xev[u], sem="dx%d" % u)
            sv_sum = b.op("vector", lambda E, u=u, tt=tt: E.tensor_reduce(
                stats[:, tt, 0:1], xdb[:, u, :], mybir.AxisListType.X, ALU.add),
                waits=[("dx%d" % u, dl_x)], inc="sv")
            sqo = xsc if own else accT[tt % 4][:]
            sqw = [("dx%d" % u, dl_x)]
            if own and tt < 2:
                sqw.append(("dvo", b.n("dvo")))
            b.op("scalar", lambda E, u=u, tt=tt, sqo=sqo: E.activation(
                sqf if own else sqo, xdb[:, u, :], AF.Square, accum_out=stats[:, tt, 1:2]),
                waits=sqw)
            sa_c = stat_chain(tt, sv_sum)
            sa_cd[tt] = sa_c
            s = lambda i, tt=tt: stats[:, tt, i:i + 1]
            if own:
                sv_xn = b.op("vector", lambda E, u=u, m=s(2), r=s(7): E.tensor_scalar(
                    xsc, xdb[:, u, :], m, r, ALU.subtract, ALU.mult),
                    waits=[("sa", sa_c), ("st", guard_st)], inc="sv")
                wb = [("sv", sv_xn), ("st", guard_st if tt == 0 else bmark[tt - 1])]
                sv_xnb = b.op("vector", lambda E, xb=xnbu[0]: E.tensor_copy(xb, xsc),
                              waits=wb, inc="sv")
                sv_acc = b.op("vector", lambda E, u=u, tt=tt: E.tensor_tensor(
                    accT[tt][:], xdb[:, u, :], xsc, ALU.add), waits=[("sv", sv_xnb)], inc="sv")
                xev[u] = [("sv", sv_acc), ("sa", sa_c)]
                tsv[tt] = sv_xnb
            else:
                xb = xnbu[tt % 2]
                wb = [("sa", sa_c), ("st", guard_st if tt < 2 else bmark[tt - 2])]
                sv_xnb = b.op("vector", lambda E, u=u, xb=xb, m=s(2), r=s(7): E.tensor_scalar(
                    xb, xdb[:, u, :], m, r, ALU.subtract, ALU.mult),
                    waits=wb, inc="sv")
                xev[u] = [("sv", sv_xnb), ("sa", sa_c)]
                tsv[tt] = sv_xnb

        def stage_b(tt):
            xb = xnbu[0] if own else xnbu[tt % 2]
            svr = tsv[tt]
            stt = 0
            for g in range(4):
                for jj in range(4):
                    j = 4 * g + jj
                    ws = []
                    if jj == 0:
                        ws = [("sv", max(svr, ptev[g % 2]))]
                    b.op("tensor", lambda E, j=j, jj=jj, g=g, xb=xb: E.transpose(
                        pT2[:, (g % 2) * 1024 + jj * 128:(g % 2) * 1024 + jj * 128 + 128],
                        xb[:, 128 * j:128 * (j + 1)], id_bf[:]),
                        waits=ws)
                stt = b.op("tensor", None, inc="st")
                ptev[g % 2] = b.op("vector", lambda E, g=g, tt=tt: E.tensor_copy(
                    xnT[:, 4 * g:4 * g + 4, 128 * tt:128 * (tt + 1)],
                    pT2[:, (g % 2) * 1024:(g % 2) * 1024 + 512].rearrange("p (c t) -> p c t", t=128)),
                    waits=[("st", stt)], inc="sv")
            bmark[tt] = stt

        if own:
            for tt in range(ntiles):
                stage_a(tt)
                stage_b(tt)
        else:
            stage_a(0)
            stage_a(1)
            for tt in range(ntiles):
                stage_b(tt)
                if tt + 2 < ntiles:
                    stage_a(tt + 2)

        last_sa = sa_cd[ntiles - 1]
        rc_mark = 0
        bc_mark = 0
        for rnd in range((ntiles + 7) // 8):
            ws0 = [("sa", last_sa), ("sv", max(ptev[0], ptev[1]))]
            if rnd > 0:
                ws0.append(("sv", rc_mark))
            for j in range(8):
                tt = 8 * rnd + j
                b.op("tensor", lambda E, tt=tt, j=j: E.transpose(
                    pD[0:1, 128 * j:128 * (j + 1)], stats[:, tt, 13:14], id_f32[:]),
                    waits=ws0 if j == 0 else ())
            sttr = b.op("tensor", None, inc="st")
            sv_row = b.op("vector", lambda E: E.tensor_copy(r1row, pD[0:1, 0:1024]),
                          waits=[("st", sttr)], inc="sv")
            for q in range(2):
                b.op("tensor", lambda E, q=q: E.matmul(
                    pD[:, 512 * q:512 * (q + 1)], ones_r[:],
                    r1row[0:1, 512 * q:512 * (q + 1)], start=True, stop=True),
                    waits=[("sv", sv_row)] if q == 0 else ())
            bc_mark = b.op("tensor", None, inc="st")
            rc_mark = b.op("vector", lambda E, rnd=rnd: E.tensor_copy(
                r1bc[:, 1024 * rnd:1024 * (rnd + 1)], pD[:, 0:1024]),
                waits=[("st", bc_mark)], inc="sv")
        return b.n("sv"), rc_mark, bc_mark

    def proj_fm(wsrc_bf, dst, sv_ready, col0, ntok, rescale_sv, prev_marks):
        ev = {0: b.n("sv"), 1: b.n("sv")}
        ntch = ntok // 512
        base_st = b.n("st")
        marks = []
        for oc in range(NT):
            u = oc % 3
            if oc >= 3:
                wld = [("st", marks[oc - 3])]
            elif prev_marks is not None:
                wld = [("st", prev_marks[u])]
            else:
                wld = [("st", base_st)]
            dli = b.dma(QU[oc % 2], kwsl[:, u, :, :],
                        wsrc_bf[:, 128 * oc:128 * (oc + 1)].rearrange("(c p) o -> p c o", p=128),
                        waits=wld, sem="dk%d" % u)
            stt = 0
            for tch in range(ntch):
                pu = tch % 2
                pdst = (pA if pu == 0 else pB)[:, 0:512]
                w0 = [("dk%d" % u, dli), ("sv", sv_ready), ("sv", ev[pu])]
                for c in range(NT):
                    b.op("tensor", lambda E, u=u, c=c, tch=tch, pdst=pdst: E.matmul(
                        pdst, kwsl[:, u, c, :], xnT[:, c, 512 * tch:512 * (tch + 1)],
                        start=(c == 0), stop=(c == NT - 1)),
                        waits=w0 if c == 0 else ())
                stt = b.op("tensor", None, inc="st")
                eu = tch % 2
                if rescale_sv is None:
                    sve = b.op("vector", lambda E, eu=eu, pdst=pdst: E.tensor_copy(
                        estg[:, eu, :], pdst),
                        waits=[("st", stt), ("de%d" % eu, b.n("de%d" % eu))], inc="sv")
                else:
                    sve = b.op("vector", lambda E, eu=eu, pdst=pdst, tch=tch: E.tensor_tensor(
                        estg[:, eu, :], pdst, r1bc[:, 512 * tch:512 * (tch + 1)], ALU.mult),
                        waits=[("st", stt), ("de%d" % eu, b.n("de%d" % eu)),
                               ("sv", rescale_sv)], inc="sv")
                ev[pu] = sve
                b.dma(QO[eu], dst[oc][:, col0 + 512 * tch:col0 + 512 * (tch + 1)],
                      estg[:, eu, :], waits=[("sv", sve)], sem="de%d" % eu)
            marks.append(stt)
        return marks

    K1l = [K1d[h] for h in range(NH)]
    K2l = [K2d[h] for h in range(NH)]
    guard = b.n("st")
    pmarks = None
    vprev = None
    for p in range(NPASS):
        sv_ready, sv_r1, guard = emit_pass(xf[PTOK * p:PTOK * (p + 1), :], NPP, False, guard)
        mkK1 = proj_fm(k1w, K1l, sv_ready, PTOK * p, PTOK, None, pmarks)
        ev = {0: b.n("sv"), 1: b.n("sv")}
        vmk = []
        for avs in range(8):
            u = avs % 2
            if avs >= 2:
                wld = [("st", vmk[avs - 2])]
            elif vprev is not None:
                wld = [("st", vprev[6 + avs])]
            else:
                wld = [("st", b.n("st"))]
            dli = b.dma(QU[u], vwsl[:, u, :, :],
                        v1w[:, 256 * avs:256 * (avs + 1)].rearrange("(c p) o -> p c o", p=128),
                        waits=wld, sem="dv%d" % u)
            stt = 0
            for tt in range(NPP):
                pu = tt % 2
                pdst = (pA if pu == 0 else pB)[:, 512:768]
                w0 = [("dv%d" % u, dli), ("sv", sv_ready), ("sv", ev[pu])]
                for c in range(NT):
                    b.op("tensor", lambda E, u=u, c=c, tt=tt, pdst=pdst: E.matmul(
                        pdst, xnT[:, c, 128 * tt:128 * (tt + 1)], vwsl[:, u, c, :],
                        start=(c == 0), stop=(c == NT - 1)),
                        waits=w0 if c == 0 else ())
                stt = b.op("tensor", None, inc="st")
                eu = tt % 2
                sve = b.op("vector", lambda E, eu=eu, pdst=pdst: E.tensor_copy(
                    estg[:, eu, 0:256], pdst),
                    waits=[("st", stt), ("de%d" % eu, b.n("de%d" % eu))], inc="sv")
                ev[pu] = sve
                h0 = avs * 2
                b.dma(QO[eu],
                      V1d[h0:h0 + 2, PTOK * p + 128 * tt:PTOK * p + 128 * (tt + 1), :].rearrange(
                          "h t d -> t h d"),
                      estg[:, eu, 0:256].rearrange("p (h d) -> p h d", h=2),
                      waits=[("sv", sve)], sem="de%d" % eu)
            vmk.append(stt)
        vprev = vmk
        mkK2 = proj_fm(wk2t, K2l, sv_ready, PTOK * p, PTOK, sv_r1, None)
        wvsl = kwsl[:].rearrange("p u c o -> p (u c o)")[:, 0:NT * 256].rearrange(
            "p (c x) -> p c x", x=256)
        dwv = b.dma("sync", wvsl, wvwoad.rearrange("(c p) x -> p c x", p=128),
                    waits=[("st", b.n("st")), ("dwv", b.n("dwv"))], sem="dwv")
        pv2h = {0: pC[:, 0:256], 1: pC[:, 512:768]}
        ev = {0: b.n("sv"), 1: b.n("sv")}
        v2mk = 0
        for tt in range(NPP):
            pu = tt % 2
            pv2 = pv2h[pu]
            w0 = [("sv", sv_ready), ("sv", ev[pu]), ("st", b.n("st")), ("dwv", dwv)] if tt <= 1 else [("sv", ev[pu])]
            for c in range(NT):
                b.op("tensor", lambda E, c=c, tt=tt, pv2=pv2: E.matmul(
                    pv2, xnT[:, c, 128 * tt:128 * (tt + 1)], wvsl[:, c, :],
                    start=(c == 0), stop=(c == NT - 1)),
                    waits=w0 if c == 0 else ())
            v2mk = b.op("tensor", None, inc="st")
            ev[pu] = b.op("vector", lambda E, tt=tt, pv2=pv2: E.tensor_scalar(
                v2stage[:, tt, :], pv2, stats[:, tt, 13:14], None, ALU.mult),
                waits=[("st", v2mk), ("dvo", b.n("dvo"))], inc="sv")
        svv = b.n("sv")
        for h in range(NH):
            b.dma("scalar", V2d[h, :, NPP * p:NPP * (p + 1), :], v2stage[:, :, 16 * h:16 * (h + 1)],
                  waits=[("sv", svv)] if h == 0 else (), sem="dvo")
        pmarks = {0: v2mk, 1: v2mk, 2: mkK2[14]}
        if BUILD_PHASES == "p1a1":
            b.barrier()
            return early_out()

    b.barrier()
    if BUILD_PHASES == "p1a":
        return early_out()

    sv_ready, sv_r1, guard = emit_pass(xo, NSLOT, True, guard)
    Q1l = [Q1d[h] for h in range(NH)]
    Q2l = [Q2d[h] for h in range(NH)]
    mkQ1 = proj_fm(q1w, Q1l, sv_ready, 0, TPC, None, pmarks)
    proj_fm(wq2t, Q2l, sv_ready, 0, TPC, sv_r1, None)
    ph = pC[0:16, 0:1024]
    for tch in range(2):
        w0 = [("sv", sv_ready), ("st", b.n("st"))]
        for c in range(NT):
            b.op("tensor", lambda E, c=c, tch=tch: E.matmul(
                ph[:, 512 * tch:512 * (tch + 1)], a2_bf[:, c, :],
                xnT[:, c, 512 * tch:512 * (tch + 1)],
                start=(c == 0), stop=(c == NT - 1)),
                waits=w0 if c == 0 else ())
    stt = b.op("tensor", None, inc="st")
    b.op("vector", lambda E: E.tensor_copy(hAr, ph), waits=[("st", stt)], inc="sv")

    smp = arC[:][:, 0:512]
    ssq = arC[:][:, 512:1024]
    pr = pD[0:1, 0:512]
    for idx, srcl in enumerate([K1l, K2l, Q1l, Q2l]):
        for h in range(NH):
            dls = b.dma("sync", smp, srcl[h][:, 0:512],
                        waits=[("sv", b.n("sv")), ("sa", b.n("sa")), ("st", b.n("st")),
                               ("de0", b.n("de0")), ("de1", b.n("de1"))], sem="dsm")
            sas = b.op("scalar", lambda E: E.activation(ssq, smp, AF.Square),
                       waits=[("dsm", dls), ("st", b.n("st"))], inc="sa")
            stt = b.op("tensor", lambda E: E.matmul(pr, ones_c[:], ssq, start=True, stop=True),
                       waits=[("sa", sas), ("sv", b.n("sv"))], inc="st")
            b.op("vector", lambda E, idx=idx, h=h: E.tensor_reduce(
                nrm4[0:1, idx * 16 + h:idx * 16 + h + 1], pr, mybir.AxisListType.X, ALU.max),
                waits=[("st", stt)], inc="sv")
    svn = b.n("sv")
    t1 = arEf[0:1, 0:32]
    b.op("gpsimd", lambda E: E.tensor_tensor(t1[0:1, 0:16], nrm4[0:1, 32:48], nrm4[0:1, 0:16], ALU.mult),
         waits=[("sv", svn)])
    sg2 = b.op("gpsimd", lambda E: E.tensor_tensor(t1[0:1, 16:32], nrm4[0:1, 48:64], nrm4[0:1, 16:32], ALU.mult),
               inc="sg")
    sa1 = b.op("scalar", lambda E: E.activation(mrow[0:1, :], t1, AF.Ln), waits=[("sg", sg2)], inc="sa")
    sa2 = b.op("scalar", lambda E: E.activation(mrow[0:1, :], mrow[0:1, :], AF.Exp, scale=0.5),
               waits=[("sa", sa1)], inc="sa")
    sa3 = b.op("scalar", lambda E: E.mul(mrow[0:1, :], mrow[0:1, :], -1.25 * C), waits=[("sa", sa2)], inc="sa")
    sv_cl = b.op("vector", lambda E: E.tensor_scalar_max(mrow[0:1, :], mrow[0:1, :], -30.0),
                 waits=[("sa", sa3)], inc="sv")
    stt = b.op("tensor", lambda E: E.matmul(pD[:, 0:32], ones_r[:], mrow[0:1, :], start=True, stop=True),
               waits=[("sv", sv_cl)], inc="st")
    b.op("vector", lambda E: E.tensor_copy(mhat[:], pD[:, 0:32]), waits=[("st", stt)], inc="sv")

    if DEBUG_OUTS:
        sv_all = b.n("sv")
        b.dma("scalar", dbg["d_V2p"], V2d, waits=[("sv", sv_all)], sem="so")
        b.dma("scalar", dbg["d_k1w"], k1w, sem="so")
        b.dma("scalar", dbg["d_wq2t"], wq2t, sem="so")
        b.dma("sync", dbg["d_mhat"], mhat[:], waits=[("sv", sv_all)], sem="so")
        b.dma("sync", dbg["d_hA"], hAr[:], waits=[("sv", sv_all)], sem="so")
        for tt in range(NSLOT):
            b.dma("sync", dbg["d_acc0"][128 * tt:128 * (tt + 1), :], accT[tt][:],
                  waits=[("sv", sv_all)], sem="so")
        for h in range(NH):
            b.dma("scalar", dbg["d_K1"][h], K1d[h], sem="so")
            b.dma("scalar", dbg["d_V1"][h], V1d[h], sem="so")
            b.dma("scalar", dbg["d_K2"][h], K2d[h], sem="so")
            b.dma("scalar", dbg["d_Q1"][h], Q1d[h], sem="so")
            b.dma("scalar", dbg["d_Q2"][h], Q2d[h], sem="so")

    b.barrier()

    if BUILD_PHASES == "p1":
        return early_out()

    ksb1 = arB[:][:, 0:8192]
    vsb = arC[:][:, 0:8192].rearrange("p (n d) -> p n d", d=128)
    qsb = arC[:][:, 8192:10240].rearrange("p (u t) -> p u t", u=2)
    ptb = arC[:][:, 10240:12288].rearrange("p (u t) -> p u t", u=2)
    rcb = arCf[:, 6144:7168]
    sumr = arEf[0:1, 0:1024]
    lnr = arEf[32:33, 0:1024]

    def attention(attn_i, Kl, Ql, av_out_fn, v_load_half, v_lhsT_fn, av_rows):
        two_mm = (av_rows == 128)
        head_s_fin = {}
        st0 = b.n("st")
        dkA = {0: b.dma("sync", ksb1[:, 0:4096], Kl[0][:, 0:4096], waits=[("st", st0)], sem="dKa")}
        dkB = {0: b.dma("sync", ksb1[:, 4096:8192], Kl[0][:, 4096:8192], sem="dKb")}
        dvA = {0: v_load_half(0, 0, st0)}
        dvB = {0: v_load_half(0, 1, st0)}
        for h in range(NH):
            u = h % 2
            wq_ld = [("st", head_s_fin[h - 2])] if h >= 2 else [("st", st0)]
            dq = b.dma("sync", qsb[:, u, :], Ql[h], waits=wq_ld, sem="dq%d" % u)
            bias = mhat[:, attn_i * 16 + h:attn_i * 16 + h + 1]
            sa_exp = {}
            st_av = {}
            prev_done_sv = b.n("sv")
            sa_head0 = b.n("sa")
            def emit_av(k):
                jmin = k // 8
                qoff = 128 * jmin
                pu = k % 2
                sg_m = sg_mask[k]
                pts = [qoff, 512, TPC] if qoff < 512 else [qoff, TPC]
                avch = [(a0, a1) for a0, a1 in zip(pts[:-1], pts[1:]) if a1 > a0]
                wa0 = [("sg", sg_m)]
                if k == 0:
                    wa0.append(("sv", prev_done_sv))
                    wa0.append(("dVa", dvA[h]))
                if k == 32:
                    wa0.append(("dVb", dvB[h]))
                for ci, (a0, a1) in enumerate(avch):
                    stopf = (a0 == qoff and k in (31, 63))
                    startf = (k == 0 and a0 in (0, 512))
                    wa = wa0 if ci == 0 else [("sa", sa_exp[k])]
                    if two_mm:
                        b.op("tensor", lambda E, k=k, a0=a0, a1=a1, pu=pu, qoff=qoff, stopf=stopf, startf=startf: E.matmul(
                            pC[0:av_rows, a0:a1], v_lhsT_fn(k), ptb[:, pu, a0 - qoff:a1 - qoff],
                            start=startf, stop=stopf),
                            waits=wa)
                        b.op("tensor", lambda E, k=k, a0=a0, a1=a1, pu=pu, qoff=qoff, stopf=stopf, startf=startf: E.matmul(
                            pD[0:1, a0:a1], ones_c[:], ptb[:, pu, a0 - qoff:a1 - qoff],
                            start=startf, stop=stopf))
                    else:
                        sums_fn = lambda E, k=k, a0=a0, a1=a1, pu=pu, qoff=qoff, stopf=stopf, startf=startf: E.matmul(
                            pC[0:33, a0:a1], ones32[:], ptb[:, pu, a0 - qoff:a1 - qoff],
                            start=startf, stop=stopf)
                        av_fn = lambda E, k=k, a0=a0, a1=a1, pu=pu, qoff=qoff: E.matmul(
                            pC[0:av_rows, a0:a1], v_lhsT_fn(k), ptb[:, pu, a0 - qoff:a1 - qoff],
                            start=False, stop=False)
                        if k == 0:
                            b.op("tensor", sums_fn, waits=wa)
                            b.op("tensor", av_fn)
                        else:
                            b.op("tensor", av_fn, waits=wa)
                            b.op("tensor", sums_fn)
                st_av[k] = b.op("tensor", None, inc="st")

            sg_mask = {}
            for k in range(NKT):
                jmin = k // 8
                qoff = 128 * jmin
                nq = TPC - qoff
                pu = k % 2
                ps_k = (pA if pu == 0 else pB)
                w0 = [("dq%d" % u, dq)]
                if k == 0:
                    w0.append(("dKa", dkA[h]))
                elif k == 32:
                    w0.append(("dKb", dkB[h]))
                if k >= 2:
                    w0.append(("sa", sa_exp[k - 2]))
                else:
                    w0.append(("sa", sa_head0))
                    w0.append(("sv", prev_done_sv))
                nch = (nq + 511) // 512
                for ch in range(nch):
                    c0 = 512 * ch
                    c1 = min(nq, c0 + 512)
                    b.op("tensor", lambda E, u=u, k=k, c0=c0, c1=c1, ps_k=ps_k, qoff=qoff: E.matmul(
                        ps_k[:, c0:c1], ksb1[:, 128 * k:128 * (k + 1)],
                        qsb[:, u, qoff + c0:qoff + c1], start=True, stop=True),
                        waits=w0 if ch == 0 else ())
                st_s = b.op("tensor", None, inc="st")
                if k == 31:
                    st_s31 = st_s
                if k == NKT - 1:
                    head_s_fin[h] = st_s
                we = [("st", st_s)]
                if k >= 2:
                    we.append(("st", st_av[k - 2]))
                sa_exp[k] = b.op("scalar", lambda E, pu=pu, nq=nq, ps_k=ps_k, bias=bias: E.activation(
                    ptb[:, pu, 0:nq], ps_k[:, 0:nq], AF.Exp, bias=bias, scale=C),
                    waits=we, inc="sa")
                sg_mask[k] = b.op("gpsimd", lambda E, pu=pu, k=k: E.tensor_tensor(
                    ptb[:, pu, 0:128], ptb[:, pu, 0:128], mask_sb[:, k, :], ALU.mult),
                    waits=[("sa", sa_exp[k])], inc="sg")
                if k >= 1:
                    emit_av(k - 1)
                if k == 32 and h + 1 < NH:
                    dkA[h + 1] = b.dma("sync", ksb1[:, 0:4096], Kl[h + 1][:, 0:4096],
                                       waits=[("st", st_s31)], sem="dKa")
                    dvA[h + 1] = v_load_half(h + 1, 0, st_av[31])
            emit_av(NKT - 1)
            if h + 1 < NH:
                dkB[h + 1] = b.dma("sync", ksb1[:, 4096:8192], Kl[h + 1][:, 4096:8192],
                                   waits=[("st", head_s_fin[h])], sem="dKb")
                dvB[h + 1] = v_load_half(h + 1, 1, st_av[NKT - 1])
            sum_src = pD[0:1, 0:1024] if two_mm else pC[32:33, 0:1024]
            sv_s = b.op("vector", lambda E, sum_src=sum_src: E.tensor_copy(sumr, sum_src),
                        waits=[("st", st_av[NKT - 1])], inc="sv")
            sa_l = b.op("scalar", lambda E: E.activation(lnr, sumr, AF.Ln),
                        waits=[("sv", sv_s)], inc="sa")
            sa_r = b.op("scalar", lambda E: E.activation(sumr, lnr, AF.Exp, scale=-1.0),
                        waits=[("sa", sa_l)], inc="sa")
            b.op("tensor", lambda E: E.matmul(pA[:, 0:512], ones_r[:], sumr[0:1, 0:512],
                                              start=True, stop=True),
                 waits=[("sa", sa_r)])
            st_b = b.op("tensor", lambda E: E.matmul(pA[:, 512:1024], ones_r[:], sumr[0:1, 512:1024],
                                                     start=True, stop=True), inc="st")
            sv_rb = b.op("vector", lambda E: E.tensor_copy(rcb, pA[:, 0:1024]),
                         waits=[("st", st_b)], inc="sv")
            av_out_fn(h, sv_rb)

    def av1s(h):
        return arXv[:, h, 1024:2048]

    def v1_load(h, half, stw):
        return b.dma("scalar", vsb[:, 32 * half:32 * (half + 1), :],
                     V1d[h, 4096 * half:4096 * (half + 1), :].rearrange("(n p) d -> p n d", p=128),
                     waits=[("st", stw)], sem="dV%s" % "ab"[half])

    def av1_out(h, sv_rb):
        b.op("vector", lambda E, h=h: E.tensor_tensor(
            av1s(h), pC[:, 0:1024], rcb, ALU.mult),
            waits=[("sv", sv_rb)], inc="sv")

    attention(0, K1l, Q1l, av1_out, v1_load, lambda k: vsb[:, k, :], 128)

    def av2_out(h, sv_rb):
        svt = b.op("vector", lambda E: E.tensor_tensor(
            msc, pC[0:16, 0:1024], rcb[0:16, :], ALU.mult),
            waits=[("sv", sv_rb), ("st", b.n("st"))], inc="sv")
        b.op("tensor", lambda E, h=h: E.matmul(
            pD[0:16, 0:512], idf16[:], msc[:, 0:512], start=(h == 0), stop=(h == NH - 1)),
            waits=[("sv", svt)])
        b.op("tensor", lambda E, h=h: E.matmul(
            pD[0:16, 512:1024], idf16[:], msc[:, 512:1024], start=(h == 0), stop=(h == NH - 1)),
            inc="st")
        if h == NH - 1:
            b.op("vector", lambda E: E.tensor_copy(modpre, pD[0:16, 0:1024]),
                 waits=[("st", b.n("st"))], inc="sv")

    v2sb = arC[:][:, 14336:14336 + NKT * 16].rearrange("p (n r) -> p n r", r=16)

    def v2_load(h, half, stw):
        return b.dma("scalar", v2sb[:, 32 * half:32 * (half + 1), :],
                     V2d[h, :, 32 * half:32 * (half + 1), :],
                     waits=[("st", stw)], sem="dV%s" % "ab"[half])

    attention(1, K2l, Q2l, av2_out, v2_load, lambda k: v2sb[:, k, :], 16)

    if DEBUG_OUTS:
        sv_all = b.n("sv")
        for h in range(NH):
            b.dma("sync", dbg["d_av1"].rearrange("p h t -> p h t")[:, h, :], av1s(h),
                  waits=[("sv", sv_all)], sem="so")
        b.dma("sync", dbg["d_mod"], modpre[:], waits=[("sv", sv_all)], sem="so")

    b.barrier()

    wofu = {0: arCf[:, 0:512], 1: arCf[:, 4608:5120]}
    wob = arC[:][:, 1024:1024 + NT * 512].rearrange("p (c o) -> p c o", o=512)
    wev = {0: b.n("sv"), 1: b.n("sv")}
    pev = {0: b.n("sv"), 1: b.n("sv")}
    for ds in range(4):
        stg = b.n("st")
        for i in range(NT):
            wu = i % 2
            dli = b.dma(QU[wu], wofu[wu], wout[128 * i:128 * (i + 1), 512 * ds:512 * (ds + 1)],
                        waits=[("sv", wev[wu])] + ([("st", stg)] if i < 2 else []),
                        sem="dwo%d" % wu)
            wev[wu] = b.op("vector", lambda E, i=i, wu=wu: E.tensor_copy(wob[:, i, :], wofu[wu]),
                           waits=[("dwo%d" % wu, dli)], inc="sv")
        sv_w = b.n("sv")
        for tt in range(NSLOT):
            pu = tt % 2
            pdst = (pA if pu == 0 else pB)[:, 0:512]
            w0 = [("sv", sv_w), ("sv", pev[pu])]
            for i in range(NT):
                b.op("tensor", lambda E, i=i, tt=tt, pdst=pdst: E.matmul(
                    pdst, arXv[:, i, 1024 + 128 * tt:1024 + 128 * (tt + 1)], wob[:, i, :],
                    start=(i == 0), stop=(i == NT - 1)),
                    waits=w0 if i == 0 else ())
            stt = b.op("tensor", None, inc="st")
            pev[pu] = b.op("vector", lambda E, tt=tt, ds=ds, pdst=pdst: E.tensor_tensor(
                accT[tt][:, 512 * ds:512 * (ds + 1)], accT[tt][:, 512 * ds:512 * (ds + 1)],
                pdst, ALU.add), waits=[("st", stt)], inc="sv")
    b.barrier()

    wT = arC[:][:, 0:8192].rearrange("p (c f) -> p c f", f=512)
    wdnT = arC[:][:, 8192:8192 + 4 * 2048].rearrange("p (c o) -> p c o", o=2048)
    g_all = arB[:][:, 8192:12288].rearrange("p (t f) -> p t f", f=512)
    wstgu = {0: arBf[:, 0:512], 1: arXf[:, 512:1024]}
    wdnfu = {0: arBf[:, 1024:1536], 1: arXf[:, 1536:2048]}
    bsl = arB[:][0:16, 4096:4608]
    gubf = arB[:][:, 6144:6656]
    guT = arB[:][:, 7168:7680].rearrange("p (q t) -> p q t", q=4)
    prod_bf = arB[:][0:16, 4608:4608 + TPC]

    sa_t = b.op("scalar", lambda E: E.activation(msc, modpre, AF.Tanh), inc="sa")
    sv_p = b.op("vector", lambda E: E.scalar_tensor_tensor(
        prod_bf[:], hAr, 0.1, msc, ALU.mult, ALU.mult), waits=[("sa", sa_t)], inc="sv")

    ptev5 = {0: b.n("sv"), 1: b.n("sv")}
    wstv = {0: 0, 1: 0}
    wdtv = {0: 0, 1: 0}
    upev = {0: b.n("sv"), 1: b.n("sv")}
    mev = {0: b.n("sv"), 1: b.n("sv")}

    def build_row_dc(wsrc, dstT, fs, dc, stguard):
        u = dc % 2
        dli = b.dma(QU[u], wstgu[u].rearrange("p (q d) -> p q d", q=4),
                    wsrc[512 * fs:512 * (fs + 1), 128 * dc:128 * (dc + 1)].rearrange(
                        "(q p) d -> p q d", p=128),
                    waits=[("st", max(wstv[u], stguard))], sem="dws%d" % u)
        for q in range(4):
            b.op("tensor", lambda E, q=q, u=u: E.transpose(
                pD[:, 512 * u + q * 128:512 * u + (q + 1) * 128],
                wstgu[u][:, 128 * q:128 * (q + 1)], id_f32[:]),
                waits=[("dws%d" % u, dli), ("sv", ptev5[u])] if q == 0 else ())
        stt = b.op("tensor", None, inc="st")
        wstv[u] = stt
        ptev5[u] = b.op("vector", lambda E, dc=dc, dstT=dstT, u=u: E.tensor_scalar(
            dstT[:, dc, :], pD[:, 512 * u:512 * u + 512], ln2c[:, dc:dc + 1], None, ALU.mult),
            waits=[("st", stt)], inc="sv")

    def build_col_dc(wsrc, dstT, fs, do):
        u = do % 2
        dli = b.dma(QU[1 - u], wdnfu[u], wsrc[128 * do:128 * (do + 1), 512 * fs:512 * (fs + 1)],
                    waits=[("st", wdtv[u])], sem="dwd%d" % u)
        for fc in range(4):
            b.op("tensor", lambda E, fc=fc, u=u: E.transpose(
                pD[:, 512 * u + fc * 128:512 * u + (fc + 1) * 128],
                wdnfu[u][:, 128 * fc:128 * (fc + 1)], id_f32[:]),
                waits=[("dwd%d" % u, dli), ("sv", ptev5[u])] if fc == 0 else ())
        stt = b.op("tensor", None, inc="st")
        wdtv[u] = stt
        ptev5[u] = b.op("vector", lambda E, do=do, dstT=dstT, u=u: E.tensor_copy(
            dstT[:, :, 128 * do:128 * (do + 1)],
            pD[:, 512 * u:512 * u + 512].rearrange("p (c o) -> p c o", o=128)),
            waits=[("st", stt)], inc="sv")

    silu_mark = [0] * NSLOT
    for fs in range(NT):
        stg0 = b.n("st")
        gv0 = b.n("sv")
        dlb = b.dma("scalar", bsl, bbd[:, 512 * fs:512 * (fs + 1)],
                    waits=[("st", stg0)], sem="dbl")
        for dc in range(NT):
            build_row_dc(win, wT, fs, dc, stg0)
        sv_w = b.n("sv")
        gaev = {0: b.n("sa"), 1: b.n("sa")}
        for tt in range(NSLOT):
            pu = tt % 2
            pg = (pA if pu == 0 else pB)[:, 0:512]
            w0 = [("sv", sv_w), ("sv", sv_p), ("dbl", dlb), ("sa", gaev[pu])]
            for c in range(NT):
                b.op("tensor", lambda E, c=c, tt=tt, pg=pg: E.matmul(
                    pg, arXv[:, c, 128 * tt:128 * (tt + 1)], wT[:, c, :],
                    start=(c == 0), stop=False),
                    waits=w0 if c == 0 else ())
            b.op("tensor", lambda E, tt=tt, pg=pg: E.matmul(
                pg, prod_bf[:, 128 * tt:128 * (tt + 1)], bsl,
                start=False, stop=True))
            st_g = b.op("tensor", None, inc="st")
            gaev[pu] = b.op("scalar", lambda E, tt=tt, pg=pg: E.activation(
                g_all[:, tt, :], pg, AF.Silu, scale=stats[:, tt, 13:14]),
                waits=[("st", st_g), ("sv", gv0)], inc="sa")
            silu_mark[tt] = gaev[pu]
            build_col_dc(wdn, wdnT, fs, 2 * tt)
            build_col_dc(wdn, wdnT, fs, 2 * tt + 1)
        stg1 = b.n("st")
        for dc in range(NT):
            build_row_dc(wup, wT, fs, dc, stg1)
        sv_w2 = b.n("sv")
        for tt in range(NSLOT):
            pu = tt % 2
            pg = (pA if pu == 0 else pB)[:, 512:1024]
            w0 = [("sv", sv_w2), ("sv", upev[pu])]
            for c in range(NT):
                b.op("tensor", lambda E, c=c, tt=tt, pg=pg: E.matmul(
                    pg, arXv[:, c, 128 * tt:128 * (tt + 1)], wT[:, c, :],
                    start=(c == 0), stop=(c == NT - 1)),
                    waits=w0 if c == 0 else ())
            st_u = b.op("tensor", None, inc="st")
            sv_gu = b.op("vector", lambda E, tt=tt, pg=pg: E.scalar_tensor_tensor(
                gubf, g_all[:, tt, :], stats[:, tt, 13:14], pg, ALU.mult, ALU.mult),
                waits=[("sa", silu_mark[tt]), ("st", st_u)], inc="sv")
            upev[pu] = sv_gu
            for q in range(4):
                b.op("tensor", lambda E, q=q, pu=pu: E.transpose(
                    pT2[:, 1024 * pu + q * 128:1024 * pu + (q + 1) * 128],
                    gubf[:, 128 * q:128 * (q + 1)], id_bf[:]),
                    waits=[("sv", max(sv_gu, ptev5[pu]))] if q == 0 else ())
            st_t = b.op("tensor", None, inc="st")
            sv_gt = b.op("vector", lambda E, pu=pu: E.tensor_copy(
                guT[:].rearrange("p q t -> p (q t)"), pT2[:, 1024 * pu:1024 * pu + 512]),
                waits=[("st", st_t)], inc="sv")
            ptev5[pu] = sv_gt
            for ds in range(4):
                pu2 = ds % 2
                pdst = pC[:, 512 * pu2:512 * (pu2 + 1)]
                w1 = [("sv", sv_gt), ("sv", mev[pu2])]
                for fc in range(4):
                    b.op("tensor", lambda E, fc=fc, ds=ds, pdst=pdst: E.matmul(
                        pdst, guT[:, fc, :], wdnT[:, fc, 512 * ds:512 * (ds + 1)],
                        start=(fc == 0), stop=(fc == 3)),
                        waits=w1 if fc == 0 else ())
                stt = b.op("tensor", None, inc="st")
                mev[pu2] = b.op("vector", lambda E, tt=tt, ds=ds, pdst=pdst: E.tensor_tensor(
                    accT[tt][:, 512 * ds:512 * (ds + 1)], accT[tt][:, 512 * ds:512 * (ds + 1)],
                    pdst, ALU.add), waits=[("st", stt)], inc="sv")

    sv_f = b.n("sv")
    for tt in range(NSLOT):
        b.dma("sync", out[128 * tt:128 * (tt + 1), :], accT[tt][:],
              waits=[("sv", sv_f)], sem="so")
    b.op("sync", None, waits=[("so", b.n("so"))])
    return _finish(nc, b)


def _finish(nc, b):
    import contextlib
    es = contextlib.ExitStack()
    sems = {}
    names = set()
    for streams in b.streams.values():
        for fn, waits, inc, inck, is_dma in streams:
            if inc:
                names.add(inc)
            for s, v in waits:
                names.add(s)
    with es:
        for nm in sorted(names):
            sems[nm] = es.enter_context(nc.semaphore(nm))
        block = es.enter_context(nc.Block())
        bodies = b.replay(sems)
        block.sync(bodies["sync"])
        block.vector(bodies["vector"])
        block.scalar(bodies["scalar"])
        block.tensor(bodies["tensor"])
        block.gpsimd(bodies["gpsimd"])
    return nc


_ROWS = {}


def _core_rows(c):
    if c not in _ROWS:
        _ROWS[c] = np.concatenate([np.arange(128 * (8 * j + c), 128 * (8 * j + c) + 128)
                                   for j in range(NSLOT)])
    return _ROWS[c]


def kernel(L0_attn_x, L0_attn_w_qkv, L0_attn_w_out, L0_ffn_ln1_w, L0_ffn_ln2_w,
           L0_ffn_Wq, L0_ffn_Wk, L0_ffn_Wv, L0_ffn_Wo, L0_ffn_A, L0_ffn_B,
           L0_ffn_W_in_base, L0_ffn_W_up_base, L0_ffn_W_down):
    f32 = np.float32
    x = np.ascontiguousarray(np.asarray(L0_attn_x, dtype=f32).reshape(SEQ, DM))
    common = dict(
        xf=x,
        wqkv=np.ascontiguousarray(np.asarray(L0_attn_w_qkv, dtype=f32)),
        wq2=np.ascontiguousarray(np.asarray(L0_ffn_Wq, dtype=f32)),
        wk2=np.ascontiguousarray(np.asarray(L0_ffn_Wk, dtype=f32)),
        wv2=np.ascontiguousarray(np.asarray(L0_ffn_Wv, dtype=f32)),
        wo2=np.ascontiguousarray(np.asarray(L0_ffn_Wo, dtype=f32)),
        aa=np.ascontiguousarray(np.asarray(L0_ffn_A, dtype=f32)),
        bbi=np.ascontiguousarray(np.asarray(L0_ffn_B, dtype=f32)),
        ln1r=np.ascontiguousarray(np.asarray(L0_ffn_ln1_w, dtype=f32).reshape(1, DM)),
        ln2r=np.ascontiguousarray(np.asarray(L0_ffn_ln2_w, dtype=f32).reshape(1, DM)),
        wout=np.ascontiguousarray(np.asarray(L0_attn_w_out, dtype=f32)),
        win=np.ascontiguousarray(np.asarray(L0_ffn_W_in_base, dtype=f32)),
        wup=np.ascontiguousarray(np.asarray(L0_ffn_W_up_base, dtype=f32)),
        wdn=np.ascontiguousarray(np.asarray(L0_ffn_W_down, dtype=f32)),
        ident=np.eye(128, dtype=f32),
        kidx=np.ascontiguousarray(np.arange(SEQ, dtype=f32).reshape(NKT, 128).T),
    )
    in_maps = []
    for c in range(NCORES):
        rows = _core_rows(c)
        m = dict(common)
        m["xo"] = np.ascontiguousarray(x[rows])
        m["qidx"] = np.ascontiguousarray(rows.astype(f32).reshape(1, TPC))
        in_maps.append(m)
    nc = build_nc()
    res = run_bass_kernel_spmd(nc, in_maps, core_ids=list(range(NCORES)))
    outp = np.empty((SEQ, DM), dtype=f32)
    for c in range(NCORES):
        outp[_core_rows(c)] = res.results[c]["out"]
    return outp.reshape(1, SEQ, DM)


# version 08
# speedup vs baseline: 1.7464x; 1.0715x over previous
import numpy as np
import ml_dtypes
import concourse.bass as bass
import concourse.mybir as mybir
from concourse.bass_utils import run_bass_kernel_spmd

F32 = mybir.dt.float32
BF16 = mybir.dt.bfloat16
FP8 = mybir.dt.float8e4
AF = mybir.ActivationFunctionType
ALU = mybir.AluOpType

NCORES = 8
SEQ = 8192
DM = 2048
FF = 8192
NH = 16
NT = 16
NSLOT = 8
TPC = 1024
NKT = 64
NPASS = 4
PTOK = 2048
NPP = 16
SCALE = float(1.0 / np.sqrt(128.0))
EPS_LN = 1e-5
EPS_RMS = 1e-6

ENGINES = ("sync", "vector", "scalar", "tensor", "gpsimd")

BUILD_PHASES = "all"
DEBUG_OUTS = False


class Sched:

    def __init__(self):
        self.streams = {e: [] for e in ENGINES}
        self.cnt = {}
        self.dma_sems = set()

    def n(self, sem):
        return self.cnt.get(sem, 0)

    def op(self, eng, fn=None, waits=(), inc=None, inck=1, is_dma=False):
        if fn is None and inc is not None and not waits:
            st = self.streams[eng]
            if st and st[-1][0] is not None and st[-1][2] == inc:
                return self.cnt.get(inc, 0)
        if inc is not None:
            self.cnt[inc] = self.cnt.get(inc, 0) + inck
        ws = tuple((s, v) for (s, v) in waits if v > 0)
        self.streams[eng].append((fn, ws, inc, inck, is_dma))
        return self.cnt.get(inc, 0) if inc else 0

    def dma(self, eng, out, in_, waits=(), sem=None, slow=False):
        sem = sem or ("dl" if eng == "sync" else "dg")
        self.dma_sems.add(sem)
        if slow:
            fn = lambda E, o=out, i=in_: E.dma_start(out=o, in_=i,
                                                     allow_slow_non_contiguous=True)
        else:
            fn = lambda E, o=out, i=in_: E.dma_start(out=o, in_=i)
        return self.op(eng, fn, waits=waits, inc=sem, inck=16, is_dma=True)

    def barrier(self):
        marks = []
        for e in ENGINES:
            st = self.streams[e]
            if not st:
                continue
            fn, ws, inc, inck, isd = st[-1]
            if inc is not None and not isd:
                marks.append((inc, self.n(inc)))
            elif fn is not None and not isd:
                m = self.op(e, None, inc="b_" + e)
                marks.append(("b_" + e, m))
        marks += [(s, self.n(s)) for s in sorted(self.dma_sems)]
        for e in ENGINES:
            self.op(e, None, waits=marks)

    def replay(self, sems):
        def make(name):
            stream = self.streams[name]

            def body(E):
                last_real = None
                last_was_dma = False
                for fn, waits, inc, inck, is_dma in stream:
                    for s, v in waits:
                        E.wait_ge(sems[s], v)
                    if fn is not None:
                        ins = fn(E)
                        last_real = ins
                        last_was_dma = is_dma
                        if inc is not None:
                            ins.then_inc(sems[inc], inck)
                    elif inc is not None:
                        if last_real is not None and not (last_was_dma and inck % 16):
                            last_real.then_inc(sems[inc], inck)
                        else:
                            E.nop().then_inc(sems[inc], inck)
            return body
        return {e: make(e) for e in ENGINES}


def build_nc():
    nc = bass.Bass("TRN2", target_bir_lowering=False, dynamic_dma_scratch_size=512)
    b = Sched()
    QU = {0: "sync", 1: "scalar"}
    QO = {0: "scalar", 1: "sync"}
    C = SCALE

    dt = nc.dram_tensor
    xf = dt("xf", [SEQ, DM], F32, kind="ExternalInput").ap()
    xo = dt("xo", [TPC, DM], F32, kind="ExternalInput").ap()
    qidx = dt("qidx", [1, TPC], F32, kind="ExternalInput").ap()
    kidx = dt("kidx", [128, NKT], F32, kind="ExternalInput").ap()
    wqkv = dt("wqkv", [DM, 3 * DM], F32, kind="ExternalInput").ap()
    wq2 = dt("wq2", [DM, DM], F32, kind="ExternalInput").ap()
    wk2 = dt("wk2", [DM, DM], F32, kind="ExternalInput").ap()
    wv2 = dt("wv2", [DM, DM], F32, kind="ExternalInput").ap()
    wo2 = dt("wo2", [DM, DM], F32, kind="ExternalInput").ap()
    aa = dt("aa", [DM, 16], F32, kind="ExternalInput").ap()
    bbi = dt("bbi", [16, FF], F32, kind="ExternalInput").ap()
    ln1r = dt("ln1r", [1, DM], F32, kind="ExternalInput").ap()
    ln2r = dt("ln2r", [1, DM], F32, kind="ExternalInput").ap()
    wout = dt("wout", [DM, DM], F32, kind="ExternalInput").ap()
    win = dt("win", [FF, DM], F32, kind="ExternalInput").ap()
    wup = dt("wup", [FF, DM], F32, kind="ExternalInput").ap()
    wdn = dt("wdn", [DM, FF], F32, kind="ExternalInput").ap()
    ident = dt("ident", [128, 128], F32, kind="ExternalInput").ap()
    out = dt("out", [TPC, DM], F32, kind="ExternalOutput").ap()

    k1w = dt("k1w", [DM, DM], BF16, kind="Internal").ap()
    v1w = dt("v1w", [DM, DM], BF16, kind="Internal").ap()
    q1w = dt("q1w", [DM, DM], BF16, kind="Internal").ap()
    wq2t = dt("wq2t", [DM, DM], BF16, kind="Internal").ap()
    wk2t = dt("wk2t", [DM, DM], BF16, kind="Internal").ap()
    bbd = dt("bbd", [16, FF], BF16, kind="Internal").ap()
    wvwoad = dt("wvwoad", [DM, 256], BF16, kind="Internal").ap()
    V2d = dt("V2d", [NH, 128, NKT, 16], BF16, kind="Internal").ap()
    K1d = dt("K1d", [NH, 128, SEQ], BF16, kind="Internal").ap()
    V1d = dt("V1d", [NH, SEQ, 128], BF16, kind="Internal").ap()
    K2d = dt("K2d", [NH, 128, SEQ], BF16, kind="Internal").ap()
    Q1d = dt("Q1d", [NH, 128, TPC], BF16, kind="Internal").ap()
    Q2d = dt("Q2d", [NH, 128, TPC], BF16, kind="Internal").ap()

    dbg = {}
    if DEBUG_OUTS:
        for nm, shp in [("d_K1", [NH, 128, SEQ]), ("d_V1", [NH, SEQ, 128]),
                        ("d_K2", [NH, 128, SEQ]), ("d_Q1", [NH, 128, TPC]),
                        ("d_Q2", [NH, 128, TPC]), ("d_V2p", [NH, 128, NKT, 16]),
                        ("d_av1", [128, NH, TPC]), ("d_mod", [16, TPC]),
                        ("d_acc0", [TPC, DM]), ("d_mhat", [128, 32]),
                        ("d_hA", [16, TPC]), ("d_k1w", [DM, DM]), ("d_wq2t", [DM, DM])]:
            dbg[nm] = dt(nm, shp, F32 if nm in ("d_acc0", "d_mhat", "d_mod", "d_hA") else BF16,
                         kind="ExternalOutput").ap()

    sbA = nc.alloc_sbuf_tensor
    id_bf = sbA("id_bf", [128, 128], BF16)
    ones_r = sbA("ones_r", [1, 128], F32)
    ones_c = sbA("ones_c", [128, 1], BF16)
    kidx_sb = sbA("kidx_sb", [128, NKT], F32)
    a2_bf = sbA("a2_bf", [128, NT, 16], BF16)
    ln1b = sbA("ln1b", [128, DM], BF16)
    id_f32 = sbA("id_f32", [128, 128], F32)
    ln1c = sbA("ln1c", [128, 16], F32)
    ln2c = sbA("ln2c", [128, 16], F32)
    mhat = sbA("mhat", [128, 32], F32)
    nrm4 = sbA("nrm4", [1, 64], F32)
    ones32 = sbA("ones32", [128, 33], BF16)
    modpre = sbA("modpre", [16, TPC], F32)[:]
    msc = sbA("msc", [16, TPC], F32)[:]
    hAr = sbA("hAr", [16, TPC], F32)[:]
    idf16 = sbA("idf16", [16, 16], F32)
    stats = sbA("stats", [128, NPP, 16], F32)
    mrow = sbA("mrow", [1, 32], F32)
    cst = sbA("cst", [128, 4], F32)
    arX = sbA("arX", [128, NT * 2048], BF16)
    arB = sbA("arB", [128, 15360], BF16)
    arC = sbA("arC", [128, 16384], BF16)
    arD = sbA("arD", [128, 4096], BF16)
    arE = sbA("arE", [128, 2048], BF16)
    accT = [sbA(f"acc{i}", [128, DM], F32) for i in range(NSLOT)]

    pA = nc.alloc_psum_tensor("pA", [128, 1024], F32)
    pB = nc.alloc_psum_tensor("pB", [128, 1024], F32)
    pC = nc.alloc_psum_tensor("pC", [128, 1024], F32)
    pD = nc.alloc_psum_tensor("pD", [128, 1024], F32)

    arBf = arB[:].bitcast(F32)
    arCf = arC[:].bitcast(F32)
    arEf = arE[:].bitcast(F32)
    arXf = arX[:].bitcast(F32)
    qib = arXf[:, 1024:2048]
    pT2 = pD[:].bitcast(BF16)

    id_f = id_f32[:]
    b.dma("sync", id_f, ident, sem="di")
    b.dma("sync", idf16[:], ident[0:16, 0:16], sem="di")
    b.dma("sync", kidx_sb[:], kidx, sem="di")
    b.dma("sync", ln1c[:], ln1r.rearrange("o (c p) -> p (o c)", p=128), sem="di", slow=True)
    b.dma("sync", ln2c[:], ln2r.rearrange("o (c p) -> p (o c)", p=128), sem="di", slow=True)
    aa_sb = arCf[:, 0:NT * 16].rearrange("p (c r) -> p c r", r=16)
    b.dma("sync", aa_sb, aa.rearrange("(c p) r -> p c r", p=128), sem="di")
    qi_row = arXf[0:1, 0:TPC]
    b.dma("sync", qi_row, qidx, sem="di")
    l1_row = arBf[0:1, 0:DM]
    b.dma("sync", l1_row, ln1r, sem="di")
    dl0 = b.n("di")
    b.op("vector", lambda E: E.tensor_copy(id_bf[:], id_f), waits=[("di", dl0)])
    b.op("vector", lambda E: E.memset(ones_r[:], 1.0))
    b.op("vector", lambda E: E.memset(ones_c[:], 1.0))
    b.op("vector", lambda E: E.memset(cst[:, 0:1], EPS_LN))
    b.op("vector", lambda E: E.memset(cst[:, 1:2], EPS_RMS))
    b.op("vector", lambda E: E.memset(cst[:, 2:3], 1.0))
    svz = b.op("vector", lambda E: E.memset(ones32[:], 0.0), inc="sv")
    b.op("vector", lambda E: E.memset(ones32[:, 32:33], 1.0), waits=[("sv", svz)])
    sv0 = b.op("vector", None, inc="sv")
    b.op("tensor", lambda E: E.matmul(pD[:, 0:512], ones_r[:], qi_row[0:1, 0:512], start=True, stop=True),
         waits=[("sv", sv0)])
    b.op("tensor", lambda E: E.matmul(pD[:, 512:1024], ones_r[:], qi_row[0:1, 512:1024], start=True, stop=True))
    for _h in range(2):
        b.op("tensor", lambda E, _h=_h: E.matmul(pA[:, 512 * _h:512 * (_h + 1)], ones_r[:],
                                                 l1_row[0:1, 512 * _h:512 * (_h + 1)], start=True, stop=True))
    for _h in range(2):
        b.op("tensor", lambda E, _h=_h: E.matmul(pB[:, 512 * _h:512 * (_h + 1)], ones_r[:],
                                                 l1_row[0:1, 1024 + 512 * _h:1024 + 512 * (_h + 1)], start=True, stop=True))
    st1b = b.op("tensor", None, inc="st")
    sv_q = b.op("vector", lambda E: E.tensor_copy(qib[:], pD[:, 0:TPC]), waits=[("st", st1b)], inc="sv")
    b.op("vector", lambda E: E.tensor_copy(ln1b[:, 0:1024], pA[:]))
    svl = b.op("vector", lambda E: E.tensor_copy(ln1b[:, 1024:2048], pB[:]), inc="sv")
    mask_sb = arD[:].bitcast(FP8).rearrange("p (k q) -> p k q", q=128)
    for k in range(NKT):
        j = k // 8
        b.op("vector", lambda E, k=k, j=j: E.tensor_scalar(
            mask_sb[:, k, :], qib[:, j * 128:(j + 1) * 128], kidx_sb[:, k:k + 1], None, ALU.is_ge),
            waits=[("sv", sv_q)] if k == 0 else ())
    svm = b.op("vector", None, inc="sv")
    for i in range(NT):
        b.op("vector", lambda E, i=i: E.tensor_scalar(
            a2_bf[:, i, :], aa_sb[:, i, :], ln2c[:, i:i + 1], None, ALU.mult))
    b.op("vector", None, inc="sv")
    bstage_f = arBf[0:16, 0:1024]
    bstage_b = arB[:][0:16, 2048:3072]
    for q in range(8):
        dq = b.dma("sync", bstage_f, bbi[:, q * 1024:(q + 1) * 1024],
                   waits=[("sv", b.n("sv"))], sem="db")
        svb = b.op("vector", lambda E: E.tensor_copy(bstage_b, bstage_f),
                   waits=[("db", dq), ("dbo", b.n("dbo"))], inc="sv")
        b.dma("scalar", bbd[:, q * 1024:(q + 1) * 1024], bstage_b, waits=[("sv", svb)], sem="dbo")
    woa_f = arCf[:, NT * 16:2 * NT * 16].rearrange("p (c r) -> p c r", r=16)
    wsl = arBf[:, 2048:6144].rearrange("p (u d) -> p u d", u=2)
    pwoa = pC[:, 0:256].rearrange("p (c r) -> p c r", r=16)
    for i in range(NT):
        dli = b.dma("sync", wsl[:, i % 2, :], wo2[128 * i:128 * (i + 1), :],
                    waits=[("st", b.n("st") - 1)] if i >= 2 else [("sv", svl)],
                    sem="dw%d" % (i % 2))
        w0 = [("dw%d" % (i % 2), dli), ("sv", svm)]
        for j in range(NT):
            b.op("tensor", lambda E, i=i, j=j: E.matmul(
                pwoa[:, j, :], wsl[:, i % 2, 128 * j:128 * (j + 1)], aa_sb[:, i, :],
                start=(i == 0 and j == 0), stop=(i == NT - 1 and j == NT - 1)),
                waits=w0 if j == 0 else ())
        b.op("tensor", None, inc="st")
    stw = b.n("st")
    svw = b.op("vector", lambda E: E.tensor_copy(woa_f[:], pwoa[:]), waits=[("st", stw)], inc="sv")
    wvstage = arC[:][:, 7168:11264].rearrange("p (c x) -> p c x", x=256)
    for i in range(NT):
        dli = b.dma("sync", wsl[:, i % 2, :], wv2[128 * i:128 * (i + 1), :],
                    waits=[("st", b.n("st") - 1)] if i >= 2 else [("st", b.n("st"))],
                    sem="dw%d" % (i % 2))
        svf = b.op("vector", lambda E, i=i: E.tensor_tensor(
            wsl[:, i % 2, :], wsl[:, i % 2, :], ln1b[:], ALU.mult),
            waits=[("dw%d" % (i % 2), dli), ("sv", svl)], inc="sv")
        w0 = [("sv", svf), ("sv", svw), ("sv", b.n("sv"))]
        for j in range(NT):
            b.op("tensor", lambda E, i=i, j=j: E.matmul(
                pwoa[:, j, :], wsl[:, i % 2, 128 * j:128 * (j + 1)], woa_f[:, i, :],
                start=True, stop=True),
                waits=w0 if j == 0 else ())
        stt = b.op("tensor", None, inc="st")
        b.op("vector", lambda E, i=i: E.tensor_copy(wvstage[:, :, 16 * i:16 * (i + 1)], pwoa[:]),
             waits=[("st", stt)], inc="sv")
    svw2 = b.op("vector", None, inc="sv")
    b.dma("scalar", wvwoad.rearrange("(c p) x -> p c x", p=128), wvstage[:],
          waits=[("sv", svw2)], sem="dwv")

    cstage = arC[:][:, 2048:6144].rearrange("p (u d) -> p u d", u=2)
    it = 0
    for i in range(NT):
        for part, (wsrc, wdst) in enumerate([
                (wqkv[128 * i:128 * (i + 1), 0:DM], q1w[128 * i:128 * (i + 1), :]),
                (wqkv[128 * i:128 * (i + 1), DM:2 * DM], k1w[128 * i:128 * (i + 1), :]),
                (wqkv[128 * i:128 * (i + 1), 2 * DM:3 * DM], v1w[128 * i:128 * (i + 1), :])]):
            u = it % 2
            dli = b.dma("sync", wsl[:, u, :], wsrc,
                        waits=[("sv", b.n("sv") - 1)] if it >= 2 else [("sv", svw2)],
                        sem="dw%d" % u)
            svc = b.op("vector", lambda E, u=u: E.tensor_copy(cstage[:, u, :], wsl[:, u, :]),
                       waits=[("dw%d" % u, dli), ("dc%d" % u, b.n("dc%d" % u))],
                       inc="sv")
            b.dma("scalar", wdst, cstage[:, u, :], waits=[("sv", svc)], sem="dc%d" % u)
            it += 1
    tstage = arC[:][:, 6144:7168].rearrange("p (u d) -> p u d", u=2)
    for wsrc, wdst in [(wq2, wq2t), (wk2, wk2t)]:
        for i in range(NT):
            u = i % 2
            dli = b.dma("sync", wsl[:, u, :], wsrc[128 * i:128 * (i + 1), :],
                        waits=[("st", b.n("st") - 1)] if i >= 2 else [("sv", b.n("sv"))],
                        sem="dw%d" % u)
            svc = b.op("vector", lambda E, u=u: E.tensor_tensor(
                cstage[:, u, :], wsl[:, u, :], ln1b[:], ALU.mult),
                waits=[("dw%d" % u, dli), ("dc%d" % u, b.n("dc%d" % u))], inc="sv")
            for g in range(4):
                for jj in range(4):
                    j = 4 * g + jj
                    ws = []
                    if jj == 0:
                        ws = [("sv", max(svc, b.n("sv")))]
                    b.op("tensor", lambda E, u=u, j=j, jj=jj, g=g: E.transpose(
                        pT2[:, (g % 2) * 1024 + jj * 128:(g % 2) * 1024 + (jj + 1) * 128],
                        cstage[:, u, 128 * j:128 * (j + 1)], id_bf[:]),
                        waits=ws)
                stt = b.op("tensor", None, inc="st")
                uu = g % 2
                sve = b.op("vector", lambda E, uu=uu: E.tensor_copy(
                    tstage[:, uu, :], pT2[:, uu * 1024:uu * 1024 + 512]),
                    waits=[("st", stt), ("dt%d" % uu, b.n("dt%d" % uu))], inc="sv")
                b.dma("scalar",
                      wdst[512 * g:512 * (g + 1), 128 * i:128 * (i + 1)].rearrange(
                          "(jj p) o -> p jj o", p=128),
                      tstage[:, uu, :].rearrange("p (jj o) -> p jj o", o=128),
                      waits=[("sv", sve)], sem="dt%d" % uu)
    def early_out():
        if BUILD_PHASES in ("p0", "p1a1", "p1a"):
            for tt in range(NSLOT):
                b.op("vector", lambda E, tt=tt: E.memset(accT[tt][:], 0.0))
            b.op("vector", None, inc="sv")
        svq = b.n("sv")
        for tt in range(NSLOT):
            b.dma("sync", out[128 * tt:128 * (tt + 1), :], accT[tt][:],
                  waits=[("sv", svq)] if tt == 0 else (), sem="so")
        b.op("sync", None, waits=[("so", b.n("so"))])
        return _finish(nc, b)

    b.barrier()
    if BUILD_PHASES == "p0":
        return early_out()

    arXv = arX[:].rearrange("p (c t) -> p c t", t=2048)
    xnT = arXv
    xdb = arCf[:, 0:6144].rearrange("p (u d) -> p u d", u=3)
    v2stage = arC[:][:, 12288:16384].rearrange("p (n x) -> p n x", x=256)
    xsc = arCf[:, 4096:6144]
    sqf = arCf[:, 6144:8192]
    xnbu = {0: arE[:][:, 0:2048], 1: accT[7][:].bitcast(BF16)[:, 0:2048]}
    r1row = arEf[0:1, 0:1024]
    r1bc = ln1b[:]
    kwsl = arB[:][:, 0:6144].rearrange("p (u c o) -> p u c o", u=3, c=NT)
    vwsl = arB[:][:, 6144:14336].rearrange("p (u c o) -> p u c o", u=2, c=NT)
    estg = arB[:][:, 14336:15360].rearrange("p (u d) -> p u d", u=2)

    def stat_chain(tt, sv_sum):
        s = lambda i: stats[:, tt, i:i + 1]
        steps = [
            lambda E: E.activation(s(2), s(0), AF.Copy, scale=1.0 / DM),
            lambda E: E.activation(s(3), s(2), AF.Square),
            lambda E: E.activation(s(4), s(1), AF.Copy, scale=1.0 / DM),
            lambda E: E.activation(s(5), s(3), AF.Relu, scale=-1.0, bias=s(4)),
            lambda E: E.activation(s(6), s(5), AF.Ln, bias=cst[:, 0:1]),
            lambda E: E.activation(s(7), s(6), AF.Exp, scale=-0.5),
            lambda E: E.activation(s(8), s(6), AF.Exp, scale=-1.0),
            lambda E: E.activation(s(9), s(8), AF.Relu, scale=-EPS_LN, bias=cst[:, 2:3]),
            lambda E: E.activation(s(10), s(9), AF.Ln, bias=cst[:, 1:2]),
            lambda E: E.activation(s(13), s(10), AF.Exp, scale=-0.5),
        ]
        last = 0
        for i, fn in enumerate(steps):
            w = [("sv", sv_sum)] if i == 0 else [("sa", last)]
            last = b.op("scalar", fn, waits=w, inc="sa")
        return last

    def mk_lnstate(src, ntiles, own, guard_st):
        return dict(src=src, n=ntiles, own=own, guard=guard_st,
                    nslots=2 if own else 3,
                    ptev={0: b.n("sv"), 1: b.n("sv")},
                    xev={u: [("sv", b.n("sv")), ("sa", b.n("sa"))]
                         for u in range(2 if own else 3)},
                    tsv={}, bmark={}, sacd={}, a1n=0)

    def stage_a1(st, tt, extra=()):
        u = tt % st["nslots"]
        own = st["own"]
        dl_x = b.dma(QU[tt % 2], xdb[:, u, :], st["src"][128 * tt:128 * (tt + 1), :],
                     waits=list(st["xev"][u]) + list(extra), sem="dx%d" % u)
        sv_sum = b.op("vector", lambda E, u=u, tt=tt: E.tensor_reduce(
            stats[:, tt, 0:1], xdb[:, u, :], mybir.AxisListType.X, ALU.add),
            waits=[("dx%d" % u, dl_x)], inc="sv")
        sqw = [("dx%d" % u, dl_x)]
        if own and tt < 2:
            sqw.append(("dvo", b.n("dvo")))
        b.op("scalar", lambda E, u=u, tt=tt, own=own: E.activation(
            sqf if own else accT[tt % 4][:], xdb[:, u, :], AF.Square,
            accum_out=stats[:, tt, 1:2]),
            waits=sqw)
        st["sacd"][tt] = stat_chain(tt, sv_sum)
        st["a1n"] = tt + 1

    def stage_a2(st, tt):
        u = tt % st["nslots"]
        own = st["own"]
        sa_c = st["sacd"][tt]
        s = lambda i, tt=tt: stats[:, tt, i:i + 1]
        if own:
            sv_xn = b.op("vector", lambda E, u=u, m=s(2), r=s(7): E.tensor_scalar(
                xsc, xdb[:, u, :], m, r, ALU.subtract, ALU.mult),
                waits=[("sa", sa_c), ("st", st["guard"])], inc="sv")
            wb = [("sv", sv_xn),
                  ("st", st["guard"] if tt == 0 else st["bmark"][tt - 1])]
            sv_xnb = b.op("vector", lambda E, xb=xnbu[0]: E.tensor_copy(xb, xsc),
                          waits=wb, inc="sv")
            sv_acc = b.op("vector", lambda E, u=u, tt=tt: E.tensor_tensor(
                accT[tt][:], xdb[:, u, :], xsc, ALU.add), waits=[("sv", sv_xnb)], inc="sv")
            st["xev"][u] = [("sv", sv_acc), ("sa", sa_c)]
            st["tsv"][tt] = sv_xnb
        else:
            xb = xnbu[tt % 2]
            wb = [("sa", sa_c),
                  ("st", st["guard"] if tt < 2 else st["bmark"][tt - 2])]
            sv_xnb = b.op("vector", lambda E, u=u, xb=xb, m=s(2), r=s(7): E.tensor_scalar(
                xb, xdb[:, u, :], m, r, ALU.subtract, ALU.mult),
                waits=wb, inc="sv")
            st["xev"][u] = [("sv", sv_xnb), ("sa", sa_c)]
            st["tsv"][tt] = sv_xnb

    def stage_b(st, tt):
        xb = xnbu[0] if st["own"] else xnbu[tt % 2]
        svr = st["tsv"][tt]
        ptev = st["ptev"]
        stt = 0
        for g in range(4):
            for jj in range(4):
                j = 4 * g + jj
                ws = []
                if jj == 0:
                    ws = [("sv", max(svr, ptev[g % 2]))]
                b.op("tensor", lambda E, j=j, jj=jj, g=g, xb=xb: E.transpose(
                    pT2[:, (g % 2) * 1024 + jj * 128:(g % 2) * 1024 + jj * 128 + 128],
                    xb[:, 128 * j:128 * (j + 1)], id_bf[:]),
                    waits=ws)
            stt = b.op("tensor", None, inc="st")
            ptev[g % 2] = b.op("vector", lambda E, g=g, tt=tt: E.tensor_copy(
                xnT[:, 4 * g:4 * g + 4, 128 * tt:128 * (tt + 1)],
                pT2[:, (g % 2) * 1024:(g % 2) * 1024 + 512].rearrange("p (c t) -> p c t", t=128)),
                waits=[("st", stt)], inc="sv")
        st["bmark"][tt] = stt

    def finish_pass(st):
        ntiles = st["n"]
        while st["a1n"] < min(st["nslots"], ntiles):
            stage_a1(st, st["a1n"])
        for tt in range(ntiles):
            stage_a2(st, tt)
            stage_b(st, tt)
            if tt + st["nslots"] < ntiles:
                stage_a1(st, tt + st["nslots"])
        ptev = st["ptev"]
        last_sa = st["sacd"][ntiles - 1]
        rc_mark = 0
        bc_mark = 0
        for rnd in range((ntiles + 7) // 8):
            ws0 = [("sa", last_sa), ("sv", max(ptev[0], ptev[1]))]
            if rnd > 0:
                ws0.append(("sv", rc_mark))
            for j in range(8):
                tt = 8 * rnd + j
                b.op("tensor", lambda E, tt=tt, j=j: E.transpose(
                    pD[0:1, 128 * j:128 * (j + 1)], stats[:, tt, 13:14], id_f32[:]),
                    waits=ws0 if j == 0 else ())
            sttr = b.op("tensor", None, inc="st")
            sv_row = b.op("vector", lambda E: E.tensor_copy(r1row, pD[0:1, 0:1024]),
                          waits=[("st", sttr)], inc="sv")
            for q in range(2):
                b.op("tensor", lambda E, q=q: E.matmul(
                    pD[:, 512 * q:512 * (q + 1)], ones_r[:],
                    r1row[0:1, 512 * q:512 * (q + 1)], start=True, stop=True),
                    waits=[("sv", sv_row)] if q == 0 else ())
            bc_mark = b.op("tensor", None, inc="st")
            rc_mark = b.op("vector", lambda E, rnd=rnd: E.tensor_copy(
                r1bc[:, 1024 * rnd:1024 * (rnd + 1)], pD[:, 0:1024]),
                waits=[("st", bc_mark)], inc="sv")
        return b.n("sv"), rc_mark, bc_mark

    def proj_fm(wsrc_bf, dst, sv_ready, col0, ntok, rescale_sv, prev_marks):
        ev = {0: b.n("sv"), 1: b.n("sv")}
        ntch = ntok // 512
        base_st = b.n("st")
        marks = []
        for oc in range(NT):
            u = oc % 3
            if oc >= 3:
                wld = [("st", marks[oc - 3])]
            elif prev_marks is not None:
                wld = [("st", prev_marks[u])]
            else:
                wld = [("st", base_st)]
            dli = b.dma(QU[oc % 2], kwsl[:, u, :, :],
                        wsrc_bf[:, 128 * oc:128 * (oc + 1)].rearrange("(c p) o -> p c o", p=128),
                        waits=wld, sem="dk%d" % u)
            stt = 0
            for tch in range(ntch):
                pu = tch % 2
                pdst = (pA if pu == 0 else pB)[:, 0:512]
                w0 = [("dk%d" % u, dli), ("sv", sv_ready), ("sv", ev[pu])]
                for c in range(NT):
                    b.op("tensor", lambda E, u=u, c=c, tch=tch, pdst=pdst: E.matmul(
                        pdst, kwsl[:, u, c, :], xnT[:, c, 512 * tch:512 * (tch + 1)],
                        start=(c == 0), stop=(c == NT - 1)),
                        waits=w0 if c == 0 else ())
                stt = b.op("tensor", None, inc="st")
                eu = tch % 2
                if rescale_sv is None:
                    sve = b.op("vector", lambda E, eu=eu, pdst=pdst: E.tensor_copy(
                        estg[:, eu, :], pdst),
                        waits=[("st", stt), ("de%d" % eu, b.n("de%d" % eu))], inc="sv")
                else:
                    sve = b.op("vector", lambda E, eu=eu, pdst=pdst, tch=tch: E.tensor_tensor(
                        estg[:, eu, :], pdst, r1bc[:, 512 * tch:512 * (tch + 1)], ALU.mult),
                        waits=[("st", stt), ("de%d" % eu, b.n("de%d" % eu)),
                               ("sv", rescale_sv)], inc="sv")
                ev[pu] = sve
                b.dma(QO[eu], dst[oc][:, col0 + 512 * tch:col0 + 512 * (tch + 1)],
                      estg[:, eu, :], waits=[("sv", sve)], sem="de%d" % eu)
            marks.append(stt)
        return marks

    K1l = [K1d[h] for h in range(NH)]
    K2l = [K2d[h] for h in range(NH)]
    pmarks = None
    vprev = None
    st_cur = mk_lnstate(xf[0:PTOK, :], NPP, False, b.n("st"))
    for p in range(NPASS):
        sv_ready, sv_r1, guard = finish_pass(st_cur)
        mkK1 = proj_fm(k1w, K1l, sv_ready, PTOK * p, PTOK, None, pmarks)
        ev = {0: b.n("sv"), 1: b.n("sv")}
        vmk = []
        for avs in range(8):
            u = avs % 2
            if avs >= 2:
                wld = [("st", vmk[avs - 2])]
            elif vprev is not None:
                wld = [("st", vprev[6 + avs])]
            else:
                wld = [("st", b.n("st"))]
            dli = b.dma(QU[u], vwsl[:, u, :, :],
                        v1w[:, 256 * avs:256 * (avs + 1)].rearrange("(c p) o -> p c o", p=128),
                        waits=wld, sem="dv%d" % u)
            stt = 0
            for tt in range(NPP):
                pu = tt % 2
                pdst = (pA if pu == 0 else pB)[:, 512:768]
                w0 = [("dv%d" % u, dli), ("sv", sv_ready), ("sv", ev[pu])]
                for c in range(NT):
                    b.op("tensor", lambda E, u=u, c=c, tt=tt, pdst=pdst: E.matmul(
                        pdst, xnT[:, c, 128 * tt:128 * (tt + 1)], vwsl[:, u, c, :],
                        start=(c == 0), stop=(c == NT - 1)),
                        waits=w0 if c == 0 else ())
                stt = b.op("tensor", None, inc="st")
                eu = tt % 2
                sve = b.op("vector", lambda E, eu=eu, pdst=pdst: E.tensor_copy(
                    estg[:, eu, 0:256], pdst),
                    waits=[("st", stt), ("de%d" % eu, b.n("de%d" % eu))], inc="sv")
                ev[pu] = sve
                h0 = avs * 2
                b.dma(QO[eu],
                      V1d[h0:h0 + 2, PTOK * p + 128 * tt:PTOK * p + 128 * (tt + 1), :].rearrange(
                          "h t d -> t h d"),
                      estg[:, eu, 0:256].rearrange("p (h d) -> p h d", h=2),
                      waits=[("sv", sve)], sem="de%d" % eu)
            vmk.append(stt)
        vprev = vmk
        mkK2 = proj_fm(wk2t, K2l, sv_ready, PTOK * p, PTOK, sv_r1, None)
        if p + 1 < NPASS:
            st_next = mk_lnstate(xf[PTOK * (p + 1):PTOK * (p + 2), :], NPP, False, guard)
        else:
            st_next = mk_lnstate(xo, NSLOT, True, guard)
        wvsl = kwsl[:].rearrange("p u c o -> p (u c o)")[:, 0:NT * 256].rearrange(
            "p (c x) -> p c x", x=256)
        dwv = b.dma("sync", wvsl, wvwoad.rearrange("(c p) x -> p c x", p=128),
                    waits=[("st", b.n("st")), ("dwv", b.n("dwv"))], sem="dwv")
        pv2h = {0: pC[:, 0:256], 1: pC[:, 512:768]}
        ev = {0: b.n("sv"), 1: b.n("sv")}
        v2mk = 0
        for tt in range(NPP):
            pu = tt % 2
            pv2 = pv2h[pu]
            w0 = [("sv", sv_ready), ("sv", ev[pu]), ("st", b.n("st")), ("dwv", dwv)] if tt <= 1 else [("sv", ev[pu])]
            for c in range(NT):
                b.op("tensor", lambda E, c=c, tt=tt, pv2=pv2: E.matmul(
                    pv2, xnT[:, c, 128 * tt:128 * (tt + 1)], wvsl[:, c, :],
                    start=(c == 0), stop=(c == NT - 1)),
                    waits=w0 if c == 0 else ())
            v2mk = b.op("tensor", None, inc="st")
            ev[pu] = b.op("vector", lambda E, tt=tt, pv2=pv2: E.tensor_scalar(
                v2stage[:, tt, :], pv2, stats[:, tt, 13:14], None, ALU.mult),
                waits=[("st", v2mk), ("dvo", b.n("dvo"))], inc="sv")
            if not st_next["own"] and tt < 3:
                stage_a1(st_next, tt, extra=[("sv", ev[pu])])
        svv = b.n("sv")
        for h in range(NH):
            b.dma("scalar", V2d[h, :, NPP * p:NPP * (p + 1), :], v2stage[:, :, 16 * h:16 * (h + 1)],
                  waits=[("sv", svv)] if h == 0 else (), sem="dvo")
        if st_next["own"]:
            for tt in range(2):
                stage_a1(st_next, tt, extra=[("sv", b.n("sv"))])
        pmarks = {0: v2mk, 1: v2mk, 2: mkK2[14]}
        st_cur = st_next
        if BUILD_PHASES == "p1a1":
            b.barrier()
            return early_out()

    b.barrier()
    if BUILD_PHASES == "p1a":
        return early_out()

    sv_ready, sv_r1, guard = finish_pass(st_cur)
    Q1l = [Q1d[h] for h in range(NH)]
    Q2l = [Q2d[h] for h in range(NH)]
    mkQ1 = proj_fm(q1w, Q1l, sv_ready, 0, TPC, None, pmarks)
    proj_fm(wq2t, Q2l, sv_ready, 0, TPC, sv_r1, None)
    ph = pC[0:16, 0:1024]
    for tch in range(2):
        w0 = [("sv", sv_ready), ("st", b.n("st"))]
        for c in range(NT):
            b.op("tensor", lambda E, c=c, tch=tch: E.matmul(
                ph[:, 512 * tch:512 * (tch + 1)], a2_bf[:, c, :],
                xnT[:, c, 512 * tch:512 * (tch + 1)],
                start=(c == 0), stop=(c == NT - 1)),
                waits=w0 if c == 0 else ())
    stt = b.op("tensor", None, inc="st")
    b.op("vector", lambda E: E.tensor_copy(hAr, ph), waits=[("st", stt)], inc="sv")

    smpu = {0: arC[:][:, 0:512], 1: arC[:][:, 512:1024]}
    ssqu = {0: arC[:][:, 1024:1536], 1: arC[:][:, 1536:2048]}
    pru = {0: pD[0:1, 0:512], 1: pD[0:1, 512:1024]}
    sqm = {0: b.n("sa"), 1: b.n("sa")}
    mmm = {0: b.n("st"), 1: b.n("st")}
    rdm = {0: b.n("sv"), 1: b.n("sv")}
    base_w = [("sv", b.n("sv")), ("sa", b.n("sa")), ("st", b.n("st")),
              ("de0", b.n("de0")), ("de1", b.n("de1"))]
    i = 0
    for idx, srcl in enumerate([K1l, K2l, Q1l, Q2l]):
        for h in range(NH):
            u = i % 2
            dls = b.dma("sync", smpu[u], srcl[h][:, 0:512],
                        waits=(base_w if i < 2 else [("sa", sqm[u])]), sem="dsm%d" % u)
            sas = b.op("scalar", lambda E, u=u: E.activation(ssqu[u], smpu[u], AF.Square),
                       waits=[("dsm%d" % u, dls), ("st", mmm[u])], inc="sa")
            sqm[u] = sas
            stt = b.op("tensor", lambda E, u=u: E.matmul(pru[u], ones_c[:], ssqu[u],
                                                         start=True, stop=True),
                       waits=[("sa", sas), ("sv", rdm[u])], inc="st")
            mmm[u] = stt
            rdm[u] = b.op("vector", lambda E, idx=idx, h=h, u=u: E.tensor_reduce(
                nrm4[0:1, idx * 16 + h:idx * 16 + h + 1], pru[u], mybir.AxisListType.X, ALU.max),
                waits=[("st", stt)], inc="sv")
            i += 1
    svn = b.n("sv")
    t1 = arEf[0:1, 0:32]
    b.op("gpsimd", lambda E: E.tensor_tensor(t1[0:1, 0:16], nrm4[0:1, 32:48], nrm4[0:1, 0:16], ALU.mult),
         waits=[("sv", svn)])
    sg2 = b.op("gpsimd", lambda E: E.tensor_tensor(t1[0:1, 16:32], nrm4[0:1, 48:64], nrm4[0:1, 16:32], ALU.mult),
               inc="sg")
    sa1 = b.op("scalar", lambda E: E.activation(mrow[0:1, :], t1, AF.Ln), waits=[("sg", sg2)], inc="sa")
    sa2 = b.op("scalar", lambda E: E.activation(mrow[0:1, :], mrow[0:1, :], AF.Exp, scale=0.5),
               waits=[("sa", sa1)], inc="sa")
    sa3 = b.op("scalar", lambda E: E.mul(mrow[0:1, :], mrow[0:1, :], -1.25 * C), waits=[("sa", sa2)], inc="sa")
    sv_cl = b.op("vector", lambda E: E.tensor_scalar_max(mrow[0:1, :], mrow[0:1, :], -30.0),
                 waits=[("sa", sa3)], inc="sv")
    stt = b.op("tensor", lambda E: E.matmul(pD[:, 0:32], ones_r[:], mrow[0:1, :], start=True, stop=True),
               waits=[("sv", sv_cl)], inc="st")
    b.op("vector", lambda E: E.tensor_copy(mhat[:], pD[:, 0:32]), waits=[("st", stt)], inc="sv")

    if DEBUG_OUTS:
        sv_all = b.n("sv")
        b.dma("scalar", dbg["d_V2p"], V2d, waits=[("sv", sv_all)], sem="so")
        b.dma("scalar", dbg["d_k1w"], k1w, sem="so")
        b.dma("scalar", dbg["d_wq2t"], wq2t, sem="so")
        b.dma("sync", dbg["d_mhat"], mhat[:], waits=[("sv", sv_all)], sem="so")
        b.dma("sync", dbg["d_hA"], hAr[:], waits=[("sv", sv_all)], sem="so")
        for tt in range(NSLOT):
            b.dma("sync", dbg["d_acc0"][128 * tt:128 * (tt + 1), :], accT[tt][:],
                  waits=[("sv", sv_all)], sem="so")
        for h in range(NH):
            b.dma("scalar", dbg["d_K1"][h], K1d[h], sem="so")
            b.dma("scalar", dbg["d_V1"][h], V1d[h], sem="so")
            b.dma("scalar", dbg["d_K2"][h], K2d[h], sem="so")
            b.dma("scalar", dbg["d_Q1"][h], Q1d[h], sem="so")
            b.dma("scalar", dbg["d_Q2"][h], Q2d[h], sem="so")

    b.barrier()

    if BUILD_PHASES == "p1":
        return early_out()

    ksb1 = arB[:][:, 0:8192]
    vsb = arC[:][:, 0:8192].rearrange("p (n d) -> p n d", d=128)
    qsb = arC[:][:, 8192:10240].rearrange("p (u t) -> p u t", u=2)
    ptb = arC[:][:, 10240:12288].rearrange("p (u t) -> p u t", u=2)
    rcb = arCf[:, 6144:7168]
    sumr = arEf[0:1, 0:1024]
    lnr = arEf[32:33, 0:1024]

    def attention(attn_i, Kl, Ql, av_out_fn, v_load_half, v_lhsT_fn, av_rows):
        two_mm = (av_rows == 128)
        head_s_fin = {}
        st0 = b.n("st")
        dkA = {0: b.dma("sync", ksb1[:, 0:4096], Kl[0][:, 0:4096], waits=[("st", st0)], sem="dKa")}
        dkB = {0: b.dma("sync", ksb1[:, 4096:8192], Kl[0][:, 4096:8192], sem="dKb")}
        dvA = {0: v_load_half(0, 0, st0)}
        dvB = {0: v_load_half(0, 1, st0)}
        for h in range(NH):
            u = h % 2
            wq_ld = [("st", head_s_fin[h - 2])] if h >= 2 else [("st", st0)]
            dq = b.dma("sync", qsb[:, u, :], Ql[h], waits=wq_ld, sem="dq%d" % u)
            bias = mhat[:, attn_i * 16 + h:attn_i * 16 + h + 1]
            sa_exp = {}
            st_av = {}
            prev_done_sv = b.n("sv")
            sa_head0 = b.n("sa")
            def emit_av(k):
                jmin = k // 8
                qoff = 128 * jmin
                pu = k % 2
                sg_m = sg_mask[k]
                pts = [qoff, 512, TPC] if qoff < 512 else [qoff, TPC]
                avch = [(a0, a1) for a0, a1 in zip(pts[:-1], pts[1:]) if a1 > a0]
                wa0 = [("sg", sg_m)]
                if k == 0:
                    wa0.append(("sv", prev_done_sv))
                    wa0.append(("dVa", dvA[h]))
                if k == 32:
                    wa0.append(("dVb", dvB[h]))
                for ci, (a0, a1) in enumerate(avch):
                    stopf = (a0 == qoff and k in (31, 63))
                    startf = (k == 0 and a0 in (0, 512))
                    wa = wa0 if ci == 0 else [("sa", sa_exp[k])]
                    if two_mm:
                        b.op("tensor", lambda E, k=k, a0=a0, a1=a1, pu=pu, qoff=qoff, stopf=stopf, startf=startf: E.matmul(
                            pC[0:av_rows, a0:a1], v_lhsT_fn(k), ptb[:, pu, a0 - qoff:a1 - qoff],
                            start=startf, stop=stopf),
                            waits=wa)
                        b.op("tensor", lambda E, k=k, a0=a0, a1=a1, pu=pu, qoff=qoff, stopf=stopf, startf=startf: E.matmul(
                            pD[0:1, a0:a1], ones_c[:], ptb[:, pu, a0 - qoff:a1 - qoff],
                            start=startf, stop=stopf))
                    else:
                        sums_fn = lambda E, k=k, a0=a0, a1=a1, pu=pu, qoff=qoff, stopf=stopf, startf=startf: E.matmul(
                            pC[0:33, a0:a1], ones32[:], ptb[:, pu, a0 - qoff:a1 - qoff],
                            start=startf, stop=stopf)
                        av_fn = lambda E, k=k, a0=a0, a1=a1, pu=pu, qoff=qoff: E.matmul(
                            pC[0:av_rows, a0:a1], v_lhsT_fn(k), ptb[:, pu, a0 - qoff:a1 - qoff],
                            start=False, stop=False)
                        if k == 0:
                            b.op("tensor", sums_fn, waits=wa)
                            b.op("tensor", av_fn)
                        else:
                            b.op("tensor", av_fn, waits=wa)
                            b.op("tensor", sums_fn)
                st_av[k] = b.op("tensor", None, inc="st")

            sg_mask = {}
            for k in range(NKT):
                jmin = k // 8
                qoff = 128 * jmin
                nq = TPC - qoff
                pu = k % 2
                ps_k = (pA if pu == 0 else pB)
                w0 = [("dq%d" % u, dq)]
                if k == 0:
                    w0.append(("dKa", dkA[h]))
                elif k == 32:
                    w0.append(("dKb", dkB[h]))
                if k >= 2:
                    w0.append(("sa", sa_exp[k - 2]))
                else:
                    w0.append(("sa", sa_head0))
                    w0.append(("sv", prev_done_sv))
                nch = (nq + 511) // 512
                for ch in range(nch):
                    c0 = 512 * ch
                    c1 = min(nq, c0 + 512)
                    b.op("tensor", lambda E, u=u, k=k, c0=c0, c1=c1, ps_k=ps_k, qoff=qoff: E.matmul(
                        ps_k[:, c0:c1], ksb1[:, 128 * k:128 * (k + 1)],
                        qsb[:, u, qoff + c0:qoff + c1], start=True, stop=True),
                        waits=w0 if ch == 0 else ())
                st_s = b.op("tensor", None, inc="st")
                if k == 31:
                    st_s31 = st_s
                if k == NKT - 1:
                    head_s_fin[h] = st_s
                we = [("st", st_s)]
                if k >= 2:
                    we.append(("st", st_av[k - 2]))
                sa_exp[k] = b.op("scalar", lambda E, pu=pu, nq=nq, ps_k=ps_k, bias=bias: E.activation(
                    ptb[:, pu, 0:nq], ps_k[:, 0:nq], AF.Exp, bias=bias, scale=C),
                    waits=we, inc="sa")
                sg_mask[k] = b.op("gpsimd", lambda E, pu=pu, k=k: E.tensor_tensor(
                    ptb[:, pu, 0:128], ptb[:, pu, 0:128], mask_sb[:, k, :], ALU.mult),
                    waits=[("sa", sa_exp[k])], inc="sg")
                if k >= 1:
                    emit_av(k - 1)
                if k == 32 and h + 1 < NH:
                    dkA[h + 1] = b.dma("sync", ksb1[:, 0:4096], Kl[h + 1][:, 0:4096],
                                       waits=[("st", st_s31)], sem="dKa")
                    dvA[h + 1] = v_load_half(h + 1, 0, st_av[31])
            emit_av(NKT - 1)
            if h + 1 < NH:
                dkB[h + 1] = b.dma("sync", ksb1[:, 4096:8192], Kl[h + 1][:, 4096:8192],
                                   waits=[("st", head_s_fin[h])], sem="dKb")
                dvB[h + 1] = v_load_half(h + 1, 1, st_av[NKT - 1])
            sum_src = pD[0:1, 0:1024] if two_mm else pC[32:33, 0:1024]
            sv_s = b.op("vector", lambda E, sum_src=sum_src: E.tensor_copy(sumr, sum_src),
                        waits=[("st", st_av[NKT - 1])], inc="sv")
            sa_l = b.op("scalar", lambda E: E.activation(lnr, sumr, AF.Ln),
                        waits=[("sv", sv_s)], inc="sa")
            sa_r = b.op("scalar", lambda E: E.activation(sumr, lnr, AF.Exp, scale=-1.0),
                        waits=[("sa", sa_l)], inc="sa")
            b.op("tensor", lambda E: E.matmul(pA[:, 0:512], ones_r[:], sumr[0:1, 0:512],
                                              start=True, stop=True),
                 waits=[("sa", sa_r)])
            st_b = b.op("tensor", lambda E: E.matmul(pA[:, 512:1024], ones_r[:], sumr[0:1, 512:1024],
                                                     start=True, stop=True), inc="st")
            sv_rb = b.op("vector", lambda E: E.tensor_copy(rcb, pA[:, 0:1024]),
                         waits=[("st", st_b)], inc="sv")
            av_out_fn(h, sv_rb)

    def av1s(h):
        return arXv[:, h, 1024:2048]

    def v1_load(h, half, stw):
        return b.dma("scalar", vsb[:, 32 * half:32 * (half + 1), :],
                     V1d[h, 4096 * half:4096 * (half + 1), :].rearrange("(n p) d -> p n d", p=128),
                     waits=[("st", stw)], sem="dV%s" % "ab"[half])

    def av1_out(h, sv_rb):
        b.op("vector", lambda E, h=h: E.tensor_tensor(
            av1s(h), pC[:, 0:1024], rcb, ALU.mult),
            waits=[("sv", sv_rb)], inc="sv")

    attention(0, K1l, Q1l, av1_out, v1_load, lambda k: vsb[:, k, :], 128)

    def av2_out(h, sv_rb):
        svt = b.op("vector", lambda E: E.tensor_tensor(
            msc, pC[0:16, 0:1024], rcb[0:16, :], ALU.mult),
            waits=[("sv", sv_rb), ("st", b.n("st"))], inc="sv")
        b.op("tensor", lambda E, h=h: E.matmul(
            pD[0:16, 0:512], idf16[:], msc[:, 0:512], start=(h == 0), stop=(h == NH - 1)),
            waits=[("sv", svt)])
        b.op("tensor", lambda E, h=h: E.matmul(
            pD[0:16, 512:1024], idf16[:], msc[:, 512:1024], start=(h == 0), stop=(h == NH - 1)),
            inc="st")
        if h == NH - 1:
            b.op("vector", lambda E: E.tensor_copy(modpre, pD[0:16, 0:1024]),
                 waits=[("st", b.n("st"))], inc="sv")

    v2sb = arC[:][:, 14336:14336 + NKT * 16].rearrange("p (n r) -> p n r", r=16)

    def v2_load(h, half, stw):
        return b.dma("scalar", v2sb[:, 32 * half:32 * (half + 1), :],
                     V2d[h, :, 32 * half:32 * (half + 1), :],
                     waits=[("st", stw)], sem="dV%s" % "ab"[half])

    attention(1, K2l, Q2l, av2_out, v2_load, lambda k: v2sb[:, k, :], 16)

    if DEBUG_OUTS:
        sv_all = b.n("sv")
        for h in range(NH):
            b.dma("sync", dbg["d_av1"].rearrange("p h t -> p h t")[:, h, :], av1s(h),
                  waits=[("sv", sv_all)], sem="so")
        b.dma("sync", dbg["d_mod"], modpre[:], waits=[("sv", sv_all)], sem="so")

    b.barrier()

    wofu = {0: arCf[:, 0:512], 1: arCf[:, 4608:5120]}
    wob = arC[:][:, 1024:1024 + NT * 512].rearrange("p (c o) -> p c o", o=512)
    wev = {0: b.n("sv"), 1: b.n("sv")}
    pev = {0: b.n("sv"), 1: b.n("sv")}
    for ds in range(4):
        stg = b.n("st")
        for i in range(NT):
            wu = i % 2
            dli = b.dma(QU[wu], wofu[wu], wout[128 * i:128 * (i + 1), 512 * ds:512 * (ds + 1)],
                        waits=[("sv", wev[wu])] + ([("st", stg)] if i < 2 else []),
                        sem="dwo%d" % wu)
            wev[wu] = b.op("vector", lambda E, i=i, wu=wu: E.tensor_copy(wob[:, i, :], wofu[wu]),
                           waits=[("dwo%d" % wu, dli)], inc="sv")
        sv_w = b.n("sv")
        for tt in range(NSLOT):
            pu = tt % 2
            pdst = (pA if pu == 0 else pB)[:, 0:512]
            w0 = [("sv", sv_w), ("sv", pev[pu])]
            for i in range(NT):
                b.op("tensor", lambda E, i=i, tt=tt, pdst=pdst: E.matmul(
                    pdst, arXv[:, i, 1024 + 128 * tt:1024 + 128 * (tt + 1)], wob[:, i, :],
                    start=(i == 0), stop=(i == NT - 1)),
                    waits=w0 if i == 0 else ())
            stt = b.op("tensor", None, inc="st")
            pev[pu] = b.op("vector", lambda E, tt=tt, ds=ds, pdst=pdst: E.tensor_tensor(
                accT[tt][:, 512 * ds:512 * (ds + 1)], accT[tt][:, 512 * ds:512 * (ds + 1)],
                pdst, ALU.add), waits=[("st", stt)], inc="sv")
    b.barrier()

    arXu = arX[:].rearrange("p (c u t) -> p c u t", u=2, t=1024)

    def winX(dc):
        return arXu[:, dc // 2, 1, 512 * (dc % 2):512 * (dc % 2) + 512]

    def wupX(dc):
        return arXu[:, 8 + dc // 2, 1, 512 * (dc % 2):512 * (dc % 2) + 512]

    wdnT = arC[:][:, 8192:8192 + 4 * 2048].rearrange("p (c o) -> p c o", o=2048)
    g_all = arB[:][:, 8192:12288].rearrange("p (t f) -> p t f", f=512)
    stg8 = arCf[:, 0:4096].rearrange("p (u d) -> p u d", u=8)
    bsl = arB[:][0:16, 4096:4608]
    gubf = arB[:][:, 6144:6656]
    guT = arB[:][:, 7168:7680].rearrange("p (q t) -> p q t", q=4)
    prod_bf = arB[:][0:16, 4608:4608 + TPC]

    sa_t = b.op("scalar", lambda E: E.activation(msc, modpre, AF.Tanh), inc="sa")
    sv_p = b.op("vector", lambda E: E.scalar_tensor_tensor(
        prod_bf[:], hAr, 0.1, msc, ALU.mult, ALU.mult), waits=[("sa", sa_t)], inc="sv")

    ptev5 = {0: b.n("sv"), 1: b.n("sv")}
    rowst = {u: 0 for u in range(4)}
    colst = {u: 0 for u in range(4)}
    gaev = {0: b.n("sa"), 1: b.n("sa")}
    upev = {0: b.n("sv"), 1: b.n("sv")}
    mev = {0: b.n("sv"), 1: b.n("sv")}
    p5_st0 = b.n("st")

    def build_row_dc(wsrc, dstf, fs, dc):
        u = dc % 4
        h = dc % 2
        dli = b.dma(QU[dc % 2], stg8[:, u, :].rearrange("p (q d) -> p q d", q=4),
                    wsrc[512 * fs:512 * (fs + 1), 128 * dc:128 * (dc + 1)].rearrange(
                        "(q p) d -> p q d", p=128),
                    waits=[("st", max(rowst[u], p5_st0))], sem="dws%d" % u)
        for q in range(4):
            b.op("tensor", lambda E, q=q, u=u, h=h: E.transpose(
                pD[:, 512 * h + q * 128:512 * h + (q + 1) * 128],
                stg8[:, u, 128 * q:128 * (q + 1)], id_f32[:]),
                waits=[("dws%d" % u, dli), ("sv", ptev5[h])] if q == 0 else ())
        stt = b.op("tensor", None, inc="st")
        rowst[u] = stt
        ptev5[h] = b.op("vector", lambda E, dc=dc, dstf=dstf, h=h: E.tensor_scalar(
            dstf(dc), pD[:, 512 * h:512 * h + 512], ln2c[:, dc:dc + 1], None, ALU.mult),
            waits=[("st", stt)], inc="sv")

    def build_col_dc(wsrc, fs, do):
        u = do % 4
        h = do % 2
        dli = b.dma(QU[1 - do % 2], stg8[:, 4 + u, :],
                    wsrc[128 * do:128 * (do + 1), 512 * fs:512 * (fs + 1)],
                    waits=[("st", max(colst[u], p5_st0))], sem="dwd%d" % u)
        for fc in range(4):
            b.op("tensor", lambda E, fc=fc, u=u, h=h: E.transpose(
                pD[:, 512 * h + fc * 128:512 * h + (fc + 1) * 128],
                stg8[:, 4 + u, 128 * fc:128 * (fc + 1)], id_f32[:]),
                waits=[("dwd%d" % u, dli), ("sv", ptev5[h])] if fc == 0 else ())
        stt = b.op("tensor", None, inc="st")
        colst[u] = stt
        ptev5[h] = b.op("vector", lambda E, do=do, h=h: E.tensor_copy(
            wdnT[:, :, 128 * do:128 * (do + 1)],
            pD[:, 512 * h:512 * h + 512].rearrange("p (c o) -> p c o", o=128)),
            waits=[("st", stt)], inc="sv")

    silu_mark = [0] * NSLOT
    for dc in range(NT):
        build_row_dc(win, winX, 0, dc)
    win_ready = b.n("sv")
    for fs in range(NT):
        gv0 = b.n("sv")
        dlb = b.dma("scalar", bsl, bbd[:, 512 * fs:512 * (fs + 1)],
                    waits=[("st", b.n("st"))], sem="dbl")
        for tt in range(NSLOT):
            pu = tt % 2
            pg = (pA if pu == 0 else pB)[:, 0:512]
            w0 = [("sv", win_ready), ("sv", sv_p), ("dbl", dlb), ("sa", gaev[pu])]
            for c in range(NT):
                b.op("tensor", lambda E, c=c, tt=tt, pg=pg: E.matmul(
                    pg, arXv[:, c, 128 * tt:128 * (tt + 1)], winX(c),
                    start=(c == 0), stop=False),
                    waits=w0 if c == 0 else ())
            b.op("tensor", lambda E, tt=tt, pg=pg: E.matmul(
                pg, prod_bf[:, 128 * tt:128 * (tt + 1)], bsl,
                start=False, stop=True))
            st_g = b.op("tensor", None, inc="st")
            gaev[pu] = b.op("scalar", lambda E, tt=tt, pg=pg: E.activation(
                g_all[:, tt, :], pg, AF.Silu, scale=stats[:, tt, 13:14]),
                waits=[("st", st_g), ("sv", gv0)], inc="sa")
            silu_mark[tt] = gaev[pu]
            build_col_dc(wdn, fs, 2 * tt)
            build_col_dc(wdn, fs, 2 * tt + 1)
            build_row_dc(wup, wupX, fs, 2 * tt)
            build_row_dc(wup, wupX, fs, 2 * tt + 1)
        wup_ready = b.n("sv")
        for tt in range(NSLOT):
            pu = tt % 2
            pg = (pA if pu == 0 else pB)[:, 512:1024]
            w0 = [("sv", wup_ready), ("sv", upev[pu])]
            for c in range(NT):
                b.op("tensor", lambda E, c=c, tt=tt, pg=pg: E.matmul(
                    pg, arXv[:, c, 128 * tt:128 * (tt + 1)], wupX(c),
                    start=(c == 0), stop=(c == NT - 1)),
                    waits=w0 if c == 0 else ())
            st_u = b.op("tensor", None, inc="st")
            sv_gu = b.op("vector", lambda E, tt=tt, pg=pg: E.scalar_tensor_tensor(
                gubf, g_all[:, tt, :], stats[:, tt, 13:14], pg, ALU.mult, ALU.mult),
                waits=[("sa", silu_mark[tt]), ("st", st_u)], inc="sv")
            upev[pu] = sv_gu
            for q in range(4):
                b.op("tensor", lambda E, q=q, pu=pu: E.transpose(
                    pT2[:, 1024 * pu + q * 128:1024 * pu + (q + 1) * 128],
                    gubf[:, 128 * q:128 * (q + 1)], id_bf[:]),
                    waits=[("sv", max(sv_gu, ptev5[pu]))] if q == 0 else ())
            st_t = b.op("tensor", None, inc="st")
            sv_gt = b.op("vector", lambda E, pu=pu: E.tensor_copy(
                guT[:].rearrange("p q t -> p (q t)"), pT2[:, 1024 * pu:1024 * pu + 512]),
                waits=[("st", st_t)], inc="sv")
            ptev5[pu] = sv_gt
            for ds in range(4):
                pu2 = ds % 2
                pdst = pC[:, 512 * pu2:512 * (pu2 + 1)]
                w1 = [("sv", sv_gt), ("sv", mev[pu2])]
                for fc in range(4):
                    b.op("tensor", lambda E, fc=fc, ds=ds, pdst=pdst: E.matmul(
                        pdst, guT[:, fc, :], wdnT[:, fc, 512 * ds:512 * (ds + 1)],
                        start=(fc == 0), stop=(fc == 3)),
                        waits=w1 if fc == 0 else ())
                stt = b.op("tensor", None, inc="st")
                mev[pu2] = b.op("vector", lambda E, tt=tt, ds=ds, pdst=pdst: E.tensor_tensor(
                    accT[tt][:, 512 * ds:512 * (ds + 1)], accT[tt][:, 512 * ds:512 * (ds + 1)],
                    pdst, ALU.add), waits=[("st", stt)], inc="sv")
            if fs + 1 < NT:
                build_row_dc(win, winX, fs + 1, 2 * tt)
                build_row_dc(win, winX, fs + 1, 2 * tt + 1)
        win_ready = b.n("sv")

    sv_f = b.n("sv")
    for tt in range(NSLOT):
        b.dma("sync", out[128 * tt:128 * (tt + 1), :], accT[tt][:],
              waits=[("sv", sv_f)], sem="so")
    b.op("sync", None, waits=[("so", b.n("so"))])
    return _finish(nc, b)


def _finish(nc, b):
    import contextlib
    es = contextlib.ExitStack()
    sems = {}
    names = set()
    for streams in b.streams.values():
        for fn, waits, inc, inck, is_dma in streams:
            if inc:
                names.add(inc)
            for s, v in waits:
                names.add(s)
    with es:
        for nm in sorted(names):
            sems[nm] = es.enter_context(nc.semaphore(nm))
        block = es.enter_context(nc.Block())
        bodies = b.replay(sems)
        block.sync(bodies["sync"])
        block.vector(bodies["vector"])
        block.scalar(bodies["scalar"])
        block.tensor(bodies["tensor"])
        block.gpsimd(bodies["gpsimd"])
    return nc


_ROWS = {}


def _core_rows(c):
    if c not in _ROWS:
        _ROWS[c] = np.concatenate([np.arange(128 * (8 * j + c), 128 * (8 * j + c) + 128)
                                   for j in range(NSLOT)])
    return _ROWS[c]


def kernel(L0_attn_x, L0_attn_w_qkv, L0_attn_w_out, L0_ffn_ln1_w, L0_ffn_ln2_w,
           L0_ffn_Wq, L0_ffn_Wk, L0_ffn_Wv, L0_ffn_Wo, L0_ffn_A, L0_ffn_B,
           L0_ffn_W_in_base, L0_ffn_W_up_base, L0_ffn_W_down):
    f32 = np.float32
    x = np.ascontiguousarray(np.asarray(L0_attn_x, dtype=f32).reshape(SEQ, DM))
    common = dict(
        xf=x,
        wqkv=np.ascontiguousarray(np.asarray(L0_attn_w_qkv, dtype=f32)),
        wq2=np.ascontiguousarray(np.asarray(L0_ffn_Wq, dtype=f32)),
        wk2=np.ascontiguousarray(np.asarray(L0_ffn_Wk, dtype=f32)),
        wv2=np.ascontiguousarray(np.asarray(L0_ffn_Wv, dtype=f32)),
        wo2=np.ascontiguousarray(np.asarray(L0_ffn_Wo, dtype=f32)),
        aa=np.ascontiguousarray(np.asarray(L0_ffn_A, dtype=f32)),
        bbi=np.ascontiguousarray(np.asarray(L0_ffn_B, dtype=f32)),
        ln1r=np.ascontiguousarray(np.asarray(L0_ffn_ln1_w, dtype=f32).reshape(1, DM)),
        ln2r=np.ascontiguousarray(np.asarray(L0_ffn_ln2_w, dtype=f32).reshape(1, DM)),
        wout=np.ascontiguousarray(np.asarray(L0_attn_w_out, dtype=f32)),
        win=np.ascontiguousarray(np.asarray(L0_ffn_W_in_base, dtype=f32)),
        wup=np.ascontiguousarray(np.asarray(L0_ffn_W_up_base, dtype=f32)),
        wdn=np.ascontiguousarray(np.asarray(L0_ffn_W_down, dtype=f32)),
        ident=np.eye(128, dtype=f32),
        kidx=np.ascontiguousarray(np.arange(SEQ, dtype=f32).reshape(NKT, 128).T),
    )
    in_maps = []
    for c in range(NCORES):
        rows = _core_rows(c)
        m = dict(common)
        m["xo"] = np.ascontiguousarray(x[rows])
        m["qidx"] = np.ascontiguousarray(rows.astype(f32).reshape(1, TPC))
        in_maps.append(m)
    nc = build_nc()
    res = run_bass_kernel_spmd(nc, in_maps, core_ids=list(range(NCORES)))
    outp = np.empty((SEQ, DM), dtype=f32)
    for c in range(NCORES):
        outp[_core_rows(c)] = res.results[c]["out"]
    return outp.reshape(1, SEQ, DM)


# version 09
# speedup vs baseline: 1.7951x; 1.0279x over previous
import numpy as np
import ml_dtypes
import concourse.bass as bass
import concourse.mybir as mybir
from concourse.bass_utils import run_bass_kernel_spmd

F32 = mybir.dt.float32
BF16 = mybir.dt.bfloat16
FP8 = mybir.dt.float8e4
AF = mybir.ActivationFunctionType
ALU = mybir.AluOpType

NCORES = 8
SEQ = 8192
DM = 2048
FF = 8192
NH = 16
NT = 16
NSLOT = 8
TPC = 1024
NKT = 64
NPASS = 4
PTOK = 2048
NPP = 16
SCALE = float(1.0 / np.sqrt(128.0))
EPS_LN = 1e-5
EPS_RMS = 1e-6

ENGINES = ("sync", "vector", "scalar", "tensor", "gpsimd")

BUILD_PHASES = "all"
DEBUG_OUTS = False


class Sched:

    def __init__(self):
        self.streams = {e: [] for e in ENGINES}
        self.cnt = {}
        self.dma_sems = set()

    def n(self, sem):
        return self.cnt.get(sem, 0)

    def op(self, eng, fn=None, waits=(), inc=None, inck=1, is_dma=False):
        if fn is None and inc is not None and not waits:
            st = self.streams[eng]
            if st and st[-1][0] is not None and st[-1][2] == inc:
                return self.cnt.get(inc, 0)
        if inc is not None:
            self.cnt[inc] = self.cnt.get(inc, 0) + inck
        ws = tuple((s, v) for (s, v) in waits if v > 0)
        self.streams[eng].append((fn, ws, inc, inck, is_dma))
        return self.cnt.get(inc, 0) if inc else 0

    def dma(self, eng, out, in_, waits=(), sem=None, slow=False):
        sem = sem or ("dl" if eng == "sync" else "dg")
        self.dma_sems.add(sem)
        if slow:
            fn = lambda E, o=out, i=in_: E.dma_start(out=o, in_=i,
                                                     allow_slow_non_contiguous=True)
        else:
            fn = lambda E, o=out, i=in_: E.dma_start(out=o, in_=i)
        return self.op(eng, fn, waits=waits, inc=sem, inck=16, is_dma=True)

    def barrier(self):
        marks = []
        for e in ENGINES:
            st = self.streams[e]
            if not st:
                continue
            fn, ws, inc, inck, isd = st[-1]
            if inc is not None and not isd:
                marks.append((inc, self.n(inc)))
            elif fn is not None and not isd:
                m = self.op(e, None, inc="b_" + e)
                marks.append(("b_" + e, m))
        marks += [(s, self.n(s)) for s in sorted(self.dma_sems)]
        for e in ENGINES:
            self.op(e, None, waits=marks)

    def replay(self, sems):
        def make(name):
            stream = self.streams[name]

            def body(E):
                last_real = None
                last_was_dma = False
                for fn, waits, inc, inck, is_dma in stream:
                    for s, v in waits:
                        E.wait_ge(sems[s], v)
                    if fn is not None:
                        ins = fn(E)
                        last_real = ins
                        last_was_dma = is_dma
                        if inc is not None:
                            ins.then_inc(sems[inc], inck)
                    elif inc is not None:
                        if last_real is not None and not (last_was_dma and inck % 16):
                            last_real.then_inc(sems[inc], inck)
                        else:
                            E.nop().then_inc(sems[inc], inck)
            return body
        return {e: make(e) for e in ENGINES}


def build_nc():
    nc = bass.Bass("TRN2", target_bir_lowering=False, dynamic_dma_scratch_size=512)
    b = Sched()
    QU = {0: "sync", 1: "scalar"}
    QO = {0: "scalar", 1: "sync"}
    C = SCALE

    dt = nc.dram_tensor
    xf = dt("xf", [SEQ, DM], F32, kind="ExternalInput").ap()
    xo = dt("xo", [TPC, DM], F32, kind="ExternalInput").ap()
    qidx = dt("qidx", [1, TPC], F32, kind="ExternalInput").ap()
    kidx = dt("kidx", [128, NKT], F32, kind="ExternalInput").ap()
    wqkv = dt("wqkv", [DM, 3 * DM], F32, kind="ExternalInput").ap()
    wq2 = dt("wq2", [DM, DM], F32, kind="ExternalInput").ap()
    wk2 = dt("wk2", [DM, DM], F32, kind="ExternalInput").ap()
    wv2 = dt("wv2", [DM, DM], F32, kind="ExternalInput").ap()
    wo2 = dt("wo2", [DM, DM], F32, kind="ExternalInput").ap()
    aa = dt("aa", [DM, 16], F32, kind="ExternalInput").ap()
    bbi = dt("bbi", [16, FF], F32, kind="ExternalInput").ap()
    ln1r = dt("ln1r", [1, DM], F32, kind="ExternalInput").ap()
    ln2r = dt("ln2r", [1, DM], F32, kind="ExternalInput").ap()
    wout = dt("wout", [DM, DM], F32, kind="ExternalInput").ap()
    win = dt("win", [FF, DM], F32, kind="ExternalInput").ap()
    wup = dt("wup", [FF, DM], F32, kind="ExternalInput").ap()
    wdn = dt("wdn", [DM, FF], F32, kind="ExternalInput").ap()
    ident = dt("ident", [128, 128], F32, kind="ExternalInput").ap()
    out = dt("out", [TPC, DM], F32, kind="ExternalOutput").ap()

    k1w = dt("k1w", [DM, DM], BF16, kind="Internal").ap()
    v1w = dt("v1w", [DM, DM], BF16, kind="Internal").ap()
    q1w = dt("q1w", [DM, DM], BF16, kind="Internal").ap()
    wq2t = dt("wq2t", [DM, DM], BF16, kind="Internal").ap()
    wk2t = dt("wk2t", [DM, DM], BF16, kind="Internal").ap()
    bbd = dt("bbd", [16, FF], BF16, kind="Internal").ap()
    wvwoad = dt("wvwoad", [DM, 256], BF16, kind="Internal").ap()
    V2d = dt("V2d", [NH, 128, NKT, 16], BF16, kind="Internal").ap()
    K1d = dt("K1d", [NH, 128, SEQ], BF16, kind="Internal").ap()
    V1d = dt("V1d", [NH, SEQ, 128], BF16, kind="Internal").ap()
    K2d = dt("K2d", [NH, 128, SEQ], BF16, kind="Internal").ap()
    Q1d = dt("Q1d", [NH, 128, TPC], BF16, kind="Internal").ap()
    Q2d = dt("Q2d", [NH, 128, TPC], BF16, kind="Internal").ap()

    dbg = {}
    if DEBUG_OUTS:
        for nm, shp in [("d_K1", [NH, 128, SEQ]), ("d_V1", [NH, SEQ, 128]),
                        ("d_K2", [NH, 128, SEQ]), ("d_Q1", [NH, 128, TPC]),
                        ("d_Q2", [NH, 128, TPC]), ("d_V2p", [NH, 128, NKT, 16]),
                        ("d_av1", [128, NH, TPC]), ("d_mod", [16, TPC]),
                        ("d_acc0", [TPC, DM]), ("d_mhat", [128, 32]),
                        ("d_hA", [16, TPC]), ("d_k1w", [DM, DM]), ("d_wq2t", [DM, DM])]:
            dbg[nm] = dt(nm, shp, F32 if nm in ("d_acc0", "d_mhat", "d_mod", "d_hA") else BF16,
                         kind="ExternalOutput").ap()

    sbA = nc.alloc_sbuf_tensor
    id_bf = sbA("id_bf", [128, 128], BF16)
    ones_r = sbA("ones_r", [1, 128], F32)
    ones_c = sbA("ones_c", [128, 1], BF16)
    kidx_sb = sbA("kidx_sb", [128, NKT], F32)
    a2_bf = sbA("a2_bf", [128, NT, 16], BF16)
    ln1b = sbA("ln1b", [128, DM], BF16)
    id_f32 = sbA("id_f32", [128, 128], F32)
    ln1c = sbA("ln1c", [128, 16], F32)
    ln2c = sbA("ln2c", [128, 16], F32)
    mhat = sbA("mhat", [128, 32], F32)
    nrm4 = sbA("nrm4", [1, 64], F32)
    ones32 = sbA("ones32", [128, 33], BF16)
    modpre = sbA("modpre", [16, TPC], F32)[:]
    msc = sbA("msc", [16, TPC], F32)[:]
    hAr = sbA("hAr", [16, TPC], F32)[:]
    idf16 = sbA("idf16", [16, 16], F32)
    stats = sbA("stats", [128, NPP, 16], F32)
    mrow = sbA("mrow", [1, 32], F32)
    cst = sbA("cst", [128, 4], F32)
    arX = sbA("arX", [128, NT * 2048], BF16)
    arB = sbA("arB", [128, 15360], BF16)
    arC = sbA("arC", [128, 16384], BF16)
    arD = sbA("arD", [128, 4096], BF16)
    arE = sbA("arE", [128, 2048], BF16)
    accT = [sbA(f"acc{i}", [128, DM], F32) for i in range(NSLOT)]

    pA = nc.alloc_psum_tensor("pA", [128, 1024], F32)
    pB = nc.alloc_psum_tensor("pB", [128, 1024], F32)
    pC = nc.alloc_psum_tensor("pC", [128, 1024], F32)
    pD = nc.alloc_psum_tensor("pD", [128, 1024], F32)

    arBf = arB[:].bitcast(F32)
    arCf = arC[:].bitcast(F32)
    arEf = arE[:].bitcast(F32)
    arXf = arX[:].bitcast(F32)
    qib = arXf[:, 1024:2048]
    pT2 = pD[:].bitcast(BF16)

    id_f = id_f32[:]
    b.dma("sync", id_f, ident, sem="di")
    b.dma("sync", idf16[:], ident[0:16, 0:16], sem="di")
    b.dma("sync", kidx_sb[:], kidx, sem="di")
    b.dma("sync", ln1c[:], ln1r.rearrange("o (c p) -> p (o c)", p=128), sem="di", slow=True)
    b.dma("sync", ln2c[:], ln2r.rearrange("o (c p) -> p (o c)", p=128), sem="di", slow=True)
    aa_sb = arCf[:, 0:NT * 16].rearrange("p (c r) -> p c r", r=16)
    b.dma("sync", aa_sb, aa.rearrange("(c p) r -> p c r", p=128), sem="di")
    qi_row = arXf[0:1, 0:TPC]
    b.dma("sync", qi_row, qidx, sem="di")
    l1_row = arBf[0:1, 0:DM]
    b.dma("sync", l1_row, ln1r, sem="di")
    dl0 = b.n("di")
    b.op("vector", lambda E: E.tensor_copy(id_bf[:], id_f), waits=[("di", dl0)])
    b.op("vector", lambda E: E.memset(ones_r[:], 1.0))
    b.op("vector", lambda E: E.memset(ones_c[:], 1.0))
    b.op("vector", lambda E: E.memset(cst[:, 0:1], EPS_LN))
    b.op("vector", lambda E: E.memset(cst[:, 1:2], EPS_RMS))
    b.op("vector", lambda E: E.memset(cst[:, 2:3], 1.0))
    svz = b.op("vector", lambda E: E.memset(ones32[:], 0.0), inc="sv")
    b.op("vector", lambda E: E.memset(ones32[:, 32:33], 1.0), waits=[("sv", svz)])
    sv0 = b.op("vector", None, inc="sv")
    b.op("tensor", lambda E: E.matmul(pD[:, 0:512], ones_r[:], qi_row[0:1, 0:512], start=True, stop=True),
         waits=[("sv", sv0)])
    b.op("tensor", lambda E: E.matmul(pD[:, 512:1024], ones_r[:], qi_row[0:1, 512:1024], start=True, stop=True))
    for _h in range(2):
        b.op("tensor", lambda E, _h=_h: E.matmul(pA[:, 512 * _h:512 * (_h + 1)], ones_r[:],
                                                 l1_row[0:1, 512 * _h:512 * (_h + 1)], start=True, stop=True))
    for _h in range(2):
        b.op("tensor", lambda E, _h=_h: E.matmul(pB[:, 512 * _h:512 * (_h + 1)], ones_r[:],
                                                 l1_row[0:1, 1024 + 512 * _h:1024 + 512 * (_h + 1)], start=True, stop=True))
    st1b = b.op("tensor", None, inc="st")
    sv_q = b.op("vector", lambda E: E.tensor_copy(qib[:], pD[:, 0:TPC]), waits=[("st", st1b)], inc="sv")
    b.op("vector", lambda E: E.tensor_copy(ln1b[:, 0:1024], pA[:]))
    svl = b.op("vector", lambda E: E.tensor_copy(ln1b[:, 1024:2048], pB[:]), inc="sv")
    mask_sb = arD[:].bitcast(FP8).rearrange("p (k q) -> p k q", q=128)
    for k in range(NKT):
        j = k // 8
        b.op("vector", lambda E, k=k, j=j: E.tensor_scalar(
            mask_sb[:, k, :], qib[:, j * 128:(j + 1) * 128], kidx_sb[:, k:k + 1], None, ALU.is_ge),
            waits=[("sv", sv_q)] if k == 0 else ())
    svm = b.op("vector", None, inc="sv")
    for i in range(NT):
        b.op("vector", lambda E, i=i: E.tensor_scalar(
            a2_bf[:, i, :], aa_sb[:, i, :], ln2c[:, i:i + 1], None, ALU.mult))
    b.op("vector", None, inc="sv")
    bstage_f = arBf[0:16, 0:1024]
    bstage_b = arB[:][0:16, 2048:3072]
    for q in range(8):
        dq = b.dma("sync", bstage_f, bbi[:, q * 1024:(q + 1) * 1024],
                   waits=[("sv", b.n("sv"))], sem="db")
        svb = b.op("vector", lambda E: E.tensor_copy(bstage_b, bstage_f),
                   waits=[("db", dq), ("dbo", b.n("dbo"))], inc="sv")
        b.dma("scalar", bbd[:, q * 1024:(q + 1) * 1024], bstage_b, waits=[("sv", svb)], sem="dbo")
    woa_f = arCf[:, NT * 16:2 * NT * 16].rearrange("p (c r) -> p c r", r=16)
    wsl = arBf[:, 2048:6144].rearrange("p (u d) -> p u d", u=2)
    pwoa = pC[:, 0:256].rearrange("p (c r) -> p c r", r=16)
    for i in range(NT):
        dli = b.dma(QU[i % 2], wsl[:, i % 2, :], wo2[128 * i:128 * (i + 1), :],
                    waits=[("st", b.n("st") - 1)] if i >= 2 else [("sv", svl)],
                    sem="dw%d" % (i % 2))
        w0 = [("dw%d" % (i % 2), dli), ("sv", svm)]
        for j in range(NT):
            b.op("tensor", lambda E, i=i, j=j: E.matmul(
                pwoa[:, j, :], wsl[:, i % 2, 128 * j:128 * (j + 1)], aa_sb[:, i, :],
                start=(i == 0 and j == 0), stop=(i == NT - 1 and j == NT - 1)),
                waits=w0 if j == 0 else ())
        b.op("tensor", None, inc="st")
    stw = b.n("st")
    svw = b.op("vector", lambda E: E.tensor_copy(woa_f[:], pwoa[:]), waits=[("st", stw)], inc="sv")
    wvstage = arC[:][:, 7168:11264].rearrange("p (c x) -> p c x", x=256)
    for i in range(NT):
        dli = b.dma(QU[i % 2], wsl[:, i % 2, :], wv2[128 * i:128 * (i + 1), :],
                    waits=[("st", b.n("st") - 1)] if i >= 2 else [("st", b.n("st"))],
                    sem="dw%d" % (i % 2))
        svf = b.op("vector", lambda E, i=i: E.tensor_tensor(
            wsl[:, i % 2, :], wsl[:, i % 2, :], ln1b[:], ALU.mult),
            waits=[("dw%d" % (i % 2), dli), ("sv", svl)], inc="sv")
        w0 = [("sv", svf), ("sv", svw), ("sv", b.n("sv"))]
        for j in range(NT):
            b.op("tensor", lambda E, i=i, j=j: E.matmul(
                pwoa[:, j, :], wsl[:, i % 2, 128 * j:128 * (j + 1)], woa_f[:, i, :],
                start=True, stop=True),
                waits=w0 if j == 0 else ())
        stt = b.op("tensor", None, inc="st")
        b.op("vector", lambda E, i=i: E.tensor_copy(wvstage[:, :, 16 * i:16 * (i + 1)], pwoa[:]),
             waits=[("st", stt)], inc="sv")
    svw2 = b.op("vector", None, inc="sv")
    b.dma("scalar", wvwoad.rearrange("(c p) x -> p c x", p=128), wvstage[:],
          waits=[("sv", svw2)], sem="dwv")

    cstage = arC[:][:, 2048:6144].rearrange("p (u d) -> p u d", u=2)
    it = 0
    for i in range(NT):
        for part, (wsrc, wdst) in enumerate([
                (wqkv[128 * i:128 * (i + 1), 0:DM], q1w[128 * i:128 * (i + 1), :]),
                (wqkv[128 * i:128 * (i + 1), DM:2 * DM], k1w[128 * i:128 * (i + 1), :]),
                (wqkv[128 * i:128 * (i + 1), 2 * DM:3 * DM], v1w[128 * i:128 * (i + 1), :])]):
            u = it % 2
            dli = b.dma(QU[u], wsl[:, u, :], wsrc,
                        waits=[("sv", b.n("sv") - 1)] if it >= 2 else [("sv", svw2)],
                        sem="dw%d" % u)
            svc = b.op("vector", lambda E, u=u: E.tensor_copy(cstage[:, u, :], wsl[:, u, :]),
                       waits=[("dw%d" % u, dli), ("dc%d" % u, b.n("dc%d" % u))],
                       inc="sv")
            b.dma(QO[u], wdst, cstage[:, u, :], waits=[("sv", svc)], sem="dc%d" % u)
            it += 1
    tstage = arC[:][:, 6144:7168].rearrange("p (u d) -> p u d", u=2)
    for wsrc, wdst in [(wq2, wq2t), (wk2, wk2t)]:
        for i in range(NT):
            u = i % 2
            dli = b.dma(QU[u], wsl[:, u, :], wsrc[128 * i:128 * (i + 1), :],
                        waits=[("st", b.n("st") - 1)] if i >= 2 else [("sv", b.n("sv"))],
                        sem="dw%d" % u)
            svc = b.op("vector", lambda E, u=u: E.tensor_tensor(
                cstage[:, u, :], wsl[:, u, :], ln1b[:], ALU.mult),
                waits=[("dw%d" % u, dli), ("dc%d" % u, b.n("dc%d" % u))], inc="sv")
            for g in range(4):
                for jj in range(4):
                    j = 4 * g + jj
                    ws = []
                    if jj == 0:
                        ws = [("sv", max(svc, b.n("sv")))]
                    b.op("tensor", lambda E, u=u, j=j, jj=jj, g=g: E.transpose(
                        pT2[:, (g % 2) * 1024 + jj * 128:(g % 2) * 1024 + (jj + 1) * 128],
                        cstage[:, u, 128 * j:128 * (j + 1)], id_bf[:]),
                        waits=ws)
                stt = b.op("tensor", None, inc="st")
                uu = g % 2
                sve = b.op("vector", lambda E, uu=uu: E.tensor_copy(
                    tstage[:, uu, :], pT2[:, uu * 1024:uu * 1024 + 512]),
                    waits=[("st", stt), ("dt%d" % uu, b.n("dt%d" % uu))], inc="sv")
                b.dma(QO[uu],
                      wdst[512 * g:512 * (g + 1), 128 * i:128 * (i + 1)].rearrange(
                          "(jj p) o -> p jj o", p=128),
                      tstage[:, uu, :].rearrange("p (jj o) -> p jj o", o=128),
                      waits=[("sv", sve)], sem="dt%d" % uu)
    def early_out():
        if BUILD_PHASES in ("p0", "p1a1", "p1a"):
            for tt in range(NSLOT):
                b.op("vector", lambda E, tt=tt: E.memset(accT[tt][:], 0.0))
            b.op("vector", None, inc="sv")
        svq = b.n("sv")
        for tt in range(NSLOT):
            b.dma("sync", out[128 * tt:128 * (tt + 1), :], accT[tt][:],
                  waits=[("sv", svq)] if tt == 0 else (), sem="so")
        b.op("sync", None, waits=[("so", b.n("so"))])
        return _finish(nc, b)

    b.barrier()
    if BUILD_PHASES == "p0":
        return early_out()

    arXv = arX[:].rearrange("p (c t) -> p c t", t=2048)
    xnT = arXv
    xdb = arCf[:, 0:6144].rearrange("p (u d) -> p u d", u=3)
    v2stage = arC[:][:, 12288:16384].rearrange("p (n x) -> p n x", x=256)
    xsc = arCf[:, 4096:6144]
    sqf = arCf[:, 6144:8192]
    xnbu = {0: arE[:][:, 0:2048], 1: accT[7][:].bitcast(BF16)[:, 0:2048]}
    r1row = arEf[0:1, 0:1024]
    r1bc = ln1b[:]
    kwsl = arB[:][:, 0:6144].rearrange("p (u c o) -> p u c o", u=3, c=NT)
    vwsl = arB[:][:, 6144:14336].rearrange("p (u c o) -> p u c o", u=2, c=NT)
    estg = arB[:][:, 14336:15360].rearrange("p (u d) -> p u d", u=2)

    def stat_chain(tt, sv_sum):
        s = lambda i: stats[:, tt, i:i + 1]
        steps = [
            lambda E: E.activation(s(2), s(0), AF.Copy, scale=1.0 / DM),
            lambda E: E.activation(s(3), s(2), AF.Square),
            lambda E: E.activation(s(4), s(1), AF.Copy, scale=1.0 / DM),
            lambda E: E.activation(s(5), s(3), AF.Relu, scale=-1.0, bias=s(4)),
            lambda E: E.activation(s(6), s(5), AF.Ln, bias=cst[:, 0:1]),
            lambda E: E.activation(s(7), s(6), AF.Exp, scale=-0.5),
            lambda E: E.activation(s(8), s(6), AF.Exp, scale=-1.0),
            lambda E: E.activation(s(9), s(8), AF.Relu, scale=-EPS_LN, bias=cst[:, 2:3]),
            lambda E: E.activation(s(10), s(9), AF.Ln, bias=cst[:, 1:2]),
            lambda E: E.activation(s(13), s(10), AF.Exp, scale=-0.5),
        ]
        last = 0
        for i, fn in enumerate(steps):
            w = [("sv", sv_sum)] if i == 0 else [("sa", last)]
            last = b.op("scalar", fn, waits=w, inc="sa")
        return last

    def mk_lnstate(src, ntiles, own, guard_st):
        return dict(src=src, n=ntiles, own=own, guard=guard_st,
                    nslots=2 if own else 3,
                    ptev={0: b.n("sv"), 1: b.n("sv")},
                    xev={u: [("sv", b.n("sv")), ("sa", b.n("sa"))]
                         for u in range(2 if own else 3)},
                    tsv={}, bmark={}, sacd={}, a1n=0)

    def stage_a1(st, tt, extra=()):
        u = tt % st["nslots"]
        own = st["own"]
        wx = list(st["xev"][u]) + list(extra)
        dxa = b.dma("sync", xdb[:, u, 0:1024], st["src"][128 * tt:128 * (tt + 1), 0:1024],
                    waits=wx, sem="dxa%d" % u)
        dxb = b.dma("scalar", xdb[:, u, 1024:2048], st["src"][128 * tt:128 * (tt + 1), 1024:2048],
                    waits=wx, sem="dxb%d" % u)
        sv_sum = b.op("vector", lambda E, u=u, tt=tt: E.tensor_reduce(
            stats[:, tt, 0:1], xdb[:, u, :], mybir.AxisListType.X, ALU.add),
            waits=[("dxa%d" % u, dxa), ("dxb%d" % u, dxb)], inc="sv")
        sqw = [("dxa%d" % u, dxa), ("dxb%d" % u, dxb)]
        if own and tt < 2:
            sqw.append(("dvo", b.n("dvo")))
        b.op("scalar", lambda E, u=u, tt=tt, own=own: E.activation(
            sqf if own else accT[tt % 4][:], xdb[:, u, :], AF.Square,
            accum_out=stats[:, tt, 1:2]),
            waits=sqw)
        st["sacd"][tt] = stat_chain(tt, sv_sum)
        st["a1n"] = tt + 1

    def stage_a2(st, tt):
        u = tt % st["nslots"]
        own = st["own"]
        sa_c = st["sacd"][tt]
        s = lambda i, tt=tt: stats[:, tt, i:i + 1]
        if own:
            sv_xn = b.op("vector", lambda E, u=u, m=s(2), r=s(7): E.tensor_scalar(
                xsc, xdb[:, u, :], m, r, ALU.subtract, ALU.mult),
                waits=[("sa", sa_c), ("st", st["guard"])], inc="sv")
            wb = [("sv", sv_xn),
                  ("st", st["guard"] if tt == 0 else st["bmark"][tt - 1])]
            sv_xnb = b.op("vector", lambda E, xb=xnbu[0]: E.tensor_copy(xb, xsc),
                          waits=wb, inc="sv")
            sv_acc = b.op("vector", lambda E, u=u, tt=tt: E.tensor_tensor(
                accT[tt][:], xdb[:, u, :], xsc, ALU.add), waits=[("sv", sv_xnb)], inc="sv")
            st["xev"][u] = [("sv", sv_acc), ("sa", sa_c)]
            st["tsv"][tt] = sv_xnb
        else:
            xb = xnbu[tt % 2]
            wb = [("sa", sa_c),
                  ("st", st["guard"] if tt < 2 else st["bmark"][tt - 2])]
            sv_xnb = b.op("vector", lambda E, u=u, xb=xb, m=s(2), r=s(7): E.tensor_scalar(
                xb, xdb[:, u, :], m, r, ALU.subtract, ALU.mult),
                waits=wb, inc="sv")
            st["xev"][u] = [("sv", sv_xnb), ("sa", sa_c)]
            st["tsv"][tt] = sv_xnb

    def stage_b(st, tt):
        xb = xnbu[0] if st["own"] else xnbu[tt % 2]
        svr = st["tsv"][tt]
        ptev = st["ptev"]
        stt = 0
        for g in range(4):
            for jj in range(4):
                j = 4 * g + jj
                ws = []
                if jj == 0:
                    ws = [("sv", max(svr, ptev[g % 2]))]
                b.op("tensor", lambda E, j=j, jj=jj, g=g, xb=xb: E.transpose(
                    pT2[:, (g % 2) * 1024 + jj * 128:(g % 2) * 1024 + jj * 128 + 128],
                    xb[:, 128 * j:128 * (j + 1)], id_bf[:]),
                    waits=ws)
            stt = b.op("tensor", None, inc="st")
            ptev[g % 2] = b.op("vector", lambda E, g=g, tt=tt: E.tensor_copy(
                xnT[:, 4 * g:4 * g + 4, 128 * tt:128 * (tt + 1)],
                pT2[:, (g % 2) * 1024:(g % 2) * 1024 + 512].rearrange("p (c t) -> p c t", t=128)),
                waits=[("st", stt)], inc="sv")
        st["bmark"][tt] = stt

    def finish_pass(st):
        ntiles = st["n"]
        while st["a1n"] < min(st["nslots"], ntiles):
            stage_a1(st, st["a1n"])
        for tt in range(ntiles):
            stage_a2(st, tt)
            stage_b(st, tt)
            if tt + st["nslots"] < ntiles:
                stage_a1(st, tt + st["nslots"])
        ptev = st["ptev"]
        last_sa = st["sacd"][ntiles - 1]
        rc_mark = 0
        bc_mark = 0
        for rnd in range((ntiles + 7) // 8):
            ws0 = [("sa", last_sa), ("sv", max(ptev[0], ptev[1]))]
            if rnd > 0:
                ws0.append(("sv", rc_mark))
            for j in range(8):
                tt = 8 * rnd + j
                b.op("tensor", lambda E, tt=tt, j=j: E.transpose(
                    pD[0:1, 128 * j:128 * (j + 1)], stats[:, tt, 13:14], id_f32[:]),
                    waits=ws0 if j == 0 else ())
            sttr = b.op("tensor", None, inc="st")
            sv_row = b.op("vector", lambda E: E.tensor_copy(r1row, pD[0:1, 0:1024]),
                          waits=[("st", sttr)], inc="sv")
            for q in range(2):
                b.op("tensor", lambda E, q=q: E.matmul(
                    pD[:, 512 * q:512 * (q + 1)], ones_r[:],
                    r1row[0:1, 512 * q:512 * (q + 1)], start=True, stop=True),
                    waits=[("sv", sv_row)] if q == 0 else ())
            bc_mark = b.op("tensor", None, inc="st")
            rc_mark = b.op("vector", lambda E, rnd=rnd: E.tensor_copy(
                r1bc[:, 1024 * rnd:1024 * (rnd + 1)], pD[:, 0:1024]),
                waits=[("st", bc_mark)], inc="sv")
        return b.n("sv"), rc_mark, bc_mark

    def proj_fm(wsrc_bf, dst, sv_ready, col0, ntok, rescale_sv, prev_marks):
        ev = {0: b.n("sv"), 1: b.n("sv")}
        ntch = ntok // 512
        base_st = b.n("st")
        marks = []
        for oc in range(NT):
            u = oc % 3
            if oc >= 3:
                wld = [("st", marks[oc - 3])]
            elif prev_marks is not None:
                wld = [("st", prev_marks[u])]
            else:
                wld = [("st", base_st)]
            dla = b.dma("sync", kwsl[:, u, 0:8, :],
                        wsrc_bf[0:1024, 128 * oc:128 * (oc + 1)].rearrange("(c p) o -> p c o", p=128),
                        waits=wld, sem="dka%d" % u)
            dlb = b.dma("scalar", kwsl[:, u, 8:16, :],
                        wsrc_bf[1024:2048, 128 * oc:128 * (oc + 1)].rearrange("(c p) o -> p c o", p=128),
                        waits=wld, sem="dkb%d" % u)
            stt = 0
            for tch in range(ntch):
                pu = tch % 2
                pdst = (pA if pu == 0 else pB)[:, 0:512]
                w0 = [("dka%d" % u, dla), ("sv", sv_ready), ("sv", ev[pu])]
                for c in range(NT):
                    wc = w0 if c == 0 else ()
                    if c == 8 and tch == 0:
                        wc = [("dkb%d" % u, dlb)]
                    b.op("tensor", lambda E, u=u, c=c, tch=tch, pdst=pdst: E.matmul(
                        pdst, kwsl[:, u, c, :], xnT[:, c, 512 * tch:512 * (tch + 1)],
                        start=(c == 0), stop=(c == NT - 1)),
                        waits=wc)
                stt = b.op("tensor", None, inc="st")
                eu = tch % 2
                if rescale_sv is None:
                    sve = b.op("vector", lambda E, eu=eu, pdst=pdst: E.tensor_copy(
                        estg[:, eu, :], pdst),
                        waits=[("st", stt), ("de%d" % eu, b.n("de%d" % eu))], inc="sv")
                else:
                    sve = b.op("vector", lambda E, eu=eu, pdst=pdst, tch=tch: E.tensor_tensor(
                        estg[:, eu, :], pdst, r1bc[:, 512 * tch:512 * (tch + 1)], ALU.mult),
                        waits=[("st", stt), ("de%d" % eu, b.n("de%d" % eu)),
                               ("sv", rescale_sv)], inc="sv")
                ev[pu] = sve
                b.dma(QO[eu], dst[oc][:, col0 + 512 * tch:col0 + 512 * (tch + 1)],
                      estg[:, eu, :], waits=[("sv", sve)], sem="de%d" % eu)
            marks.append(stt)
        return marks

    K1l = [K1d[h] for h in range(NH)]
    K2l = [K2d[h] for h in range(NH)]
    pmarks = None
    vprev = None
    st_cur = mk_lnstate(xf[0:PTOK, :], NPP, False, b.n("st"))
    for p in range(NPASS):
        sv_ready, sv_r1, guard = finish_pass(st_cur)
        mkK1 = proj_fm(k1w, K1l, sv_ready, PTOK * p, PTOK, None, pmarks)
        ev = {0: b.n("sv"), 1: b.n("sv")}
        vmk = []
        for avs in range(8):
            u = avs % 2
            if avs >= 2:
                wld = [("st", vmk[avs - 2])]
            elif vprev is not None:
                wld = [("st", vprev[6 + avs])]
            else:
                wld = [("st", b.n("st"))]
            dla = b.dma("sync", vwsl[:, u, 0:8, :],
                        v1w[0:1024, 256 * avs:256 * (avs + 1)].rearrange("(c p) o -> p c o", p=128),
                        waits=wld, sem="dva%d" % u)
            dlb = b.dma("scalar", vwsl[:, u, 8:16, :],
                        v1w[1024:2048, 256 * avs:256 * (avs + 1)].rearrange("(c p) o -> p c o", p=128),
                        waits=wld, sem="dvb%d" % u)
            stt = 0
            for tt in range(NPP):
                pu = tt % 2
                pdst = (pA if pu == 0 else pB)[:, 512:768]
                w0 = [("dva%d" % u, dla), ("sv", sv_ready), ("sv", ev[pu])]
                for c in range(NT):
                    wc = w0 if c == 0 else ()
                    if c == 8 and tt == 0:
                        wc = [("dvb%d" % u, dlb)]
                    b.op("tensor", lambda E, u=u, c=c, tt=tt, pdst=pdst: E.matmul(
                        pdst, xnT[:, c, 128 * tt:128 * (tt + 1)], vwsl[:, u, c, :],
                        start=(c == 0), stop=(c == NT - 1)),
                        waits=wc)
                stt = b.op("tensor", None, inc="st")
                eu = tt % 2
                sve = b.op("vector", lambda E, eu=eu, pdst=pdst: E.tensor_copy(
                    estg[:, eu, 0:256], pdst),
                    waits=[("st", stt), ("de%d" % eu, b.n("de%d" % eu))], inc="sv")
                ev[pu] = sve
                h0 = avs * 2
                b.dma(QO[eu],
                      V1d[h0:h0 + 2, PTOK * p + 128 * tt:PTOK * p + 128 * (tt + 1), :].rearrange(
                          "h t d -> t h d"),
                      estg[:, eu, 0:256].rearrange("p (h d) -> p h d", h=2),
                      waits=[("sv", sve)], sem="de%d" % eu)
            vmk.append(stt)
        vprev = vmk
        mkK2 = proj_fm(wk2t, K2l, sv_ready, PTOK * p, PTOK, sv_r1, None)
        if p + 1 < NPASS:
            st_next = mk_lnstate(xf[PTOK * (p + 1):PTOK * (p + 2), :], NPP, False, guard)
        else:
            st_next = mk_lnstate(xo, NSLOT, True, guard)
        wvsl = kwsl[:].rearrange("p u c o -> p (u c o)")[:, 0:NT * 256].rearrange(
            "p (c x) -> p c x", x=256)
        wv_w = [("st", b.n("st")), ("dwv", b.n("dwv")), ("dwu", b.n("dwu"))]
        dwv = b.dma("sync", wvsl[:, 0:8, :],
                    wvwoad[0:1024].rearrange("(c p) x -> p c x", p=128),
                    waits=wv_w, sem="dwv")
        dwu = b.dma("scalar", wvsl[:, 8:16, :],
                    wvwoad[1024:2048].rearrange("(c p) x -> p c x", p=128),
                    waits=wv_w, sem="dwu")
        pv2h = {0: pC[:, 0:256], 1: pC[:, 512:768]}
        ev = {0: b.n("sv"), 1: b.n("sv")}
        v2mk = 0
        for tt in range(NPP):
            pu = tt % 2
            pv2 = pv2h[pu]
            w0 = [("sv", sv_ready), ("sv", ev[pu]), ("st", b.n("st")), ("dwv", dwv)] if tt <= 1 else [("sv", ev[pu])]
            for c in range(NT):
                wc = w0 if c == 0 else ()
                if c == 8 and tt <= 1:
                    wc = [("dwu", dwu)]
                b.op("tensor", lambda E, c=c, tt=tt, pv2=pv2: E.matmul(
                    pv2, xnT[:, c, 128 * tt:128 * (tt + 1)], wvsl[:, c, :],
                    start=(c == 0), stop=(c == NT - 1)),
                    waits=wc)
            v2mk = b.op("tensor", None, inc="st")
            ev[pu] = b.op("vector", lambda E, tt=tt, pv2=pv2: E.tensor_scalar(
                v2stage[:, tt, :], pv2, stats[:, tt, 13:14], None, ALU.mult),
                waits=[("st", v2mk), ("dvo", b.n("dvo"))], inc="sv")
            if not st_next["own"] and tt < 3:
                stage_a1(st_next, tt, extra=[("sv", ev[pu])])
        svv = b.n("sv")
        for h in range(NH):
            b.dma("scalar", V2d[h, :, NPP * p:NPP * (p + 1), :], v2stage[:, :, 16 * h:16 * (h + 1)],
                  waits=[("sv", svv)] if h == 0 else (), sem="dvo")
        if st_next["own"]:
            for tt in range(2):
                stage_a1(st_next, tt, extra=[("sv", b.n("sv"))])
        pmarks = {0: v2mk, 1: v2mk, 2: mkK2[14]}
        st_cur = st_next
        if BUILD_PHASES == "p1a1":
            b.barrier()
            return early_out()

    b.barrier()
    if BUILD_PHASES == "p1a":
        return early_out()

    sv_ready, sv_r1, guard = finish_pass(st_cur)
    Q1l = [Q1d[h] for h in range(NH)]
    Q2l = [Q2d[h] for h in range(NH)]
    mkQ1 = proj_fm(q1w, Q1l, sv_ready, 0, TPC, None, pmarks)
    proj_fm(wq2t, Q2l, sv_ready, 0, TPC, sv_r1, None)
    ph = pC[0:16, 0:1024]
    for tch in range(2):
        w0 = [("sv", sv_ready), ("st", b.n("st"))]
        for c in range(NT):
            b.op("tensor", lambda E, c=c, tch=tch: E.matmul(
                ph[:, 512 * tch:512 * (tch + 1)], a2_bf[:, c, :],
                xnT[:, c, 512 * tch:512 * (tch + 1)],
                start=(c == 0), stop=(c == NT - 1)),
                waits=w0 if c == 0 else ())
    stt = b.op("tensor", None, inc="st")
    b.op("vector", lambda E: E.tensor_copy(hAr, ph), waits=[("st", stt)], inc="sv")

    smpu = {0: arC[:][:, 0:512], 1: arC[:][:, 512:1024]}
    ssqu = {0: arC[:][:, 1024:1536], 1: arC[:][:, 1536:2048]}
    pru = {0: pD[0:1, 0:512], 1: pD[0:1, 512:1024]}
    sqm = {0: b.n("sa"), 1: b.n("sa")}
    mmm = {0: b.n("st"), 1: b.n("st")}
    rdm = {0: b.n("sv"), 1: b.n("sv")}
    base_w = [("sv", b.n("sv")), ("sa", b.n("sa")), ("st", b.n("st")),
              ("de0", b.n("de0")), ("de1", b.n("de1"))]
    i = 0
    for idx, srcl in enumerate([K1l, K2l, Q1l, Q2l]):
        for h in range(NH):
            u = i % 2
            dls = b.dma("sync", smpu[u], srcl[h][:, 0:512],
                        waits=(base_w if i < 2 else [("sa", sqm[u])]), sem="dsm%d" % u)
            sas = b.op("scalar", lambda E, u=u: E.activation(ssqu[u], smpu[u], AF.Square),
                       waits=[("dsm%d" % u, dls), ("st", mmm[u])], inc="sa")
            sqm[u] = sas
            stt = b.op("tensor", lambda E, u=u: E.matmul(pru[u], ones_c[:], ssqu[u],
                                                         start=True, stop=True),
                       waits=[("sa", sas), ("sv", rdm[u])], inc="st")
            mmm[u] = stt
            rdm[u] = b.op("vector", lambda E, idx=idx, h=h, u=u: E.tensor_reduce(
                nrm4[0:1, idx * 16 + h:idx * 16 + h + 1], pru[u], mybir.AxisListType.X, ALU.max),
                waits=[("st", stt)], inc="sv")
            i += 1
    svn = b.n("sv")
    t1 = arEf[0:1, 0:32]
    b.op("gpsimd", lambda E: E.tensor_tensor(t1[0:1, 0:16], nrm4[0:1, 32:48], nrm4[0:1, 0:16], ALU.mult),
         waits=[("sv", svn)])
    sg2 = b.op("gpsimd", lambda E: E.tensor_tensor(t1[0:1, 16:32], nrm4[0:1, 48:64], nrm4[0:1, 16:32], ALU.mult),
               inc="sg")
    sa1 = b.op("scalar", lambda E: E.activation(mrow[0:1, :], t1, AF.Ln), waits=[("sg", sg2)], inc="sa")
    sa2 = b.op("scalar", lambda E: E.activation(mrow[0:1, :], mrow[0:1, :], AF.Exp, scale=0.5),
               waits=[("sa", sa1)], inc="sa")
    sa3 = b.op("scalar", lambda E: E.mul(mrow[0:1, :], mrow[0:1, :], -1.25 * C), waits=[("sa", sa2)], inc="sa")
    sv_cl = b.op("vector", lambda E: E.tensor_scalar_max(mrow[0:1, :], mrow[0:1, :], -30.0),
                 waits=[("sa", sa3)], inc="sv")
    stt = b.op("tensor", lambda E: E.matmul(pD[:, 0:32], ones_r[:], mrow[0:1, :], start=True, stop=True),
               waits=[("sv", sv_cl)], inc="st")
    b.op("vector", lambda E: E.tensor_copy(mhat[:], pD[:, 0:32]), waits=[("st", stt)], inc="sv")

    if DEBUG_OUTS:
        sv_all = b.n("sv")
        b.dma("scalar", dbg["d_V2p"], V2d, waits=[("sv", sv_all)], sem="so")
        b.dma("scalar", dbg["d_k1w"], k1w, sem="so")
        b.dma("scalar", dbg["d_wq2t"], wq2t, sem="so")
        b.dma("sync", dbg["d_mhat"], mhat[:], waits=[("sv", sv_all)], sem="so")
        b.dma("sync", dbg["d_hA"], hAr[:], waits=[("sv", sv_all)], sem="so")
        for tt in range(NSLOT):
            b.dma("sync", dbg["d_acc0"][128 * tt:128 * (tt + 1), :], accT[tt][:],
                  waits=[("sv", sv_all)], sem="so")
        for h in range(NH):
            b.dma("scalar", dbg["d_K1"][h], K1d[h], sem="so")
            b.dma("scalar", dbg["d_V1"][h], V1d[h], sem="so")
            b.dma("scalar", dbg["d_K2"][h], K2d[h], sem="so")
            b.dma("scalar", dbg["d_Q1"][h], Q1d[h], sem="so")
            b.dma("scalar", dbg["d_Q2"][h], Q2d[h], sem="so")

    b.barrier()

    if BUILD_PHASES == "p1":
        return early_out()

    ksb1 = arB[:][:, 0:8192]
    vsb = arC[:][:, 0:8192].rearrange("p (n d) -> p n d", d=128)
    qsb = arC[:][:, 8192:10240].rearrange("p (u t) -> p u t", u=2)
    ptb = arC[:][:, 10240:12288].rearrange("p (u t) -> p u t", u=2)
    rcb = arCf[:, 6144:7168]
    sumr = arEf[0:1, 0:1024]
    lnr = arEf[32:33, 0:1024]

    def attention(attn_i, Kl, Ql, av_out_fn, v_load_half, v_lhsT_fn, av_rows):
        two_mm = (av_rows == 128)
        head_s_fin = {}
        st0 = b.n("st")
        dkA = {0: b.dma("sync", ksb1[:, 0:4096], Kl[0][:, 0:4096], waits=[("st", st0)], sem="dKa")}
        dkB = {0: b.dma("sync", ksb1[:, 4096:8192], Kl[0][:, 4096:8192], sem="dKb")}
        dvA = {0: v_load_half(0, 0, st0)}
        dvB = {0: v_load_half(0, 1, st0)}
        for h in range(NH):
            u = h % 2
            wq_ld = [("st", head_s_fin[h - 2])] if h >= 2 else [("st", st0)]
            dq = b.dma("sync", qsb[:, u, :], Ql[h], waits=wq_ld, sem="dq%d" % u)
            bias = mhat[:, attn_i * 16 + h:attn_i * 16 + h + 1]
            sa_exp = {}
            st_av = {}
            prev_done_sv = b.n("sv")
            sa_head0 = b.n("sa")
            def emit_av(k):
                jmin = k // 8
                qoff = 128 * jmin
                pu = k % 2
                sg_m = sg_mask[k]
                pts = [qoff, 512, TPC] if qoff < 512 else [qoff, TPC]
                avch = [(a0, a1) for a0, a1 in zip(pts[:-1], pts[1:]) if a1 > a0]
                wa0 = [("sg", sg_m)]
                if k == 0:
                    wa0.append(("sv", prev_done_sv))
                    wa0.append(("dVa", dvA[h]))
                if k == 32:
                    wa0.append(("dVb", dvB[h]))
                for ci, (a0, a1) in enumerate(avch):
                    stopf = (a0 == qoff and k in (31, 63))
                    startf = (k == 0 and a0 in (0, 512))
                    wa = wa0 if ci == 0 else [("sa", sa_exp[k])]
                    if two_mm:
                        b.op("tensor", lambda E, k=k, a0=a0, a1=a1, pu=pu, qoff=qoff, stopf=stopf, startf=startf: E.matmul(
                            pC[0:av_rows, a0:a1], v_lhsT_fn(k), ptb[:, pu, a0 - qoff:a1 - qoff],
                            start=startf, stop=stopf),
                            waits=wa)
                        b.op("tensor", lambda E, k=k, a0=a0, a1=a1, pu=pu, qoff=qoff, stopf=stopf, startf=startf: E.matmul(
                            pD[0:1, a0:a1], ones_c[:], ptb[:, pu, a0 - qoff:a1 - qoff],
                            start=startf, stop=stopf))
                    else:
                        sums_fn = lambda E, k=k, a0=a0, a1=a1, pu=pu, qoff=qoff, stopf=stopf, startf=startf: E.matmul(
                            pC[0:33, a0:a1], ones32[:], ptb[:, pu, a0 - qoff:a1 - qoff],
                            start=startf, stop=stopf)
                        av_fn = lambda E, k=k, a0=a0, a1=a1, pu=pu, qoff=qoff: E.matmul(
                            pC[0:av_rows, a0:a1], v_lhsT_fn(k), ptb[:, pu, a0 - qoff:a1 - qoff],
                            start=False, stop=False)
                        if k == 0:
                            b.op("tensor", sums_fn, waits=wa)
                            b.op("tensor", av_fn)
                        else:
                            b.op("tensor", av_fn, waits=wa)
                            b.op("tensor", sums_fn)
                st_av[k] = b.op("tensor", None, inc="st")

            sg_mask = {}
            for k in range(NKT):
                jmin = k // 8
                qoff = 128 * jmin
                nq = TPC - qoff
                pu = k % 2
                ps_k = (pA if pu == 0 else pB)
                w0 = [("dq%d" % u, dq)]
                if k == 0:
                    w0.append(("dKa", dkA[h]))
                elif k == 32:
                    w0.append(("dKb", dkB[h]))
                if k >= 2:
                    w0.append(("sa", sa_exp[k - 2]))
                else:
                    w0.append(("sa", sa_head0))
                    if not two_mm:
                        w0.append(("sv", prev_done_sv))
                nch = (nq + 511) // 512
                for ch in range(nch):
                    c0 = 512 * ch
                    c1 = min(nq, c0 + 512)
                    b.op("tensor", lambda E, u=u, k=k, c0=c0, c1=c1, ps_k=ps_k, qoff=qoff: E.matmul(
                        ps_k[:, c0:c1], ksb1[:, 128 * k:128 * (k + 1)],
                        qsb[:, u, qoff + c0:qoff + c1], start=True, stop=True),
                        waits=w0 if ch == 0 else ())
                st_s = b.op("tensor", None, inc="st")
                if k == 31:
                    st_s31 = st_s
                if k == NKT - 1:
                    head_s_fin[h] = st_s
                we = [("st", st_s)]
                if k >= 2:
                    we.append(("st", st_av[k - 2]))
                sa_exp[k] = b.op("scalar", lambda E, pu=pu, nq=nq, ps_k=ps_k, bias=bias: E.activation(
                    ptb[:, pu, 0:nq], ps_k[:, 0:nq], AF.Exp, bias=bias, scale=C),
                    waits=we, inc="sa")
                sg_mask[k] = b.op("gpsimd", lambda E, pu=pu, k=k: E.tensor_tensor(
                    ptb[:, pu, 0:128], ptb[:, pu, 0:128], mask_sb[:, k, :], ALU.mult),
                    waits=[("sa", sa_exp[k])], inc="sg")
                if k >= 1:
                    emit_av(k - 1)
                if k == 32 and h + 1 < NH:
                    dkA[h + 1] = b.dma("sync", ksb1[:, 0:4096], Kl[h + 1][:, 0:4096],
                                       waits=[("st", st_s31)], sem="dKa")
                    dvA[h + 1] = v_load_half(h + 1, 0, st_av[31])
            emit_av(NKT - 1)
            if h + 1 < NH:
                dkB[h + 1] = b.dma("sync", ksb1[:, 4096:8192], Kl[h + 1][:, 4096:8192],
                                   waits=[("st", head_s_fin[h])], sem="dKb")
                dvB[h + 1] = v_load_half(h + 1, 1, st_av[NKT - 1])
            sum_src = pD[0:1, 0:1024] if two_mm else pC[32:33, 0:1024]
            sv_s = b.op("vector", lambda E, sum_src=sum_src: E.tensor_copy(sumr, sum_src),
                        waits=[("st", st_av[NKT - 1])], inc="sv")
            sa_l = b.op("scalar", lambda E: E.activation(lnr, sumr, AF.Ln),
                        waits=[("sv", sv_s)], inc="sa")
            sa_r = b.op("scalar", lambda E: E.activation(sumr, lnr, AF.Exp, scale=-1.0),
                        waits=[("sa", sa_l)], inc="sa")
            pbc = pD if two_mm else pA
            b.op("tensor", lambda E, pbc=pbc: E.matmul(pbc[:, 0:512], ones_r[:], sumr[0:1, 0:512],
                                                       start=True, stop=True),
                 waits=[("sa", sa_r)])
            st_b = b.op("tensor", lambda E, pbc=pbc: E.matmul(pbc[:, 512:1024], ones_r[:],
                                                              sumr[0:1, 512:1024],
                                                              start=True, stop=True), inc="st")
            sv_rb = b.op("vector", lambda E, pbc=pbc: E.tensor_copy(rcb, pbc[:, 0:1024]),
                         waits=[("st", st_b)], inc="sv")
            av_out_fn(h, sv_rb)

    def av1s(h):
        return arXv[:, h, 1024:2048]

    def v1_load(h, half, stw):
        return b.dma("scalar", vsb[:, 32 * half:32 * (half + 1), :],
                     V1d[h, 4096 * half:4096 * (half + 1), :].rearrange("(n p) d -> p n d", p=128),
                     waits=[("st", stw)], sem="dV%s" % "ab"[half])

    def av1_out(h, sv_rb):
        b.op("vector", lambda E, h=h: E.tensor_tensor(
            av1s(h), pC[:, 0:1024], rcb, ALU.mult),
            waits=[("sv", sv_rb)], inc="sv")

    attention(0, K1l, Q1l, av1_out, v1_load, lambda k: vsb[:, k, :], 128)

    def av2_out(h, sv_rb):
        svt = b.op("vector", lambda E: E.tensor_tensor(
            msc, pC[0:16, 0:1024], rcb[0:16, :], ALU.mult),
            waits=[("sv", sv_rb), ("st", b.n("st"))], inc="sv")
        b.op("tensor", lambda E, h=h: E.matmul(
            pD[0:16, 0:512], idf16[:], msc[:, 0:512], start=(h == 0), stop=(h == NH - 1)),
            waits=[("sv", svt)])
        b.op("tensor", lambda E, h=h: E.matmul(
            pD[0:16, 512:1024], idf16[:], msc[:, 512:1024], start=(h == 0), stop=(h == NH - 1)),
            inc="st")
        if h == NH - 1:
            b.op("vector", lambda E: E.tensor_copy(modpre, pD[0:16, 0:1024]),
                 waits=[("st", b.n("st"))], inc="sv")

    v2sb = arC[:][:, 14336:14336 + NKT * 16].rearrange("p (n r) -> p n r", r=16)

    def v2_load(h, half, stw):
        return b.dma("scalar", v2sb[:, 32 * half:32 * (half + 1), :],
                     V2d[h, :, 32 * half:32 * (half + 1), :],
                     waits=[("st", stw)], sem="dV%s" % "ab"[half])

    attention(1, K2l, Q2l, av2_out, v2_load, lambda k: v2sb[:, k, :], 16)

    if DEBUG_OUTS:
        sv_all = b.n("sv")
        for h in range(NH):
            b.dma("sync", dbg["d_av1"].rearrange("p h t -> p h t")[:, h, :], av1s(h),
                  waits=[("sv", sv_all)], sem="so")
        b.dma("sync", dbg["d_mod"], modpre[:], waits=[("sv", sv_all)], sem="so")

    b.barrier()

    wofu = {0: arCf[:, 0:512], 1: arCf[:, 4608:5120]}
    wob = arC[:][:, 1024:1024 + NT * 512].rearrange("p (c o) -> p c o", o=512)
    wev = {0: b.n("sv"), 1: b.n("sv")}
    pev = {0: b.n("sv"), 1: b.n("sv")}
    for ds in range(4):
        stg = b.n("st")
        for i in range(NT):
            wu = i % 2
            dli = b.dma(QU[wu], wofu[wu], wout[128 * i:128 * (i + 1), 512 * ds:512 * (ds + 1)],
                        waits=[("sv", wev[wu])] + ([("st", stg)] if i < 2 else []),
                        sem="dwo%d" % wu)
            wev[wu] = b.op("vector", lambda E, i=i, wu=wu: E.tensor_copy(wob[:, i, :], wofu[wu]),
                           waits=[("dwo%d" % wu, dli)], inc="sv")
        sv_w = b.n("sv")
        for tt in range(NSLOT):
            pu = tt % 2
            pdst = (pA if pu == 0 else pB)[:, 0:512]
            w0 = [("sv", sv_w), ("sv", pev[pu])]
            for i in range(NT):
                b.op("tensor", lambda E, i=i, tt=tt, pdst=pdst: E.matmul(
                    pdst, arXv[:, i, 1024 + 128 * tt:1024 + 128 * (tt + 1)], wob[:, i, :],
                    start=(i == 0), stop=(i == NT - 1)),
                    waits=w0 if i == 0 else ())
            stt = b.op("tensor", None, inc="st")
            pev[pu] = b.op("vector", lambda E, tt=tt, ds=ds, pdst=pdst: E.tensor_tensor(
                accT[tt][:, 512 * ds:512 * (ds + 1)], accT[tt][:, 512 * ds:512 * (ds + 1)],
                pdst, ALU.add), waits=[("st", stt)], inc="sv")
    b.barrier()

    arXu = arX[:].rearrange("p (c u t) -> p c u t", u=2, t=1024)

    def winX(dc):
        return arXu[:, dc // 2, 1, 512 * (dc % 2):512 * (dc % 2) + 512]

    def wupX(dc):
        return arXu[:, 8 + dc // 2, 1, 512 * (dc % 2):512 * (dc % 2) + 512]

    wdnT = arC[:][:, 8192:8192 + 4 * 2048].rearrange("p (c o) -> p c o", o=2048)
    g_all = arB[:][:, 8192:12288].rearrange("p (t f) -> p t f", f=512)
    stg8 = arCf[:, 0:4096].rearrange("p (u d) -> p u d", u=8)
    bsl = arB[:][0:16, 4096:4608]
    gubf = arB[:][:, 6144:6656]
    guT = arB[:][:, 7168:7680].rearrange("p (q t) -> p q t", q=4)
    prod_bf = arB[:][0:16, 4608:4608 + TPC]

    sa_t = b.op("scalar", lambda E: E.activation(msc, modpre, AF.Tanh), inc="sa")
    sv_p = b.op("vector", lambda E: E.scalar_tensor_tensor(
        prod_bf[:], hAr, 0.1, msc, ALU.mult, ALU.mult), waits=[("sa", sa_t)], inc="sv")

    ptev5 = {0: b.n("sv"), 1: b.n("sv")}
    rowst = {u: 0 for u in range(4)}
    colst = {u: 0 for u in range(4)}
    gaev = {0: b.n("sa"), 1: b.n("sa")}
    upev = {0: b.n("sv"), 1: b.n("sv")}
    mev = {0: b.n("sv"), 1: b.n("sv")}
    p5_st0 = b.n("st")

    chainq = []

    def row_dma(wsrc, fs, dc):
        u = dc % 4
        dli = b.dma(QU[dc % 2], stg8[:, u, :].rearrange("p (q d) -> p q d", q=4),
                    wsrc[512 * fs:512 * (fs + 1), 128 * dc:128 * (dc + 1)].rearrange(
                        "(q p) d -> p q d", p=128),
                    waits=[("st", max(rowst[u], p5_st0))], sem="dws%d" % u)
        return dli

    def col_dma(wsrc, fs, do):
        u = do % 4
        dli = b.dma(QU[1 - do % 2], stg8[:, 4 + u, :],
                    wsrc[128 * do:128 * (do + 1), 512 * fs:512 * (fs + 1)],
                    waits=[("st", max(colst[u], p5_st0))], sem="dwd%d" % u)
        return dli

    def row_fin(dstf, dc, dli):
        u = dc % 4
        h = dc % 2
        for q in range(4):
            b.op("tensor", lambda E, q=q, u=u, h=h: E.transpose(
                pD[:, 512 * h + q * 128:512 * h + (q + 1) * 128],
                stg8[:, u, 128 * q:128 * (q + 1)], id_f32[:]),
                waits=[("dws%d" % u, dli), ("sv", ptev5[h])] if q == 0 else ())
        stt = b.op("tensor", None, inc="st")
        rowst[u] = stt
        ptev5[h] = b.op("vector", lambda E, dc=dc, dstf=dstf, h=h: E.tensor_scalar(
            dstf(dc), pD[:, 512 * h:512 * h + 512], ln2c[:, dc:dc + 1], None, ALU.mult),
            waits=[("st", stt)], inc="sv")

    def col_fin(do, dli):
        u = do % 4
        h = do % 2
        for fc in range(4):
            b.op("tensor", lambda E, fc=fc, u=u, h=h: E.transpose(
                pD[:, 512 * h + fc * 128:512 * h + (fc + 1) * 128],
                stg8[:, 4 + u, 128 * fc:128 * (fc + 1)], id_f32[:]),
                waits=[("dwd%d" % u, dli), ("sv", ptev5[h])] if fc == 0 else ())
        stt = b.op("tensor", None, inc="st")
        colst[u] = stt
        ptev5[h] = b.op("vector", lambda E, do=do, h=h: E.tensor_copy(
            wdnT[:, :, 128 * do:128 * (do + 1)],
            pD[:, 512 * h:512 * h + 512].rearrange("p (c o) -> p c o", o=128)),
            waits=[("st", stt)], inc="sv")

    def q_row(wsrc, dstf, fs, dc):
        chainq.append(("r", dstf, dc, row_dma(wsrc, fs, dc)))

    def q_col(wsrc, fs, do):
        chainq.append(("c", None, do, col_dma(wsrc, fs, do)))

    def flush_chains():
        for kind, dstf, dc, dli in chainq:
            if kind == "r":
                row_fin(dstf, dc, dli)
            else:
                col_fin(dc, dli)
        chainq.clear()

    def build_row_dc(wsrc, dstf, fs, dc):
        row_fin(dstf, dc, row_dma(wsrc, fs, dc))

    silu_mark = [0] * NSLOT
    for dc in range(NT):
        build_row_dc(win, winX, 0, dc)
    win_ready = b.n("sv")
    for fs in range(NT):
        gv0 = b.n("sv")
        dlb = b.dma("scalar", bsl, bbd[:, 512 * fs:512 * (fs + 1)],
                    waits=[("st", b.n("st"))], sem="dbl")
        for tt in range(NSLOT):
            q_col(wdn, fs, 2 * tt)
            q_col(wdn, fs, 2 * tt + 1)
            q_row(wup, wupX, fs, 2 * tt)
            q_row(wup, wupX, fs, 2 * tt + 1)
            pu = tt % 2
            pg = (pA if pu == 0 else pB)[:, 0:512]
            w0 = [("sv", win_ready), ("sv", sv_p), ("dbl", dlb), ("sa", gaev[pu])]
            for c in range(NT):
                b.op("tensor", lambda E, c=c, tt=tt, pg=pg: E.matmul(
                    pg, arXv[:, c, 128 * tt:128 * (tt + 1)], winX(c),
                    start=(c == 0), stop=False),
                    waits=w0 if c == 0 else ())
            b.op("tensor", lambda E, tt=tt, pg=pg: E.matmul(
                pg, prod_bf[:, 128 * tt:128 * (tt + 1)], bsl,
                start=False, stop=True))
            st_g = b.op("tensor", None, inc="st")
            gaev[pu] = b.op("scalar", lambda E, tt=tt, pg=pg: E.activation(
                g_all[:, tt, :], pg, AF.Silu, scale=stats[:, tt, 13:14]),
                waits=[("st", st_g), ("sv", gv0)], inc="sa")
            silu_mark[tt] = gaev[pu]
            flush_chains()
        wup_ready = b.n("sv")
        for tt in range(NSLOT):
            pu = tt % 2
            pg = (pA if pu == 0 else pB)[:, 512:1024]
            w0 = [("sv", wup_ready), ("sv", upev[pu])]
            for c in range(NT):
                b.op("tensor", lambda E, c=c, tt=tt, pg=pg: E.matmul(
                    pg, arXv[:, c, 128 * tt:128 * (tt + 1)], wupX(c),
                    start=(c == 0), stop=(c == NT - 1)),
                    waits=w0 if c == 0 else ())
            st_u = b.op("tensor", None, inc="st")
            sv_gu = b.op("vector", lambda E, tt=tt, pg=pg: E.scalar_tensor_tensor(
                gubf, g_all[:, tt, :], stats[:, tt, 13:14], pg, ALU.mult, ALU.mult),
                waits=[("sa", silu_mark[tt]), ("st", st_u)], inc="sv")
            upev[pu] = sv_gu
            for q in range(4):
                b.op("tensor", lambda E, q=q, pu=pu: E.transpose(
                    pT2[:, 1024 * pu + q * 128:1024 * pu + (q + 1) * 128],
                    gubf[:, 128 * q:128 * (q + 1)], id_bf[:]),
                    waits=[("sv", max(sv_gu, ptev5[pu]))] if q == 0 else ())
            st_t = b.op("tensor", None, inc="st")
            sv_gt = b.op("vector", lambda E, pu=pu: E.tensor_copy(
                guT[:].rearrange("p q t -> p (q t)"), pT2[:, 1024 * pu:1024 * pu + 512]),
                waits=[("st", st_t)], inc="sv")
            ptev5[pu] = sv_gt
            for ds in range(4):
                pu2 = ds % 2
                pdst = pC[:, 512 * pu2:512 * (pu2 + 1)]
                w1 = [("sv", sv_gt), ("sv", mev[pu2])]
                for fc in range(4):
                    b.op("tensor", lambda E, fc=fc, ds=ds, pdst=pdst: E.matmul(
                        pdst, guT[:, fc, :], wdnT[:, fc, 512 * ds:512 * (ds + 1)],
                        start=(fc == 0), stop=(fc == 3)),
                        waits=w1 if fc == 0 else ())
                stt = b.op("tensor", None, inc="st")
                mev[pu2] = b.op("vector", lambda E, tt=tt, ds=ds, pdst=pdst: E.tensor_tensor(
                    accT[tt][:, 512 * ds:512 * (ds + 1)], accT[tt][:, 512 * ds:512 * (ds + 1)],
                    pdst, ALU.add), waits=[("st", stt)], inc="sv")
            flush_chains()
            if fs + 1 < NT:
                q_row(win, winX, fs + 1, 2 * tt)
                q_row(win, winX, fs + 1, 2 * tt + 1)
        flush_chains()
        win_ready = b.n("sv")

    sv_f = b.n("sv")
    for tt in range(NSLOT):
        b.dma("sync", out[128 * tt:128 * (tt + 1), :], accT[tt][:],
              waits=[("sv", sv_f)], sem="so")
    b.op("sync", None, waits=[("so", b.n("so"))])
    return _finish(nc, b)


def _finish(nc, b):
    import contextlib
    es = contextlib.ExitStack()
    sems = {}
    names = set()
    for streams in b.streams.values():
        for fn, waits, inc, inck, is_dma in streams:
            if inc:
                names.add(inc)
            for s, v in waits:
                names.add(s)
    with es:
        for nm in sorted(names):
            sems[nm] = es.enter_context(nc.semaphore(nm))
        block = es.enter_context(nc.Block())
        bodies = b.replay(sems)
        block.sync(bodies["sync"])
        block.vector(bodies["vector"])
        block.scalar(bodies["scalar"])
        block.tensor(bodies["tensor"])
        block.gpsimd(bodies["gpsimd"])
    return nc


_ROWS = {}


def _core_rows(c):
    if c not in _ROWS:
        _ROWS[c] = np.concatenate([np.arange(128 * (8 * j + c), 128 * (8 * j + c) + 128)
                                   for j in range(NSLOT)])
    return _ROWS[c]


def kernel(L0_attn_x, L0_attn_w_qkv, L0_attn_w_out, L0_ffn_ln1_w, L0_ffn_ln2_w,
           L0_ffn_Wq, L0_ffn_Wk, L0_ffn_Wv, L0_ffn_Wo, L0_ffn_A, L0_ffn_B,
           L0_ffn_W_in_base, L0_ffn_W_up_base, L0_ffn_W_down):
    f32 = np.float32
    x = np.ascontiguousarray(np.asarray(L0_attn_x, dtype=f32).reshape(SEQ, DM))
    common = dict(
        xf=x,
        wqkv=np.ascontiguousarray(np.asarray(L0_attn_w_qkv, dtype=f32)),
        wq2=np.ascontiguousarray(np.asarray(L0_ffn_Wq, dtype=f32)),
        wk2=np.ascontiguousarray(np.asarray(L0_ffn_Wk, dtype=f32)),
        wv2=np.ascontiguousarray(np.asarray(L0_ffn_Wv, dtype=f32)),
        wo2=np.ascontiguousarray(np.asarray(L0_ffn_Wo, dtype=f32)),
        aa=np.ascontiguousarray(np.asarray(L0_ffn_A, dtype=f32)),
        bbi=np.ascontiguousarray(np.asarray(L0_ffn_B, dtype=f32)),
        ln1r=np.ascontiguousarray(np.asarray(L0_ffn_ln1_w, dtype=f32).reshape(1, DM)),
        ln2r=np.ascontiguousarray(np.asarray(L0_ffn_ln2_w, dtype=f32).reshape(1, DM)),
        wout=np.ascontiguousarray(np.asarray(L0_attn_w_out, dtype=f32)),
        win=np.ascontiguousarray(np.asarray(L0_ffn_W_in_base, dtype=f32)),
        wup=np.ascontiguousarray(np.asarray(L0_ffn_W_up_base, dtype=f32)),
        wdn=np.ascontiguousarray(np.asarray(L0_ffn_W_down, dtype=f32)),
        ident=np.eye(128, dtype=f32),
        kidx=np.ascontiguousarray(np.arange(SEQ, dtype=f32).reshape(NKT, 128).T),
    )
    in_maps = []
    for c in range(NCORES):
        rows = _core_rows(c)
        m = dict(common)
        m["xo"] = np.ascontiguousarray(x[rows])
        m["qidx"] = np.ascontiguousarray(rows.astype(f32).reshape(1, TPC))
        in_maps.append(m)
    nc = build_nc()
    res = run_bass_kernel_spmd(nc, in_maps, core_ids=list(range(NCORES)))
    outp = np.empty((SEQ, DM), dtype=f32)
    for c in range(NCORES):
        outp[_core_rows(c)] = res.results[c]["out"]
    return outp.reshape(1, SEQ, DM)


# version 13
# speedup vs baseline: 1.9084x; 1.0631x over previous
import numpy as np
import ml_dtypes
import concourse.bass as bass
import concourse.mybir as mybir
from concourse.bass_utils import run_bass_kernel_spmd

F32 = mybir.dt.float32
BF16 = mybir.dt.bfloat16
FP8 = mybir.dt.float8e4
AF = mybir.ActivationFunctionType
ALU = mybir.AluOpType

NCORES = 8
SEQ = 8192
DM = 2048
FF = 8192
NH = 16
NT = 16
NSLOT = 8
TPC = 1024
NKT = 64
NPASS = 4
PTOK = 2048
NPP = 16
SCALE = float(1.0 / np.sqrt(128.0))
EPS_LN = 1e-5
EPS_RMS = 1e-6

ENGINES = ("sync", "vector", "scalar", "tensor", "gpsimd")

BUILD_PHASES = "all"
DEBUG_OUTS = False


class Sched:

    def __init__(self):
        self.streams = {e: [] for e in ENGINES}
        self.cnt = {}
        self.dma_sems = set()

    def n(self, sem):
        return self.cnt.get(sem, 0)

    def op(self, eng, fn=None, waits=(), inc=None, inck=1, is_dma=False):
        if fn is None and inc is not None and not waits:
            st = self.streams[eng]
            if st and st[-1][0] is not None and st[-1][2] == inc:
                return self.cnt.get(inc, 0)
        if inc is not None:
            self.cnt[inc] = self.cnt.get(inc, 0) + inck
        ws = tuple((s, v) for (s, v) in waits if v > 0)
        self.streams[eng].append((fn, ws, inc, inck, is_dma))
        return self.cnt.get(inc, 0) if inc else 0

    def dma(self, eng, out, in_, waits=(), sem=None, slow=False):
        sem = sem or ("dl" if eng == "sync" else "dg")
        self.dma_sems.add(sem)
        if slow:
            fn = lambda E, o=out, i=in_: E.dma_start(out=o, in_=i,
                                                     allow_slow_non_contiguous=True)
        else:
            fn = lambda E, o=out, i=in_: E.dma_start(out=o, in_=i)
        return self.op(eng, fn, waits=waits, inc=sem, inck=16, is_dma=True)

    def barrier(self):
        marks = []
        for e in ENGINES:
            st = self.streams[e]
            if not st:
                continue
            fn, ws, inc, inck, isd = st[-1]
            if inc is not None and not isd:
                marks.append((inc, self.n(inc)))
            elif fn is not None and not isd:
                m = self.op(e, None, inc="b_" + e)
                marks.append(("b_" + e, m))
        marks += [(s, self.n(s)) for s in sorted(self.dma_sems)]
        for e in ENGINES:
            self.op(e, None, waits=marks)

    def replay(self, sems):
        def make(name):
            stream = self.streams[name]

            def body(E):
                last_real = None
                last_was_dma = False
                for fn, waits, inc, inck, is_dma in stream:
                    for s, v in waits:
                        E.wait_ge(sems[s], v)
                    if fn is not None:
                        ins = fn(E)
                        last_real = ins
                        last_was_dma = is_dma
                        if inc is not None:
                            ins.then_inc(sems[inc], inck)
                    elif inc is not None:
                        if last_real is not None and not (last_was_dma and inck % 16):
                            last_real.then_inc(sems[inc], inck)
                        else:
                            E.nop().then_inc(sems[inc], inck)
            return body
        return {e: make(e) for e in ENGINES}


def build_nc():
    nc = bass.Bass("TRN2", target_bir_lowering=False, dynamic_dma_scratch_size=512)
    b = Sched()
    QU = {0: "sync", 1: "scalar"}
    QO = {0: "scalar", 1: "sync"}
    C = SCALE

    dt = nc.dram_tensor
    xf = dt("xf", [SEQ, DM], F32, kind="ExternalInput").ap()
    xo = dt("xo", [TPC, DM], F32, kind="ExternalInput").ap()
    qidx = dt("qidx", [1, TPC], F32, kind="ExternalInput").ap()
    kidx = dt("kidx", [128, NKT], F32, kind="ExternalInput").ap()
    wqkv = dt("wqkv", [DM, 3 * DM], F32, kind="ExternalInput").ap()
    wq2 = dt("wq2", [DM, DM], F32, kind="ExternalInput").ap()
    wk2 = dt("wk2", [DM, DM], F32, kind="ExternalInput").ap()
    wv2 = dt("wv2", [DM, DM], F32, kind="ExternalInput").ap()
    wo2 = dt("wo2", [DM, DM], F32, kind="ExternalInput").ap()
    aa = dt("aa", [DM, 16], F32, kind="ExternalInput").ap()
    bbi = dt("bbi", [16, FF], F32, kind="ExternalInput").ap()
    ln1r = dt("ln1r", [1, DM], F32, kind="ExternalInput").ap()
    ln2r = dt("ln2r", [1, DM], F32, kind="ExternalInput").ap()
    wout = dt("wout", [DM, DM], F32, kind="ExternalInput").ap()
    win = dt("win", [FF, DM], F32, kind="ExternalInput").ap()
    wup = dt("wup", [FF, DM], F32, kind="ExternalInput").ap()
    wdn = dt("wdn", [DM, FF], F32, kind="ExternalInput").ap()
    ident = dt("ident", [128, 128], F32, kind="ExternalInput").ap()
    out = dt("out", [TPC, DM], F32, kind="ExternalOutput").ap()

    k1w = dt("k1w", [DM, DM], BF16, kind="Internal").ap()
    v1w = dt("v1w", [DM, DM], BF16, kind="Internal").ap()
    q1w = dt("q1w", [DM, DM], BF16, kind="Internal").ap()
    woutb = dt("woutb", [DM, DM], BF16, kind="Internal").ap()
    wq2t = dt("wq2t", [DM, DM], BF16, kind="Internal").ap()
    wk2t = dt("wk2t", [DM, DM], BF16, kind="Internal").ap()
    bbd = dt("bbd", [16, FF], BF16, kind="Internal").ap()
    wvwoad = dt("wvwoad", [DM, 256], BF16, kind="Internal").ap()
    V2d = dt("V2d", [NH, 128, NKT, 16], BF16, kind="Internal").ap()
    K1d = dt("K1d", [NH, 128, SEQ], BF16, kind="Internal").ap()
    V1d = dt("V1d", [NH, SEQ, 128], BF16, kind="Internal").ap()
    K2d = dt("K2d", [NH, 128, SEQ], BF16, kind="Internal").ap()
    Q1d = dt("Q1d", [NH, 128, TPC], BF16, kind="Internal").ap()
    Q2d = dt("Q2d", [NH, 128, TPC], BF16, kind="Internal").ap()

    dbg = {}
    if DEBUG_OUTS:
        for nm, shp in [("d_K1", [NH, 128, SEQ]), ("d_V1", [NH, SEQ, 128]),
                        ("d_K2", [NH, 128, SEQ]), ("d_Q1", [NH, 128, TPC]),
                        ("d_Q2", [NH, 128, TPC]), ("d_V2p", [NH, 128, NKT, 16]),
                        ("d_av1", [128, NH, TPC]), ("d_mod", [16, TPC]),
                        ("d_acc0", [TPC, DM]), ("d_mhat", [128, 32]),
                        ("d_hA", [16, TPC]), ("d_k1w", [DM, DM]), ("d_wq2t", [DM, DM]),
                        ("d_idbf", [128, 128]), ("d_idf", [128, 128]),
                        ("d_xnTo", [128, NT, TPC]), ("d_arE", [128, 2048])]:
            dbg[nm] = dt(nm, shp, F32 if nm in ("d_acc0", "d_mhat", "d_mod", "d_hA", "d_idf") else BF16,
                         kind="ExternalOutput").ap()

    sbA = nc.alloc_sbuf_tensor
    id_bf = sbA("id_bf", [128, 128], BF16)
    ones_r = sbA("ones_r", [1, 128], F32)
    ones_c = sbA("ones_c", [128, 1], BF16)
    kidx_sb = sbA("kidx_sb", [128, NKT], F32)
    a2_bf = sbA("a2_bf", [128, NT, 16], BF16)
    ln1b = sbA("ln1b", [128, DM], BF16)
    id_f32 = sbA("id_f32", [128, 128], F32)
    ln1c = sbA("ln1c", [128, 16], F32)
    ln2c = sbA("ln2c", [128, 16], F32)
    mhat = sbA("mhat", [128, 32], F32)
    nrm4 = sbA("nrm4", [1, 64], F32)
    ones32 = sbA("ones32", [128, 33], BF16)
    modpre = sbA("modpre", [16, TPC], F32)[:]
    msc = sbA("msc", [16, TPC], F32)[:]
    hAr = sbA("hAr", [16, TPC], F32)[:]
    idf16 = sbA("idf16", [16, 16], F32)
    stats = sbA("stats", [128, NPP, 16], F32)
    mrow = sbA("mrow", [1, 32], F32)
    cst = sbA("cst", [128, 4], F32)
    arX = sbA("arX", [128, NT * 2048], BF16)
    arB = sbA("arB", [128, 15360], BF16)
    arC = sbA("arC", [128, 16384], BF16)
    arD = sbA("arD", [128, 4096], BF16)
    arE = sbA("arE", [128, 2048], BF16)
    accT = [sbA(f"acc{i}", [128, DM], F32) for i in range(NSLOT)]

    pA = nc.alloc_psum_tensor("pA", [128, 1024], F32)
    pB = nc.alloc_psum_tensor("pB", [128, 1024], F32)
    pC = nc.alloc_psum_tensor("pC", [128, 1024], F32)
    pD = nc.alloc_psum_tensor("pD", [128, 1024], F32)

    arBf = arB[:].bitcast(F32)
    arCf = arC[:].bitcast(F32)
    arEf = arE[:].bitcast(F32)
    arXf = arX[:].bitcast(F32)
    qib = arXf[:, 1024:2048]
    pT2 = pD[:].bitcast(BF16)

    id_f = id_f32[:]
    b.dma("sync", id_f, ident, sem="di")
    b.dma("sync", idf16[:], ident[0:16, 0:16], sem="di")
    b.dma("sync", kidx_sb[:], kidx, sem="di")
    b.dma("sync", ln1c[:], ln1r.rearrange("o (c p) -> p (o c)", p=128), sem="di", slow=True)
    b.dma("sync", ln2c[:], ln2r.rearrange("o (c p) -> p (o c)", p=128), sem="di", slow=True)
    aa_sb = arCf[:, 0:NT * 16].rearrange("p (c r) -> p c r", r=16)
    b.dma("sync", aa_sb, aa.rearrange("(c p) r -> p c r", p=128), sem="di")
    qi_row = arXf[0:1, 0:TPC]
    b.dma("sync", qi_row, qidx, sem="di")
    l1_row = arBf[0:1, 0:DM]
    b.dma("sync", l1_row, ln1r, sem="di")
    dl0 = b.n("di")
    b.op("vector", lambda E: E.tensor_copy(id_bf[:], id_f), waits=[("di", dl0)])
    b.op("vector", lambda E: E.memset(ones_r[:], 1.0))
    b.op("vector", lambda E: E.memset(ones_c[:], 1.0))
    b.op("vector", lambda E: E.memset(cst[:, 0:1], EPS_LN))
    b.op("vector", lambda E: E.memset(cst[:, 1:2], EPS_RMS))
    b.op("vector", lambda E: E.memset(cst[:, 2:3], 1.0))
    svz = b.op("vector", lambda E: E.memset(ones32[:], 0.0), inc="sv")
    b.op("vector", lambda E: E.memset(ones32[:, 32:33], 1.0), waits=[("sv", svz)])
    sv0 = b.op("vector", None, inc="sv")
    b.op("tensor", lambda E: E.matmul(pD[:, 0:512], ones_r[:], qi_row[0:1, 0:512], start=True, stop=True),
         waits=[("sv", sv0)])
    b.op("tensor", lambda E: E.matmul(pD[:, 512:1024], ones_r[:], qi_row[0:1, 512:1024], start=True, stop=True))
    for _h in range(2):
        b.op("tensor", lambda E, _h=_h: E.matmul(pA[:, 512 * _h:512 * (_h + 1)], ones_r[:],
                                                 l1_row[0:1, 512 * _h:512 * (_h + 1)], start=True, stop=True))
    for _h in range(2):
        b.op("tensor", lambda E, _h=_h: E.matmul(pB[:, 512 * _h:512 * (_h + 1)], ones_r[:],
                                                 l1_row[0:1, 1024 + 512 * _h:1024 + 512 * (_h + 1)], start=True, stop=True))
    st1b = b.op("tensor", None, inc="st")
    sv_q = b.op("vector", lambda E: E.tensor_copy(qib[:], pD[:, 0:TPC]), waits=[("st", st1b)], inc="sv")
    b.op("vector", lambda E: E.tensor_copy(ln1b[:, 0:1024], pA[:]))
    svl = b.op("vector", lambda E: E.tensor_copy(ln1b[:, 1024:2048], pB[:]), inc="sv")
    mask_sb = arD[:].bitcast(FP8).rearrange("p (k q) -> p k q", q=128)
    for k in range(NKT):
        j = k // 8
        b.op("vector", lambda E, k=k, j=j: E.tensor_scalar(
            mask_sb[:, k, :], qib[:, j * 128:(j + 1) * 128], kidx_sb[:, k:k + 1], None, ALU.is_ge),
            waits=[("sv", sv_q)] if k == 0 else ())
    svm = b.op("vector", None, inc="sv")
    for i in range(NT):
        b.op("vector", lambda E, i=i: E.tensor_scalar(
            a2_bf[:, i, :], aa_sb[:, i, :], ln2c[:, i:i + 1], None, ALU.mult))
    b.op("vector", None, inc="sv")
    bstage_f = arBf[0:16, 0:1024]
    bstage_b = arB[:][0:16, 2048:3072]
    for q in range(8):
        dq = b.dma("sync", bstage_f, bbi[:, q * 1024:(q + 1) * 1024],
                   waits=[("sv", b.n("sv"))], sem="db")
        svb = b.op("vector", lambda E: E.tensor_copy(bstage_b, bstage_f),
                   waits=[("db", dq), ("dbo", b.n("dbo"))], inc="sv")
        b.dma("scalar", bbd[:, q * 1024:(q + 1) * 1024], bstage_b, waits=[("sv", svb)], sem="dbo")
    woa_f = arCf[:, NT * 16:2 * NT * 16].rearrange("p (c r) -> p c r", r=16)
    wsl = arBf[:, 2048:6144].rearrange("p (u d) -> p u d", u=2)
    pwoa = pC[:, 0:256].rearrange("p (c r) -> p c r", r=16)
    for i in range(NT):
        dli = b.dma(QU[i % 2], wsl[:, i % 2, :], wo2[128 * i:128 * (i + 1), :],
                    waits=[("st", b.n("st") - 1)] if i >= 2 else [("sv", svl)],
                    sem="dw%d" % (i % 2))
        w0 = [("dw%d" % (i % 2), dli), ("sv", svm)]
        for j in range(NT):
            b.op("tensor", lambda E, i=i, j=j: E.matmul(
                pwoa[:, j, :], wsl[:, i % 2, 128 * j:128 * (j + 1)], aa_sb[:, i, :],
                start=(i == 0 and j == 0), stop=(i == NT - 1 and j == NT - 1)),
                waits=w0 if j == 0 else ())
        b.op("tensor", None, inc="st")
    stw = b.n("st")
    svw = b.op("vector", lambda E: E.tensor_copy(woa_f[:], pwoa[:]), waits=[("st", stw)], inc="sv")
    wvstage = arC[:][:, 7168:11264].rearrange("p (c x) -> p c x", x=256)
    for i in range(NT):
        dli = b.dma(QU[i % 2], wsl[:, i % 2, :], wv2[128 * i:128 * (i + 1), :],
                    waits=[("st", b.n("st") - 1)] if i >= 2 else [("st", b.n("st"))],
                    sem="dw%d" % (i % 2))
        svf = b.op("vector", lambda E, i=i: E.tensor_tensor(
            wsl[:, i % 2, :], wsl[:, i % 2, :], ln1b[:], ALU.mult),
            waits=[("dw%d" % (i % 2), dli), ("sv", svl)], inc="sv")
        w0 = [("sv", svf), ("sv", svw), ("sv", b.n("sv"))]
        for j in range(NT):
            b.op("tensor", lambda E, i=i, j=j: E.matmul(
                pwoa[:, j, :], wsl[:, i % 2, 128 * j:128 * (j + 1)], woa_f[:, i, :],
                start=True, stop=True),
                waits=w0 if j == 0 else ())
        stt = b.op("tensor", None, inc="st")
        b.op("vector", lambda E, i=i: E.tensor_copy(wvstage[:, :, 16 * i:16 * (i + 1)], pwoa[:]),
             waits=[("st", stt)], inc="sv")
    svw2 = b.op("vector", None, inc="sv")
    b.dma("scalar", wvwoad.rearrange("(c p) x -> p c x", p=128), wvstage[:],
          waits=[("sv", svw2)], sem="dwv")

    cstage = arC[:][:, 2048:6144].rearrange("p (u d) -> p u d", u=2)
    it = 0
    for i in range(NT):
        for part, (wsrc, wdst) in enumerate([
                (wqkv[128 * i:128 * (i + 1), 0:DM], q1w[128 * i:128 * (i + 1), :]),
                (wqkv[128 * i:128 * (i + 1), DM:2 * DM], k1w[128 * i:128 * (i + 1), :]),
                (wqkv[128 * i:128 * (i + 1), 2 * DM:3 * DM], v1w[128 * i:128 * (i + 1), :]),
                (wout[128 * i:128 * (i + 1), :], woutb[128 * i:128 * (i + 1), :])]):
            u = it % 2
            dli = b.dma(QU[u], wsl[:, u, :], wsrc,
                        waits=[("sv", b.n("sv") - 1)] if it >= 2 else [("sv", svw2)],
                        sem="dw%d" % u)
            svc = b.op("vector", lambda E, u=u: E.tensor_copy(cstage[:, u, :], wsl[:, u, :]),
                       waits=[("dw%d" % u, dli), ("dc%d" % u, b.n("dc%d" % u))],
                       inc="sv")
            b.dma(QO[u], wdst, cstage[:, u, :], waits=[("sv", svc)], sem="dc%d" % u)
            it += 1
    tstage = arC[:][:, 6144:7168].rearrange("p (u d) -> p u d", u=2)
    for wsrc, wdst in [(wq2, wq2t), (wk2, wk2t)]:
        for i in range(NT):
            u = i % 2
            dli = b.dma(QU[u], wsl[:, u, :], wsrc[128 * i:128 * (i + 1), :],
                        waits=[("st", b.n("st") - 1)] if i >= 2 else [("sv", b.n("sv"))],
                        sem="dw%d" % u)
            svc = b.op("vector", lambda E, u=u: E.tensor_tensor(
                cstage[:, u, :], wsl[:, u, :], ln1b[:], ALU.mult),
                waits=[("dw%d" % u, dli), ("dc%d" % u, b.n("dc%d" % u))], inc="sv")
            for g in range(4):
                for jj in range(4):
                    j = 4 * g + jj
                    ws = []
                    if jj == 0:
                        ws = [("sv", max(svc, b.n("sv")))]
                    b.op("tensor", lambda E, u=u, j=j, jj=jj, g=g: E.transpose(
                        pT2[:, (g % 2) * 1024 + jj * 128:(g % 2) * 1024 + (jj + 1) * 128],
                        cstage[:, u, 128 * j:128 * (j + 1)], id_bf[:]),
                        waits=ws)
                stt = b.op("tensor", None, inc="st")
                uu = g % 2
                sve = b.op("vector", lambda E, uu=uu: E.tensor_copy(
                    tstage[:, uu, :], pT2[:, uu * 1024:uu * 1024 + 512]),
                    waits=[("st", stt), ("dt%d" % uu, b.n("dt%d" % uu))], inc="sv")
                b.dma(QO[uu],
                      wdst[512 * g:512 * (g + 1), 128 * i:128 * (i + 1)].rearrange(
                          "(jj p) o -> p jj o", p=128),
                      tstage[:, uu, :].rearrange("p (jj o) -> p jj o", o=128),
                      waits=[("sv", sve)], sem="dt%d" % uu)
    def early_out():
        if BUILD_PHASES in ("p0", "p1a1", "p1a"):
            for tt in range(NSLOT):
                b.op("vector", lambda E, tt=tt: E.memset(accT[tt][:], 0.0))
            b.op("vector", None, inc="sv")
        svq = b.n("sv")
        for tt in range(NSLOT):
            b.dma("sync", out[128 * tt:128 * (tt + 1), :], accT[tt][:],
                  waits=[("sv", svq)] if tt == 0 else (), sem="so")
        b.op("sync", None, waits=[("so", b.n("so"))])
        return _finish(nc, b)

    b.barrier()
    if BUILD_PHASES == "p0":
        return early_out()

    arXv = arX[:].rearrange("p (c t) -> p c t", t=2048)
    xnT = arXv
    xdb = arCf[:, 0:6144].rearrange("p (u d) -> p u d", u=3)
    v2stage = arC[:][:, 12288:16384].rearrange("p (n x) -> p n x", x=256)
    xsc = arCf[:, 4096:6144]
    sqf = arCf[:, 6144:8192]
    xnbu = {0: arE[:][:, 0:2048], 1: accT[7][:].bitcast(BF16)[:, 0:2048]}
    r1row = arEf[0:1, 0:1024]
    r1bc = ln1b[:]
    kwsl = arB[:][:, 0:6144].rearrange("p (u c o) -> p u c o", u=3, c=NT)
    vwsl = arB[:][:, 6144:14336].rearrange("p (u c o) -> p u c o", u=2, c=NT)
    estg = arB[:][:, 14336:15360].rearrange("p (u d) -> p u d", u=2)

    def stat_chain(tt, sv_sum):
        s = lambda i: stats[:, tt, i:i + 1]
        steps = [
            lambda E: E.activation(s(2), s(0), AF.Copy, scale=1.0 / DM),
            lambda E: E.activation(s(3), s(2), AF.Square),
            lambda E: E.activation(s(4), s(1), AF.Copy, scale=1.0 / DM),
            lambda E: E.activation(s(5), s(3), AF.Relu, scale=-1.0, bias=s(4)),
            lambda E: E.activation(s(6), s(5), AF.Ln, bias=cst[:, 0:1]),
            lambda E: E.activation(s(7), s(6), AF.Exp, scale=-0.5),
            lambda E: E.activation(s(8), s(6), AF.Exp, scale=-1.0),
            lambda E: E.activation(s(9), s(8), AF.Relu, scale=-EPS_LN, bias=cst[:, 2:3]),
            lambda E: E.activation(s(10), s(9), AF.Ln, bias=cst[:, 1:2]),
            lambda E: E.activation(s(13), s(10), AF.Exp, scale=-0.5),
        ]
        last = 0
        for i, fn in enumerate(steps):
            w = [("sv", sv_sum)] if i == 0 else [("sa", last)]
            last = b.op("scalar", fn, waits=w, inc="sa")
        return last

    def mk_lnstate(src, ntiles, own, guard_st):
        return dict(src=src, n=ntiles, own=own, guard=guard_st,
                    nslots=2 if own else 3,
                    ptev={0: b.n("sv"), 1: b.n("sv")},
                    xev={u: [("sv", b.n("sv")), ("sa", b.n("sa"))]
                         for u in range(2 if own else 3)},
                    tsv={}, bmark={}, sacd={}, a1n=0)

    def stage_a1(st, tt, extra=()):
        u = tt % st["nslots"]
        own = st["own"]
        wx = list(st["xev"][u]) + list(extra)
        dxa = b.dma("sync", xdb[:, u, 0:1024], st["src"][128 * tt:128 * (tt + 1), 0:1024],
                    waits=wx, sem="dxa%d" % u)
        dxb = b.dma("scalar", xdb[:, u, 1024:2048], st["src"][128 * tt:128 * (tt + 1), 1024:2048],
                    waits=wx, sem="dxb%d" % u)
        sv_sum = b.op("vector", lambda E, u=u, tt=tt: E.tensor_reduce(
            stats[:, tt, 0:1], xdb[:, u, :], mybir.AxisListType.X, ALU.add),
            waits=[("dxa%d" % u, dxa), ("dxb%d" % u, dxb)], inc="sv")
        sqw = [("dxa%d" % u, dxa), ("dxb%d" % u, dxb)]
        if own and tt < 2:
            sqw.append(("dvo", b.n("dvo")))
        b.op("scalar", lambda E, u=u, tt=tt, own=own: E.activation(
            sqf if own else accT[tt % 4][:], xdb[:, u, :], AF.Square,
            accum_out=stats[:, tt, 1:2]),
            waits=sqw)
        st["sacd"][tt] = stat_chain(tt, sv_sum)
        st["a1n"] = tt + 1

    def stage_a2(st, tt):
        u = tt % st["nslots"]
        own = st["own"]
        sa_c = st["sacd"][tt]
        s = lambda i, tt=tt: stats[:, tt, i:i + 1]
        if own:
            sv_xn = b.op("vector", lambda E, u=u, m=s(2), r=s(7): E.tensor_scalar(
                xsc, xdb[:, u, :], m, r, ALU.subtract, ALU.mult),
                waits=[("sa", sa_c), ("st", st["guard"])], inc="sv")
            wb = [("sv", sv_xn),
                  ("st", st["guard"] if tt == 0 else st["bmark"][tt - 1])]
            sv_xnb = b.op("vector", lambda E, xb=xnbu[0]: E.tensor_copy(xb, xsc),
                          waits=wb, inc="sv")
            sv_acc = b.op("vector", lambda E, u=u, tt=tt: E.tensor_tensor(
                accT[tt][:], xdb[:, u, :], xsc, ALU.add), waits=[("sv", sv_xnb)], inc="sv")
            st["xev"][u] = [("sv", sv_acc), ("sa", sa_c)]
            st["tsv"][tt] = sv_xnb
        else:
            xb = xnbu[tt % 2]
            wb = [("sa", sa_c),
                  ("st", st["guard"] if tt < 2 else st["bmark"][tt - 2])]
            sv_xnb = b.op("vector", lambda E, u=u, xb=xb, m=s(2), r=s(7): E.tensor_scalar(
                xb, xdb[:, u, :], m, r, ALU.subtract, ALU.mult),
                waits=wb, inc="sv")
            st["xev"][u] = [("sv", sv_xnb), ("sa", sa_c)]
            st["tsv"][tt] = sv_xnb

    def stage_b(st, tt):
        xb = xnbu[0] if st["own"] else xnbu[tt % 2]
        svr = st["tsv"][tt]
        ptev = st["ptev"]
        stt = 0
        for g in range(4):
            for jj in range(4):
                j = 4 * g + jj
                ws = []
                if jj == 0:
                    ws = [("sv", max(svr, ptev[g % 2]))]
                b.op("tensor", lambda E, j=j, jj=jj, g=g, xb=xb: E.transpose(
                    pT2[:, (g % 2) * 1024 + jj * 128:(g % 2) * 1024 + jj * 128 + 128],
                    xb[:, 128 * j:128 * (j + 1)], id_bf[:]),
                    waits=ws)
            stt = b.op("tensor", None, inc="st")
            ptev[g % 2] = b.op("vector", lambda E, g=g, tt=tt: E.tensor_copy(
                xnT[:, 4 * g:4 * g + 4, 128 * tt:128 * (tt + 1)],
                pT2[:, (g % 2) * 1024:(g % 2) * 1024 + 512].rearrange("p (c t) -> p c t", t=128)),
                waits=[("st", stt)], inc="sv")
        st["bmark"][tt] = stt

    def finish_pass(st):
        ntiles = st["n"]
        while st["a1n"] < min(st["nslots"], ntiles):
            stage_a1(st, st["a1n"])
        for tt in range(ntiles):
            stage_a2(st, tt)
            stage_b(st, tt)
            if tt + st["nslots"] < ntiles:
                stage_a1(st, tt + st["nslots"])
        ptev = st["ptev"]
        last_sa = st["sacd"][ntiles - 1]
        rc_mark = 0
        bc_mark = 0
        for rnd in range((ntiles + 7) // 8):
            ws0 = [("sa", last_sa), ("sv", max(ptev[0], ptev[1]))]
            if rnd > 0:
                ws0.append(("sv", rc_mark))
            for j in range(8):
                tt = 8 * rnd + j
                b.op("tensor", lambda E, tt=tt, j=j: E.transpose(
                    pD[0:1, 128 * j:128 * (j + 1)], stats[:, tt, 13:14], id_f32[:]),
                    waits=ws0 if j == 0 else ())
            sttr = b.op("tensor", None, inc="st")
            sv_row = b.op("vector", lambda E: E.tensor_copy(r1row, pD[0:1, 0:1024]),
                          waits=[("st", sttr)], inc="sv")
            for q in range(2):
                b.op("tensor", lambda E, q=q: E.matmul(
                    pD[:, 512 * q:512 * (q + 1)], ones_r[:],
                    r1row[0:1, 512 * q:512 * (q + 1)], start=True, stop=True),
                    waits=[("sv", sv_row)] if q == 0 else ())
            bc_mark = b.op("tensor", None, inc="st")
            rc_mark = b.op("vector", lambda E, rnd=rnd: E.tensor_copy(
                r1bc[:, 1024 * rnd:1024 * (rnd + 1)], pD[:, 0:1024]),
                waits=[("st", bc_mark)], inc="sv")
        return b.n("sv"), rc_mark, bc_mark

    def proj_fm(wsrc_bf, dst, sv_ready, col0, ntok, rescale_sv, prev_marks):
        ev = {0: b.n("sv"), 1: b.n("sv")}
        ntch = ntok // 512
        base_st = b.n("st")
        marks = []
        for oc in range(NT):
            u = oc % 3
            if oc >= 3:
                wld = [("st", marks[oc - 3])]
            elif prev_marks is not None:
                wld = [("st", prev_marks[u])]
            else:
                wld = [("st", base_st)]
            dla = b.dma("sync", kwsl[:, u, 0:8, :],
                        wsrc_bf[0:1024, 128 * oc:128 * (oc + 1)].rearrange("(c p) o -> p c o", p=128),
                        waits=wld, sem="dka%d" % u)
            dlb = b.dma("scalar", kwsl[:, u, 8:16, :],
                        wsrc_bf[1024:2048, 128 * oc:128 * (oc + 1)].rearrange("(c p) o -> p c o", p=128),
                        waits=wld, sem="dkb%d" % u)
            stt = 0
            for tch in range(ntch):
                pu = tch % 2
                pdst = (pA if pu == 0 else pB)[:, 0:512]
                w0 = [("dka%d" % u, dla), ("sv", sv_ready), ("sv", ev[pu])]
                for c in range(NT):
                    wc = w0 if c == 0 else ()
                    if c == 8 and tch == 0:
                        wc = [("dkb%d" % u, dlb)]
                    b.op("tensor", lambda E, u=u, c=c, tch=tch, pdst=pdst: E.matmul(
                        pdst, kwsl[:, u, c, :], xnT[:, c, 512 * tch:512 * (tch + 1)],
                        start=(c == 0), stop=(c == NT - 1)),
                        waits=wc)
                stt = b.op("tensor", None, inc="st")
                eu = tch % 2
                if rescale_sv is None:
                    sve = b.op("vector", lambda E, eu=eu, pdst=pdst: E.tensor_copy(
                        estg[:, eu, :], pdst),
                        waits=[("st", stt), ("de%d" % eu, b.n("de%d" % eu))], inc="sv")
                else:
                    sve = b.op("vector", lambda E, eu=eu, pdst=pdst, tch=tch: E.tensor_tensor(
                        estg[:, eu, :], pdst, r1bc[:, 512 * tch:512 * (tch + 1)], ALU.mult),
                        waits=[("st", stt), ("de%d" % eu, b.n("de%d" % eu)),
                               ("sv", rescale_sv)], inc="sv")
                ev[pu] = sve
                b.dma(QO[eu], dst[oc][:, col0 + 512 * tch:col0 + 512 * (tch + 1)],
                      estg[:, eu, :], waits=[("sv", sve)], sem="de%d" % eu)
            marks.append(stt)
        return marks

    K1l = [K1d[h] for h in range(NH)]
    K2l = [K2d[h] for h in range(NH)]
    pmarks = None
    vprev = None
    st_cur = mk_lnstate(xf[0:PTOK, :], NPP, False, b.n("st"))
    for p in range(NPASS):
        sv_ready, sv_r1, guard = finish_pass(st_cur)
        mkK1 = proj_fm(k1w, K1l, sv_ready, PTOK * p, PTOK, None, pmarks)
        ev = {0: b.n("sv"), 1: b.n("sv")}
        vmk = []
        for avs in range(8):
            u = avs % 2
            if avs >= 2:
                wld = [("st", vmk[avs - 2])]
            elif vprev is not None:
                wld = [("st", vprev[6 + avs])]
            else:
                wld = [("st", b.n("st"))]
            dla = b.dma("sync", vwsl[:, u, 0:8, :],
                        v1w[0:1024, 256 * avs:256 * (avs + 1)].rearrange("(c p) o -> p c o", p=128),
                        waits=wld, sem="dva%d" % u)
            dlb = b.dma("scalar", vwsl[:, u, 8:16, :],
                        v1w[1024:2048, 256 * avs:256 * (avs + 1)].rearrange("(c p) o -> p c o", p=128),
                        waits=wld, sem="dvb%d" % u)
            stt = 0
            for tt in range(NPP):
                pu = tt % 2
                pdst = (pA if pu == 0 else pB)[:, 512:768]
                w0 = [("dva%d" % u, dla), ("sv", sv_ready), ("sv", ev[pu])]
                for c in range(NT):
                    wc = w0 if c == 0 else ()
                    if c == 8 and tt == 0:
                        wc = [("dvb%d" % u, dlb)]
                    b.op("tensor", lambda E, u=u, c=c, tt=tt, pdst=pdst: E.matmul(
                        pdst, xnT[:, c, 128 * tt:128 * (tt + 1)], vwsl[:, u, c, :],
                        start=(c == 0), stop=(c == NT - 1)),
                        waits=wc)
                stt = b.op("tensor", None, inc="st")
                eu = tt % 2
                sve = b.op("vector", lambda E, eu=eu, pdst=pdst: E.tensor_copy(
                    estg[:, eu, 0:256], pdst),
                    waits=[("st", stt), ("de%d" % eu, b.n("de%d" % eu))], inc="sv")
                ev[pu] = sve
                h0 = avs * 2
                b.dma(QO[eu],
                      V1d[h0:h0 + 2, PTOK * p + 128 * tt:PTOK * p + 128 * (tt + 1), :].rearrange(
                          "h t d -> t h d"),
                      estg[:, eu, 0:256].rearrange("p (h d) -> p h d", h=2),
                      waits=[("sv", sve)], sem="de%d" % eu)
            vmk.append(stt)
        vprev = vmk
        mkK2 = proj_fm(wk2t, K2l, sv_ready, PTOK * p, PTOK, sv_r1, None)
        if p + 1 < NPASS:
            st_next = mk_lnstate(xf[PTOK * (p + 1):PTOK * (p + 2), :], NPP, False, guard)
        else:
            st_next = mk_lnstate(xo, NSLOT, True, guard)
        wvsl = kwsl[:].rearrange("p u c o -> p (u c o)")[:, 0:NT * 256].rearrange(
            "p (c x) -> p c x", x=256)
        wv_w = [("st", b.n("st")), ("dwv", b.n("dwv")), ("dwu", b.n("dwu"))]
        dwv = b.dma("sync", wvsl[:, 0:8, :],
                    wvwoad[0:1024].rearrange("(c p) x -> p c x", p=128),
                    waits=wv_w, sem="dwv")
        dwu = b.dma("scalar", wvsl[:, 8:16, :],
                    wvwoad[1024:2048].rearrange("(c p) x -> p c x", p=128),
                    waits=wv_w, sem="dwu")
        pv2h = {0: pC[:, 0:256], 1: pC[:, 512:768]}
        ev = {0: b.n("sv"), 1: b.n("sv")}
        v2mk = 0
        for tt in range(NPP):
            pu = tt % 2
            pv2 = pv2h[pu]
            w0 = [("sv", sv_ready), ("sv", ev[pu]), ("st", b.n("st")), ("dwv", dwv)] if tt <= 1 else [("sv", ev[pu])]
            for c in range(NT):
                wc = w0 if c == 0 else ()
                if c == 8 and tt <= 1:
                    wc = [("dwu", dwu)]
                b.op("tensor", lambda E, c=c, tt=tt, pv2=pv2: E.matmul(
                    pv2, xnT[:, c, 128 * tt:128 * (tt + 1)], wvsl[:, c, :],
                    start=(c == 0), stop=(c == NT - 1)),
                    waits=wc)
            v2mk = b.op("tensor", None, inc="st")
            ev[pu] = b.op("vector", lambda E, tt=tt, pv2=pv2: E.tensor_scalar(
                v2stage[:, tt, :], pv2, stats[:, tt, 13:14], None, ALU.mult),
                waits=[("st", v2mk), ("dvo", b.n("dvo"))], inc="sv")
            if not st_next["own"] and tt < 3:
                stage_a1(st_next, tt, extra=[("sv", ev[pu])])
        svv = b.n("sv")
        for h in range(NH):
            b.dma("scalar", V2d[h, :, NPP * p:NPP * (p + 1), :], v2stage[:, :, 16 * h:16 * (h + 1)],
                  waits=[("sv", svv)] if h == 0 else (), sem="dvo")
        if st_next["own"]:
            for tt in range(2):
                stage_a1(st_next, tt, extra=[("sv", b.n("sv"))])
        pmarks = {0: v2mk, 1: v2mk, 2: mkK2[14]}
        st_cur = st_next
        if BUILD_PHASES == "p1a1":
            b.barrier()
            return early_out()

    b.barrier()
    if BUILD_PHASES == "p1a":
        return early_out()

    sv_ready, sv_r1, guard = finish_pass(st_cur)
    Q1l = [Q1d[h] for h in range(NH)]
    Q2l = [Q2d[h] for h in range(NH)]
    mkQ1 = proj_fm(q1w, Q1l, sv_ready, 0, TPC, None, pmarks)
    proj_fm(wq2t, Q2l, sv_ready, 0, TPC, sv_r1, None)
    ph = pC[0:16, 0:1024]
    for tch in range(2):
        w0 = [("sv", sv_ready), ("st", b.n("st"))]
        for c in range(NT):
            b.op("tensor", lambda E, c=c, tch=tch: E.matmul(
                ph[:, 512 * tch:512 * (tch + 1)], a2_bf[:, c, :],
                xnT[:, c, 512 * tch:512 * (tch + 1)],
                start=(c == 0), stop=(c == NT - 1)),
                waits=w0 if c == 0 else ())
    stt = b.op("tensor", None, inc="st")
    b.op("vector", lambda E: E.tensor_copy(hAr, ph), waits=[("st", stt)], inc="sv")

    smpu = {0: arC[:][:, 0:512], 1: arC[:][:, 512:1024]}
    ssqu = {0: arC[:][:, 1024:1536], 1: arC[:][:, 1536:2048]}
    pru = {0: pD[0:1, 0:512], 1: pD[0:1, 512:1024]}
    sqm = {0: b.n("sa"), 1: b.n("sa")}
    mmm = {0: b.n("st"), 1: b.n("st")}
    rdm = {0: b.n("sv"), 1: b.n("sv")}
    base_w = [("sv", b.n("sv")), ("sa", b.n("sa")), ("st", b.n("st")),
              ("de0", b.n("de0")), ("de1", b.n("de1"))]
    i = 0
    for idx, srcl in enumerate([K1l, K2l, Q1l, Q2l]):
        for h in range(NH):
            u = i % 2
            dls = b.dma("sync", smpu[u], srcl[h][:, 0:512],
                        waits=(base_w if i < 2 else [("sa", sqm[u])]), sem="dsm%d" % u)
            sas = b.op("scalar", lambda E, u=u: E.activation(ssqu[u], smpu[u], AF.Square),
                       waits=[("dsm%d" % u, dls), ("st", mmm[u])], inc="sa")
            sqm[u] = sas
            stt = b.op("tensor", lambda E, u=u: E.matmul(pru[u], ones_c[:], ssqu[u],
                                                         start=True, stop=True),
                       waits=[("sa", sas), ("sv", rdm[u])], inc="st")
            mmm[u] = stt
            rdm[u] = b.op("vector", lambda E, idx=idx, h=h, u=u: E.tensor_reduce(
                nrm4[0:1, idx * 16 + h:idx * 16 + h + 1], pru[u], mybir.AxisListType.X, ALU.max),
                waits=[("st", stt)], inc="sv")
            i += 1
    svn = b.n("sv")
    t1 = arEf[0:1, 0:32]
    b.op("gpsimd", lambda E: E.tensor_tensor(t1[0:1, 0:16], nrm4[0:1, 32:48], nrm4[0:1, 0:16], ALU.mult),
         waits=[("sv", svn)])
    sg2 = b.op("gpsimd", lambda E: E.tensor_tensor(t1[0:1, 16:32], nrm4[0:1, 48:64], nrm4[0:1, 16:32], ALU.mult),
               inc="sg")
    sa1 = b.op("scalar", lambda E: E.activation(mrow[0:1, :], t1, AF.Ln), waits=[("sg", sg2)], inc="sa")
    sa2 = b.op("scalar", lambda E: E.activation(mrow[0:1, :], mrow[0:1, :], AF.Exp, scale=0.5),
               waits=[("sa", sa1)], inc="sa")
    sa3 = b.op("scalar", lambda E: E.mul(mrow[0:1, :], mrow[0:1, :], -1.25 * C), waits=[("sa", sa2)], inc="sa")
    sv_cl = b.op("vector", lambda E: E.tensor_scalar_max(mrow[0:1, :], mrow[0:1, :], -30.0),
                 waits=[("sa", sa3)], inc="sv")
    stt = b.op("tensor", lambda E: E.matmul(pD[:, 0:32], ones_r[:], mrow[0:1, :], start=True, stop=True),
               waits=[("sv", sv_cl)], inc="st")
    b.op("vector", lambda E: E.tensor_copy(mhat[:], pD[:, 0:32]), waits=[("st", stt)], inc="sv")

    if DEBUG_OUTS:
        sv_all = b.n("sv")
        b.dma("scalar", dbg["d_V2p"], V2d, waits=[("sv", sv_all)], sem="so")
        b.dma("scalar", dbg["d_k1w"], k1w, sem="so")
        b.dma("scalar", dbg["d_wq2t"], wq2t, sem="so")
        b.dma("sync", dbg["d_mhat"], mhat[:], waits=[("sv", sv_all)], sem="so")
        b.dma("sync", dbg["d_hA"], hAr[:], waits=[("sv", sv_all)], sem="so")
        for tt in range(NSLOT):
            b.dma("sync", dbg["d_acc0"][128 * tt:128 * (tt + 1), :], accT[tt][:],
                  waits=[("sv", sv_all)], sem="so")
        for h in range(NH):
            b.dma("scalar", dbg["d_K1"][h], K1d[h], sem="so")
            b.dma("scalar", dbg["d_V1"][h], V1d[h], sem="so")
            b.dma("scalar", dbg["d_K2"][h], K2d[h], sem="so")
            b.dma("scalar", dbg["d_Q1"][h], Q1d[h], sem="so")
            b.dma("scalar", dbg["d_Q2"][h], Q2d[h], sem="so")

    b.barrier()

    if BUILD_PHASES == "p1":
        return early_out()

    ksb1 = arB[:][:, 0:8192]
    vsb = arC[:][:, 0:8192].rearrange("p (n d) -> p n d", d=128)
    qsb = arC[:][:, 8192:10240].rearrange("p (u t) -> p u t", u=2)
    ptb = arC[:][:, 10240:12288].rearrange("p (u t) -> p u t", u=2)
    rcb = arCf[:, 6144:7168]
    sumr = arEf[0:1, 0:1024]
    lnr = arEf[32:33, 0:1024]

    def attention(attn_i, Kl, Ql, av_out_fn, v_load_half, v_lhsT_fn, av_rows):
        two_mm = (av_rows == 128)
        head_s_fin = {}
        st0 = b.n("st")
        dkA = {0: b.dma("sync", ksb1[:, 0:4096], Kl[0][:, 0:4096], waits=[("st", st0)], sem="dKa")}
        dkB = {0: b.dma("sync", ksb1[:, 4096:8192], Kl[0][:, 4096:8192], sem="dKb")}
        dvA = {0: v_load_half(0, 0, st0)}
        dvB = {0: v_load_half(0, 1, st0)}
        for h in range(NH):
            u = h % 2
            wq_ld = [("st", head_s_fin[h - 2])] if h >= 2 else [("st", st0)]
            dq = b.dma("sync", qsb[:, u, :], Ql[h], waits=wq_ld, sem="dq%d" % u)
            bias = mhat[:, attn_i * 16 + h:attn_i * 16 + h + 1]
            sa_exp = {}
            st_av = {}
            prev_done_sv = b.n("sv")
            sa_head0 = b.n("sa")
            def emit_av(k):
                jmin = k // 8
                qoff = 128 * jmin
                pu = k % 2
                sg_m = sg_mask[k]
                dend = min(qoff + 128, TPC)
                if k == 0:
                    chunks = [(0, 512, True), (512, TPC, False)]
                else:
                    chunks = []
                    if dend < 512:
                        chunks.append((dend, 512, False))
                    if max(512, dend) < TPC:
                        chunks.append((max(512, dend), TPC, False))
                    chunks.append((qoff, dend, True))
                extras = []
                if k == 0:
                    extras.append(("sv", prev_done_sv))
                    extras.append(("dVa", dvA[h]))
                if k == 32:
                    extras.append(("dVb", dvB[h]))
                first = True
                for (a0, a1, isdiag) in chunks:
                    stopf = (a0 == qoff and k in (31, 63))
                    startf = (k == 0)
                    wa = list(extras) if first else []
                    if isdiag:
                        wa.append(("sg", sg_m))
                    elif first:
                        wa.append(("sa", sa_exp[k]))
                    first = False
                    if two_mm:
                        b.op("tensor", lambda E, k=k, a0=a0, a1=a1, pu=pu, qoff=qoff, stopf=stopf, startf=startf: E.matmul(
                            pC[0:av_rows, a0:a1], v_lhsT_fn(k), ptb[:, pu, a0 - qoff:a1 - qoff],
                            start=startf, stop=stopf),
                            waits=wa)
                        b.op("tensor", lambda E, k=k, a0=a0, a1=a1, pu=pu, qoff=qoff, stopf=stopf, startf=startf: E.matmul(
                            pD[0:1, a0:a1], ones_c[:], ptb[:, pu, a0 - qoff:a1 - qoff],
                            start=startf, stop=stopf))
                    else:
                        b.op("tensor", lambda E, k=k, a0=a0, a1=a1, pu=pu, qoff=qoff, stopf=stopf, startf=startf: E.matmul(
                            pC[0:av_rows, a0:a1], v_lhsT_fn(k), ptb[:, pu, a0 - qoff:a1 - qoff],
                            start=startf, stop=stopf),
                            waits=wa)
                st_av[k] = b.op("tensor", None, inc="st")

            sg_mask = {}
            for k in range(NKT):
                jmin = k // 8
                qoff = 128 * jmin
                nq = TPC - qoff
                pu = k % 2
                ps_k = (pA if pu == 0 else pB)
                w0 = [("dq%d" % u, dq)]
                if k == 0:
                    w0.append(("dKa", dkA[h]))
                elif k == 32:
                    w0.append(("dKb", dkB[h]))
                if k >= 2:
                    w0.append(("sa", sa_exp[k - 2]))
                else:
                    w0.append(("sa", sa_head0))
                    if not two_mm:
                        w0.append(("sv", prev_done_sv))
                nch = (nq + 511) // 512
                for ch in range(nch):
                    c0 = 512 * ch
                    c1 = min(nq, c0 + 512)
                    b.op("tensor", lambda E, u=u, k=k, c0=c0, c1=c1, ps_k=ps_k, qoff=qoff: E.matmul(
                        ps_k[:, c0:c1], ksb1[:, 128 * k:128 * (k + 1)],
                        qsb[:, u, qoff + c0:qoff + c1], start=True, stop=True),
                        waits=w0 if ch == 0 else ())
                st_s = b.op("tensor", None, inc="st")
                if k == 31:
                    st_s31 = st_s
                if k == NKT - 1:
                    head_s_fin[h] = st_s
                we = [("st", st_s)]
                if k >= 2:
                    we.append(("st", st_av[k - 2]))
                sa_exp[k] = b.op("scalar", lambda E, pu=pu, nq=nq, ps_k=ps_k, bias=bias: E.activation(
                    ptb[:, pu, 0:nq], ps_k[:, 0:nq], AF.Exp, bias=bias, scale=C),
                    waits=we, inc="sa")
                sg_mask[k] = b.op("gpsimd", lambda E, pu=pu, k=k: E.tensor_tensor(
                    ptb[:, pu, 0:128], ptb[:, pu, 0:128], mask_sb[:, k, :], ALU.mult),
                    waits=[("sa", sa_exp[k])], inc="sg")
                if k >= 1:
                    emit_av(k - 1)
                if k == 32 and h + 1 < NH:
                    dkA[h + 1] = b.dma("sync", ksb1[:, 0:4096], Kl[h + 1][:, 0:4096],
                                       waits=[("st", st_s31)], sem="dKa")
                    dvA[h + 1] = v_load_half(h + 1, 0, st_av[31])
            emit_av(NKT - 1)
            if h + 1 < NH:
                dkB[h + 1] = b.dma("sync", ksb1[:, 4096:8192], Kl[h + 1][:, 4096:8192],
                                   waits=[("st", head_s_fin[h])], sem="dKb")
                dvB[h + 1] = v_load_half(h + 1, 1, st_av[NKT - 1])
            sum_src = pD[0:1, 0:1024] if two_mm else pC[32:33, 0:1024]
            sv_s = b.op("vector", lambda E, sum_src=sum_src: E.tensor_copy(sumr, sum_src),
                        waits=[("st", st_av[NKT - 1])], inc="sv")
            sa_l = b.op("scalar", lambda E: E.activation(lnr, sumr, AF.Ln),
                        waits=[("sv", sv_s)], inc="sa")
            sa_r = b.op("scalar", lambda E: E.activation(sumr, lnr, AF.Exp, scale=-1.0),
                        waits=[("sa", sa_l)], inc="sa")
            pbc = pD if two_mm else pA
            b.op("tensor", lambda E, pbc=pbc: E.matmul(pbc[:, 0:512], ones_r[:], sumr[0:1, 0:512],
                                                       start=True, stop=True),
                 waits=[("sa", sa_r)])
            st_b = b.op("tensor", lambda E, pbc=pbc: E.matmul(pbc[:, 512:1024], ones_r[:],
                                                              sumr[0:1, 512:1024],
                                                              start=True, stop=True), inc="st")
            sv_rb = b.op("vector", lambda E, pbc=pbc: E.tensor_copy(rcb, pbc[:, 0:1024]),
                         waits=[("st", st_b)], inc="sv")
            av_out_fn(h, sv_rb)

    def av1s(h):
        return arXv[:, h, 1024:2048]

    def v1_load(h, half, stw):
        return b.dma("scalar", vsb[:, 32 * half:32 * (half + 1), :],
                     V1d[h, 4096 * half:4096 * (half + 1), :].rearrange("(n p) d -> p n d", p=128),
                     waits=[("st", stw)], sem="dV%s" % "ab"[half])

    def av1_out(h, sv_rb):
        b.op("vector", lambda E, h=h: E.tensor_tensor(
            av1s(h), pC[:, 0:1024], rcb, ALU.mult),
            waits=[("sv", sv_rb)], inc="sv")

    attention(0, K1l, Q1l, av1_out, v1_load, lambda k: vsb[:, k, :], 128)
    st_attn1 = b.n("st")

    wobs = {0: arC[:][:, 0:8192].rearrange("p (c o) -> p c o", o=512),
            1: arC[:][:, 8192:16384].rearrange("p (c o) -> p c o", o=512)}

    def wob_load(ds, waits):
        wb = wobs[ds % 2]
        for i in range(NT):
            m = b.dma("scalar", wb[:, i, :],
                      woutb[128 * i:128 * (i + 1), 512 * ds:512 * (ds + 1)],
                      waits=waits if i == 0 else (), sem="dwo")
        return m

    dwo_mark = {}
    dwo_mark[0] = wob_load(0, [("st", st_attn1)])

    def av2_out(h, sv_rb):
        svt = b.op("vector", lambda E: E.tensor_tensor(
            msc, pC[0:16, 0:1024], rcb[0:16, :], ALU.mult),
            waits=[("sv", sv_rb), ("st", b.n("st"))], inc="sv")
        b.op("tensor", lambda E, h=h: E.matmul(
            pD[0:16, 0:512], idf16[:], msc[:, 0:512], start=(h == 0), stop=(h == NH - 1)),
            waits=[("sv", svt)])
        b.op("tensor", lambda E, h=h: E.matmul(
            pD[0:16, 512:1024], idf16[:], msc[:, 512:1024], start=(h == 0), stop=(h == NH - 1)),
            inc="st")
        if h == NH - 1:
            b.op("vector", lambda E: E.tensor_copy(modpre, pD[0:16, 0:1024]),
                 waits=[("st", b.n("st"))], inc="sv")

    v2sb = arB[:][:, 8192:8192 + NKT * 33].rearrange("p (n r) -> p n r", r=33)
    svz2 = b.op("vector", lambda E: E.memset(
        v2sb[:].rearrange("p n r -> p (n r)"), 0.0), inc="sv")
    b.op("vector", lambda E: E.memset(v2sb[:, :, 32:33], 1.0),
         waits=[("sv", svz2)], inc="sv")

    def v2_load(h, half, stw):
        return b.dma("scalar", v2sb[:, 32 * half:32 * (half + 1), 0:16],
                     V2d[h, :, 32 * half:32 * (half + 1), :],
                     waits=[("st", stw)], sem="dV%s" % "ab"[half])

    attention(1, K2l, Q2l, av2_out, v2_load, lambda k: v2sb[:, k, :], 33)
    st_attn2 = b.n("st")
    dwo_mark[1] = wob_load(1, [("st", st_attn2), ("sv", b.n("sv"))])

    if DEBUG_OUTS:
        sv_all = b.n("sv")
        for h in range(NH):
            b.dma("sync", dbg["d_av1"].rearrange("p h t -> p h t")[:, h, :], av1s(h),
                  waits=[("sv", sv_all)], sem="so")
        b.dma("sync", dbg["d_mod"], modpre[:], waits=[("sv", sv_all)], sem="so")
        b.dma("sync", dbg["d_idbf"], id_bf[:], sem="so")
        b.dma("sync", dbg["d_idf"], id_f32[:], sem="so")
        b.dma("sync", dbg["d_xnTo"], arXv[:, :, 0:TPC], sem="so")
        b.dma("sync", dbg["d_arE"], arE[:], sem="so")

    b.barrier()

    pev = {0: b.n("sv"), 1: b.n("sv")}
    for ds in range(4):
        wob = wobs[ds % 2]
        for tt in range(NSLOT):
            pu = tt % 2
            pdst = (pA if pu == 0 else pB)[:, 0:512]
            w0 = [("dwo", dwo_mark[ds]), ("sv", pev[pu])]
            for i in range(NT):
                b.op("tensor", lambda E, i=i, tt=tt, pdst=pdst, wob=wob: E.matmul(
                    pdst, arXv[:, i, 1024 + 128 * tt:1024 + 128 * (tt + 1)], wob[:, i, :],
                    start=(i == 0), stop=(i == NT - 1)),
                    waits=w0 if i == 0 else ())
            stt = b.op("tensor", None, inc="st")
            pev[pu] = b.op("vector", lambda E, tt=tt, ds=ds, pdst=pdst: E.tensor_tensor(
                accT[tt][:, 512 * ds:512 * (ds + 1)], accT[tt][:, 512 * ds:512 * (ds + 1)],
                pdst, ALU.add), waits=[("st", stt)], inc="sv")
        if ds + 2 < 4:
            dwo_mark[ds + 2] = wob_load(ds + 2, [("st", b.n("st"))])
    b.barrier()

    arXu = arX[:].rearrange("p (c u t) -> p c u t", u=2, t=1024)

    def winX(dc):
        return arXu[:, dc // 2, 1, 512 * (dc % 2):512 * (dc % 2) + 512]

    def wupX(dc):
        return arXu[:, 8 + dc // 2, 1, 512 * (dc % 2):512 * (dc % 2) + 512]

    wdnT = arC[:][:, 8192:8192 + 4 * 2048].rearrange("p (c o) -> p c o", o=2048)
    g_all = arB[:][:, 8192:12288].rearrange("p (t f) -> p t f", f=512)
    stg8 = arCf[:, 0:4096].rearrange("p (u d) -> p u d", u=8)
    bsl = arB[:][0:16, 4096:4608]
    gubf = arB[:][:, 6144:6656]
    guT = arB[:][:, 7168:7680].rearrange("p (q t) -> p q t", q=4)
    prod_bf = arB[:][0:16, 4608:4608 + TPC]

    sa_t = b.op("scalar", lambda E: E.activation(msc, modpre, AF.Tanh), inc="sa")
    sv_p = b.op("vector", lambda E: E.scalar_tensor_tensor(
        prod_bf[:], hAr, 0.1, msc, ALU.mult, ALU.mult), waits=[("sa", sa_t)], inc="sv")

    ptev5 = {0: b.n("sv"), 1: b.n("sv")}
    rowst = {u: 0 for u in range(4)}
    colst = {u: 0 for u in range(4)}
    gaev = {0: b.n("sa"), 1: b.n("sa")}
    upev = {0: b.n("sv"), 1: b.n("sv")}
    mev = {0: b.n("sv"), 1: b.n("sv")}
    p5_st0 = b.n("st")

    chainq = []

    def row_dma(wsrc, fs, dc):
        u = dc % 4
        dli = b.dma(QU[dc % 2], stg8[:, u, :].rearrange("p (q d) -> p q d", q=4),
                    wsrc[512 * fs:512 * (fs + 1), 128 * dc:128 * (dc + 1)].rearrange(
                        "(q p) d -> p q d", p=128),
                    waits=[("st", max(rowst[u], p5_st0))], sem="dws%d" % u)
        return dli

    def col_dma(wsrc, fs, do):
        u = do % 4
        dli = b.dma(QU[1 - do % 2], stg8[:, 4 + u, :],
                    wsrc[128 * do:128 * (do + 1), 512 * fs:512 * (fs + 1)],
                    waits=[("st", max(colst[u], p5_st0))], sem="dwd%d" % u)
        return dli

    def row_fin(dstf, dc, dli):
        u = dc % 4
        h = dc % 2
        for q in range(4):
            b.op("tensor", lambda E, q=q, u=u, h=h: E.transpose(
                pD[:, 512 * h + q * 128:512 * h + (q + 1) * 128],
                stg8[:, u, 128 * q:128 * (q + 1)], id_f32[:]),
                waits=[("dws%d" % u, dli), ("sv", ptev5[h])] if q == 0 else ())
        stt = b.op("tensor", None, inc="st")
        rowst[u] = stt
        ptev5[h] = b.op("vector", lambda E, dc=dc, dstf=dstf, h=h: E.tensor_scalar(
            dstf(dc), pD[:, 512 * h:512 * h + 512], ln2c[:, dc:dc + 1], None, ALU.mult),
            waits=[("st", stt)], inc="sv")

    def col_fin(do, dli):
        u = do % 4
        h = do % 2
        for fc in range(4):
            b.op("tensor", lambda E, fc=fc, u=u, h=h: E.transpose(
                pD[:, 512 * h + fc * 128:512 * h + (fc + 1) * 128],
                stg8[:, 4 + u, 128 * fc:128 * (fc + 1)], id_f32[:]),
                waits=[("dwd%d" % u, dli), ("sv", ptev5[h])] if fc == 0 else ())
        stt = b.op("tensor", None, inc="st")
        colst[u] = stt
        ptev5[h] = b.op("vector", lambda E, do=do, h=h: E.tensor_copy(
            wdnT[:, :, 128 * do:128 * (do + 1)],
            pD[:, 512 * h:512 * h + 512].rearrange("p (c o) -> p c o", o=128)),
            waits=[("st", stt)], inc="sv")

    def q_row(wsrc, dstf, fs, dc):
        chainq.append(("r", dstf, dc, row_dma(wsrc, fs, dc)))

    def q_col(wsrc, fs, do):
        chainq.append(("c", None, do, col_dma(wsrc, fs, do)))

    def flush_chains():
        for kind, dstf, dc, dli in chainq:
            if kind == "r":
                row_fin(dstf, dc, dli)
            else:
                col_fin(dc, dli)
        chainq.clear()

    def build_row_dc(wsrc, dstf, fs, dc):
        row_fin(dstf, dc, row_dma(wsrc, fs, dc))

    silu_mark = [0] * NSLOT
    for dc in range(NT):
        build_row_dc(win, winX, 0, dc)
    win_ready = b.n("sv")
    for fs in range(NT):
        gv0 = b.n("sv")
        dlb = b.dma("scalar", bsl, bbd[:, 512 * fs:512 * (fs + 1)],
                    waits=[("st", b.n("st"))], sem="dbl")
        for tt in range(NSLOT):
            q_col(wdn, fs, 2 * tt)
            q_col(wdn, fs, 2 * tt + 1)
            q_row(wup, wupX, fs, 2 * tt)
            q_row(wup, wupX, fs, 2 * tt + 1)
            pu = tt % 2
            pg = (pA if pu == 0 else pB)[:, 0:512]
            w0 = [("sv", win_ready), ("sv", sv_p), ("dbl", dlb), ("sa", gaev[pu])]
            for c in range(NT):
                b.op("tensor", lambda E, c=c, tt=tt, pg=pg: E.matmul(
                    pg, arXv[:, c, 128 * tt:128 * (tt + 1)], winX(c),
                    start=(c == 0), stop=False),
                    waits=w0 if c == 0 else ())
            b.op("tensor", lambda E, tt=tt, pg=pg: E.matmul(
                pg, prod_bf[:, 128 * tt:128 * (tt + 1)], bsl,
                start=False, stop=True))
            st_g = b.op("tensor", None, inc="st")
            gaev[pu] = b.op("scalar", lambda E, tt=tt, pg=pg: E.activation(
                g_all[:, tt, :], pg, AF.Silu, scale=stats[:, tt, 13:14]),
                waits=[("st", st_g), ("sv", gv0)], inc="sa")
            silu_mark[tt] = gaev[pu]
            flush_chains()
        wup_ready = b.n("sv")
        for tt in range(NSLOT):
            pu = tt % 2
            pg = (pA if pu == 0 else pB)[:, 512:1024]
            w0 = [("sv", wup_ready), ("sv", upev[pu])]
            for c in range(NT):
                b.op("tensor", lambda E, c=c, tt=tt, pg=pg: E.matmul(
                    pg, arXv[:, c, 128 * tt:128 * (tt + 1)], wupX(c),
                    start=(c == 0), stop=(c == NT - 1)),
                    waits=w0 if c == 0 else ())
            st_u = b.op("tensor", None, inc="st")
            sv_gu = b.op("vector", lambda E, tt=tt, pg=pg: E.scalar_tensor_tensor(
                gubf, g_all[:, tt, :], stats[:, tt, 13:14], pg, ALU.mult, ALU.mult),
                waits=[("sa", silu_mark[tt]), ("st", st_u)], inc="sv")
            upev[pu] = sv_gu
            for q in range(4):
                b.op("tensor", lambda E, q=q, pu=pu: E.transpose(
                    pT2[:, 1024 * pu + q * 128:1024 * pu + (q + 1) * 128],
                    gubf[:, 128 * q:128 * (q + 1)], id_bf[:]),
                    waits=[("sv", max(sv_gu, ptev5[pu]))] if q == 0 else ())
            st_t = b.op("tensor", None, inc="st")
            sv_gt = b.op("vector", lambda E, pu=pu: E.tensor_copy(
                guT[:].rearrange("p q t -> p (q t)"), pT2[:, 1024 * pu:1024 * pu + 512]),
                waits=[("st", st_t)], inc="sv")
            ptev5[pu] = sv_gt
            for ds in range(4):
                pu2 = ds % 2
                pdst = pC[:, 512 * pu2:512 * (pu2 + 1)]
                w1 = [("sv", sv_gt), ("sv", mev[pu2])]
                for fc in range(4):
                    b.op("tensor", lambda E, fc=fc, ds=ds, pdst=pdst: E.matmul(
                        pdst, guT[:, fc, :], wdnT[:, fc, 512 * ds:512 * (ds + 1)],
                        start=(fc == 0), stop=(fc == 3)),
                        waits=w1 if fc == 0 else ())
                stt = b.op("tensor", None, inc="st")
                mev[pu2] = b.op("vector", lambda E, tt=tt, ds=ds, pdst=pdst: E.tensor_tensor(
                    accT[tt][:, 512 * ds:512 * (ds + 1)], accT[tt][:, 512 * ds:512 * (ds + 1)],
                    pdst, ALU.add), waits=[("st", stt)], inc="sv")
            flush_chains()
            if fs + 1 < NT:
                q_row(win, winX, fs + 1, 2 * tt)
                q_row(win, winX, fs + 1, 2 * tt + 1)
        flush_chains()
        win_ready = b.n("sv")

    sv_f = b.n("sv")
    for tt in range(NSLOT):
        b.dma("sync", out[128 * tt:128 * (tt + 1), :], accT[tt][:],
              waits=[("sv", sv_f)], sem="so")
    b.op("sync", None, waits=[("so", b.n("so"))])
    return _finish(nc, b)


def _finish(nc, b):
    import contextlib
    es = contextlib.ExitStack()
    sems = {}
    names = set()
    for streams in b.streams.values():
        for fn, waits, inc, inck, is_dma in streams:
            if inc:
                names.add(inc)
            for s, v in waits:
                names.add(s)
    with es:
        for nm in sorted(names):
            sems[nm] = es.enter_context(nc.semaphore(nm))
        block = es.enter_context(nc.Block())
        bodies = b.replay(sems)
        block.sync(bodies["sync"])
        block.vector(bodies["vector"])
        block.scalar(bodies["scalar"])
        block.tensor(bodies["tensor"])
        block.gpsimd(bodies["gpsimd"])
    return nc


_ROWS = {}


def _core_rows(c):
    if c not in _ROWS:
        _ROWS[c] = np.concatenate([np.arange(128 * (8 * j + c), 128 * (8 * j + c) + 128)
                                   for j in range(NSLOT)])
    return _ROWS[c]


def kernel(L0_attn_x, L0_attn_w_qkv, L0_attn_w_out, L0_ffn_ln1_w, L0_ffn_ln2_w,
           L0_ffn_Wq, L0_ffn_Wk, L0_ffn_Wv, L0_ffn_Wo, L0_ffn_A, L0_ffn_B,
           L0_ffn_W_in_base, L0_ffn_W_up_base, L0_ffn_W_down):
    f32 = np.float32
    x = np.ascontiguousarray(np.asarray(L0_attn_x, dtype=f32).reshape(SEQ, DM))
    common = dict(
        xf=x,
        wqkv=np.ascontiguousarray(np.asarray(L0_attn_w_qkv, dtype=f32)),
        wq2=np.ascontiguousarray(np.asarray(L0_ffn_Wq, dtype=f32)),
        wk2=np.ascontiguousarray(np.asarray(L0_ffn_Wk, dtype=f32)),
        wv2=np.ascontiguousarray(np.asarray(L0_ffn_Wv, dtype=f32)),
        wo2=np.ascontiguousarray(np.asarray(L0_ffn_Wo, dtype=f32)),
        aa=np.ascontiguousarray(np.asarray(L0_ffn_A, dtype=f32)),
        bbi=np.ascontiguousarray(np.asarray(L0_ffn_B, dtype=f32)),
        ln1r=np.ascontiguousarray(np.asarray(L0_ffn_ln1_w, dtype=f32).reshape(1, DM)),
        ln2r=np.ascontiguousarray(np.asarray(L0_ffn_ln2_w, dtype=f32).reshape(1, DM)),
        wout=np.ascontiguousarray(np.asarray(L0_attn_w_out, dtype=f32)),
        win=np.ascontiguousarray(np.asarray(L0_ffn_W_in_base, dtype=f32)),
        wup=np.ascontiguousarray(np.asarray(L0_ffn_W_up_base, dtype=f32)),
        wdn=np.ascontiguousarray(np.asarray(L0_ffn_W_down, dtype=f32)),
        ident=np.eye(128, dtype=f32),
        kidx=np.ascontiguousarray(np.arange(SEQ, dtype=f32).reshape(NKT, 128).T),
    )
    in_maps = []
    for c in range(NCORES):
        rows = _core_rows(c)
        m = dict(common)
        m["xo"] = np.ascontiguousarray(x[rows])
        m["qidx"] = np.ascontiguousarray(rows.astype(f32).reshape(1, TPC))
        in_maps.append(m)
    nc = build_nc()
    res = run_bass_kernel_spmd(nc, in_maps, core_ids=list(range(NCORES)))
    outp = np.empty((SEQ, DM), dtype=f32)
    for c in range(NCORES):
        outp[_core_rows(c)] = res.results[c]["out"]
    return outp.reshape(1, SEQ, DM)
